# version 02
# baseline (speedup 1.0000x reference)
import numpy as np
import concourse.bass as bass
import concourse.mybir as mybir
from concourse.bass_utils import run_bass_kernel_spmd

F32 = mybir.dt.float32
AF = mybir.ActivationFunctionType
ALU = mybir.AluOpType
AX = mybir.AxisListType

T = 4096
D = 1024
E = 2048
N = 16
RK = 64
NS = 64
DH = 512
DQ = 256
TC = 512
NCH = T // TC
NBLK = T // 128
EQ = 512
NEQ = E // EQ
ETQ = EQ // 128

ARENA_COLS = 49152

_DEBUG = []
_PHASES = ["mpre", "ffn1", "m1", "m2", "ffn2"]
_FSUB = ["fpre", "fa", "fb", "fc"]
_FBL = 99


class Builder:
    def __init__(self, nc):
        self.nc = nc
        self.q = {"sync": [], "tensor": [], "scalar": [], "vector": [], "gpsimd": []}
        self.cnt = {}
        self.sems = {}
        self.last_inc = {}

    def sem(self, name):
        if name not in self.sems:
            self.sems[name] = self.nc.semaphore(name).__enter__()
            self.cnt[name] = 0
        return self.sems[name]

    def on(self, eng, fn):
        self.q[eng].append(fn)

    def do(self, eng, build, inc=None, inc_n=1):
        if inc is not None:
            self.sem(inc)
            self.cnt[inc] += inc_n
            s, n = self.sems[inc], inc_n

            def fn(e, build=build, s=s, n=n):
                build(e).then_inc(s, n)
            self.on(eng, fn)
            self.last_inc[eng] = inc
        else:
            self.on(eng, build)
            self.last_inc[eng] = None

    def wait(self, eng, sem):
        self.sem(sem)
        v = self.cnt[sem]
        if v == 0:
            return
        s = self.sems[sem]
        self.on(eng, lambda e, s=s, v=v: e.wait_ge(s, v))

    def wait_val(self, eng, sem, v):
        s = self.sem(sem)
        if v > 0:
            self.on(eng, lambda e, s=s, v=v: e.wait_ge(s, v))

    def flush(self, block):
        for eng in ["sync", "tensor", "scalar", "vector", "gpsimd"]:
            fns = self.q[eng]
            if not fns:
                continue

            def runner(e, fns=fns):
                for f in fns:
                    f(e)
            getattr(block, eng)(runner)


class Arena:
    def __init__(self, handle):
        self.h = handle
        self.base = 0
        self.marks = []
        self.peak = 0

    def push(self):
        self.marks.append(self.base)

    def pop(self):
        self.base = self.marks.pop()

    def alloc(self, cols, parts=128):
        off = (self.base + 15) // 16 * 16
        self.base = off + cols
        self.peak = max(self.peak, self.base)
        assert self.base <= ARENA_COLS, f"arena overflow: {self.base}"
        return Tile(self.h, parts, off, cols)


class Tile:
    def __init__(self, h, parts, off, cols):
        self.h = h
        self.parts = parts
        self.off = off
        self.cols = cols

    def ap(self):
        return self.h[0:self.parts, self.off:self.off + self.cols]

    def c(self, a, b, parts=None):
        parts = self.parts if parts is None else parts
        assert 0 <= a <= b <= self.cols
        return self.h[0:parts, self.off + a:self.off + b]

    def sub(self, a, b):
        return Tile(self.h, self.parts, self.off + a, b - a)


def build_program():
    nc = bass.Bass("TRN2", target_bir_lowering=False)
    g = {}

    def din(name, shape):
        g[name] = nc.dram_tensor(name, shape, F32, kind="ExternalInput").ap()

    din("x_in", [T, D])
    din("in_proj_w", [2 * E, D]);  din("conv_w", [E, 4]);   din("conv_b", [E])
    din("x_proj_w", [96, E]);      din("dt_proj_w", [E, RK]); din("dt_proj_b", [E])
    din("A_log", [E, N]);          din("Dvec", [E]);        din("out_proj_w", [D, E])
    din("slot_init", [NS, D]);     din("w1", [2 * D, DH]);  din("b1", [DH])
    din("w2", [DH, NS]);           din("b2", [NS]);         din("tau", [1])
    din("Wih", [3 * D, D]);        din("Whh", [3 * D, D])
    din("bih", [3 * D]);           din("bhh", [3 * D])
    din("wq", [D, DQ]);            din("bq", [DQ])
    din("wk", [D, DQ]);            din("bk", [DQ])
    din("wv", [D, D]);             din("bv", [D])
    din("ln_g", [D]);              din("ln_b", [D])
    din("wvp", [D, D]);            din("bvp", [D])
    din("wo", [D, D]);             din("bo", [D])

    g["out"] = nc.dram_tensor("out", [T, D], F32, kind="ExternalOutput").ap()
    g["s1"] = nc.dram_tensor("s1_scr", [T, D], F32).ap()
    g["s2"] = nc.dram_tensor("s2_scr", [T, D], F32).ap()
    g["xc_h"] = nc.dram_tensor("xc_scr", [E, T], F32).ap()
    g["zs_h"] = nc.dram_tensor("zs_scr", [E, T], F32).ap()
    g["xdbl_h"] = nc.dram_tensor("xdbl_scr", [96, T], F32).ap()
    g["ipT_h"] = nc.dram_tensor("ipT_scr", [8, 128, 2 * E], F32).ap()
    g["opT_h"] = nc.dram_tensor("opT_scr", [8, 128, E], F32).ap()

    dbg = {}
    for name, shape in [("s1", [T, D]), ("s2", [T, D]), ("xdbl", [96, T]),
                        ("xc", [E, T]), ("zs", [E, T])]:
        if name in _DEBUG:
            dbg[name] = nc.dram_tensor("dbg_" + name, shape, F32,
                                       kind="ExternalOutput").ap()

    arena_h = nc.alloc_sbuf_tensor("arena", [128, ARENA_COLS], F32)
    ar = Arena(arena_h)
    b = Builder(nc)

    ident = ar.alloc(128)
    ones_row = ar.alloc(128, parts=1)
    h_state = ar.alloc(256)
    vecs = ar.alloc(1024)

    VE = {}
    _vo = [0]

    def valloc(n):
        o = _vo[0]; _vo[0] += n
        assert _vo[0] <= 1024
        return o

    for nm, n in [("msum", 1), ("mneg", 1), ("lv", 1), ("rs", 1),
                  ("v8", 8), ("m1n", 1), ("Z", 1), ("Zr", 1),
                  ("gelub", 4), ("b2c", 1), ("tauc", 1), ("taubc", 1),
                  ("denr", 1), ("att1", 1), ("att2", 1), ("att3", 1),
                  ("sc1", 2), ("sc2", 2), ("sc3", 2),
                  ("bihc", 24), ("bhhc", 24), ("bqc", 2), ("bkc", 2),
                  ("convw", 64), ("convb", 16), ("Dc", 16), ("dtb", 16),
                  ("Acols", 256)]:
        VE[nm] = valloc(n)

    def V(nm, k=0, w=1, parts=128):
        o = vecs.off + VE[nm] + k
        return arena_h[0:parts, o:o + w]

    psA = nc.alloc_psum_tensor("psA", [128, 512], F32)
    psB = nc.alloc_psum_tensor("psB", [128, 512], F32)
    psC = nc.alloc_psum_tensor("psC", [128, 512], F32)
    psD = nc.alloc_psum_tensor("psD", [128, 512], F32)
    psE = nc.alloc_psum_tensor("psE", [128, 512], F32)
    ps_slot = nc.alloc_psum_tensor("ps_slot", [64, 1025], F32)

    SEM_DMA = "dq"
    SEM_OUT = "dout"

    def dma_in(src_ap, dst_ap):
        b.do("sync", lambda e, s=src_ap, d=dst_ap: e.dma_start(out=d, in_=s),
             inc=SEM_DMA, inc_n=16)

    def dma_out(dst_ap, src_ap, sem=SEM_OUT):
        b.do("sync", lambda e, s=src_ap, d=dst_ap: e.dma_start(out=d, in_=s),
             inc=sem, inc_n=16)

    def wait_dma(eng):
        b.wait(eng, SEM_DMA)

    def bar(eng):
        li = b.last_inc.get(eng)
        assert li is not None, f"bar({eng}) after inc-less instruction"
        b.wait(eng, li)

    def mm(ps_ap, s_ap, m_ap, start, stop, inc=None):
        b.do("tensor",
             lambda e, o=ps_ap, s=s_ap, m=m_ap, st=start, sp=stop:
             e.matmul(o, s, m, start=st, stop=sp, skip_group_check=True), inc=inc)

    def transpose(ps_ap, in_ap, rows, inc=None):
        b.do("tensor",
             lambda e, o=ps_ap, i=in_ap, r=rows:
             e.transpose(o, i, ident.c(0, r, parts=r)), inc=inc)

    def act(out_ap, in_ap, func, bias=0.0, scale=1.0, accum=None, inc=None):
        if func == AF.Copy and not isinstance(bias, float):
            func = AF.Identity
        b.do("scalar",
             lambda e, o=out_ap, i=in_ap, f=func, bi=bias, sc=scale, a=accum:
             e.activation(o, i, f, bias=bi, scale=sc, accum_out=a), inc=inc)

    def vtt(out_ap, a_ap, b_ap, op, inc=None):
        b.do("vector", lambda e, o=out_ap, x=a_ap, y=b_ap, p=op:
             e.tensor_tensor(o, x, y, p), inc=inc)

    def vts(out_ap, a_ap, s, op, inc=None):
        b.do("vector", lambda e, o=out_ap, x=a_ap, sc=s, p=op:
             e.tensor_scalar(o, x, sc, None, p), inc=inc)

    def vstt(out_ap, in0_ap, s, op0, op1, in1_ap, inc=None):
        b.do("vector", lambda e, o=out_ap, x=in0_ap, sc=s, p0=op0, p1=op1, y=in1_ap:
             e.scalar_tensor_tensor(o, x, sc, y, p0, p1), inc=inc)

    def vcopy(out_ap, in_ap, inc=None):
        b.do("vector", lambda e, o=out_ap, i=in_ap: e.tensor_copy(o, i), inc=inc)

    def bcast(ps_ap, row_ap, parts, inc):
        b.do("tensor", lambda e, o=ps_ap, r=row_ap, p=parts:
             e.matmul(o, ones_row.c(0, p, parts=1), r, start=True, stop=True,
                      skip_group_check=True), inc=inc)

    def layernorm_block(x_f, out_f, sq_tile):
        xa = x_f()
        b.wait("vector", "a_ln")
        b.do("vector", lambda e, a=xa: e.reduce_sum(V("msum"), a, axis=AX.X),
             inc="v_ln")
        bar("vector")
        vts(V("mneg"), V("msum"), -1.0 / D, ALU.mult, inc="v_ln")
        b.wait("scalar", "v_ln")
        act(out_f(), x_f(), AF.Copy, bias=V("mneg"), inc="a_ln")
        act(sq_tile.ap(), out_f(), AF.Square, accum=V("lv"), inc="a_ln")
        bar("scalar")
        act(V("lv"), V("lv"), AF.Ln, scale=1.0 / D, bias=1e-5, inc="a_ln")
        bar("scalar")
        act(V("rs"), V("lv"), AF.Exp, scale=-0.5, inc="a_ln")
        bar("scalar")
        act(out_f(), out_f(), AF.Copy, scale=V("rs"), inc="a_ln")

    def load_tiles(src_full, rows, cols, dst_tile):
        nrt = (rows + 127) // 128
        for rt in range(nrt):
            r0, r1 = rt * 128, min(rows, rt * 128 + 128)
            dma_in(src_full[r0:r1, :], dst_tile.c(rt * cols, rt * cols + cols,
                                                  parts=r1 - r0))

    def load_cols(src_vec, n, dst_f, scratch_row):
        b.wait("sync", "p_psE")
        dma_in(src_vec.unsqueeze(0), scratch_row.c(0, n, parts=1))
        wait_dma("tensor")
        for t in range((n + 127) // 128):
            w = min(128, n - t * 128)
            b.wait("tensor", "f_psE")
            transpose(psE[0:w, 0:1], scratch_row.c(t * 128, t * 128 + w, parts=1), 1,
                      inc="p_psE")
            b.wait("scalar", "p_psE")
            act(dst_f(t), psE[0:w, 0:1], AF.Copy, inc="f_psE")

    CONSTS = [1e-5, 0.1, 1e-8]
    cst = ar.alloc(16)
    for i, v in enumerate(CONSTS):
        b.do("vector", lambda e, i=i, v=v: e.memset(cst.c(i, i + 1), v), inc="v_pre")
        nc.const_aps.register(F32, v, cst.c(i, i + 1))

    b.do("gpsimd", lambda e: e.iota(ident.ap(), [[-1, 128]], channel_multiplier=1,
                                    allow_small_or_imprecise_dtypes=True), inc="g_pre")
    b.wait("vector", "g_pre")
    vts(ident.ap(), ident.ap(), 0.0, ALU.is_equal, inc="v_pre")
    b.do("vector", lambda e: e.memset(ones_row.ap(), 1.0), inc="v_pre")
    b.do("vector", lambda e: e.memset(h_state.ap(), 0.0), inc="v_pre")
    bar("vector")
    b.wait("tensor", "v_pre")
    b.wait("scalar", "v_pre")
    b.wait("sync", "v_pre")

    def ffn_call(in_hbm, out_hbm, out_sem, cid=0):
        def tapd(name, shape):
            if cid == 0 and name in _DEBUG:
                if "dbg_" + name not in g:
                    g["dbg_" + name] = nc.dram_tensor("dbg_" + name, shape, F32,
                                                      kind="ExternalOutput").ap()
                return True
            return False
        ar.push()
        alpha = ar.alloc(NBLK * NS)
        alphaT = ar.alloc(T, parts=64)
        svwo2 = ar.alloc(D, parts=64)
        scr = ar.alloc(3 * D, parts=1)
        sq_s = ar.alloc(D)
        si_sb = ar.alloc(D, parts=64)

        ar.push()
        if "fpre" not in _FSUB:
            ar.pop(); ar.pop()
            return
        w1a = ar.alloc(8 * DH)
        w2h = ar.alloc(4 * NS)
        load_tiles(g["w1"][0:D, :], D, DH, w1a)
        w2r = ar.alloc(4 * NS)
        load_tiles(g["w2"], DH, NS, w2r)
        wait_dma("scalar")
        for kt in range(4):
            act(w2h.c(kt * NS, (kt + 1) * NS), w2r.c(kt * NS, (kt + 1) * NS),
                AF.Copy, scale=0.5, inc="a_fp")
        tcol = ar.alloc(16, parts=1)
        dma_in(g["tau"].unsqueeze(0), tcol.c(0, 1, parts=1))
        wait_dma("scalar")
        act(tcol.c(1, 2, parts=1), tcol.c(0, 1, parts=1), AF.Abs, inc="a_fp")
        bar("scalar")
        act(tcol.c(2, 3, parts=1), tcol.c(1, 2, parts=1), AF.Copy, bias=0.1, inc="a_fp")
        b.wait("vector", "a_fp")
        b.do("vector", lambda e: e.reciprocal(tcol.c(3, 4, parts=1),
                                              tcol.c(2, 3, parts=1)), inc="v_fp")
        bar("vector")
        b.wait("tensor", "v_fp")
        b.wait("tensor", "f_psE")
        bcast(psE[0:64, 0:1], tcol.c(3, 4, parts=1), 64, inc="p_psE")
        b.wait("scalar", "p_psE")
        act(V("tauc", parts=64), psE[0:64, 0:1], AF.Copy, inc="f_psE")
        load_cols(g["b2"], NS, lambda t: V("b2c", parts=64), scr)
        b.wait("vector", "f_psE")
        vtt(V("taubc", parts=64), V("tauc", parts=64), V("b2c", parts=64), ALU.mult,
            inc="v_fp")
        dma_in(g["slot_init"], si_sb.ap())
        onames = ar.alloc(16, parts=64)
        b.do("vector", lambda e: e.memset(onames.c(0, 1, parts=64), 1.0 / 64), inc="v_fp")
        bar("vector")
        wait_dma("tensor")
        b.wait("tensor", "v_fp")
        smean = ar.alloc(D, parts=1)
        for h in range(2):
            b.wait("tensor", "f_psE")
            mm(psE[0:1, 0:512], onames.c(0, 1, parts=64),
               si_sb.c(h * 512, (h + 1) * 512, parts=64), True, True, inc="p_psE")
            b.wait("scalar", "p_psE")
            act(smean.c(h * 512, (h + 1) * 512, parts=1), psE[0:1, 0:512], AF.Copy,
                inc="f_psE")
        smc = ar.alloc(8)
        b.wait("sync", "f_psE")
        for t in range(8):
            b.wait("tensor", "f_psE")
            transpose(psE[0:128, 0:1], smean.c(t * 128, (t + 1) * 128, parts=1), 1,
                      inc="p_psE")
            b.wait("scalar", "p_psE")
            act(smc.c(t, t + 1), psE[0:128, 0:1], AF.Copy, inc="f_psE")
        w1b = ar.alloc(8 * DH)
        load_tiles(g["w1"][D:2 * D, :], D, DH, w1b)
        wait_dma("tensor")
        b.wait("tensor", "f_psE")
        for jt in range(4):
            b.wait("tensor", "f_psE")
            for kt in range(8):
                mm(psE[0:128, 0:1],
                   w1b.c(kt * DH + jt * 128, kt * DH + (jt + 1) * 128),
                   smc.c(kt, kt + 1), kt == 0, kt == 7,
                   inc="p_psE" if kt == 7 else None)
            b.wait("scalar", "p_psE")
            act(V("gelub", jt), psE[0:128, 0:1], AF.Copy, inc="f_psE")
        b1c = ar.alloc(4)
        load_cols(g["b1"], DH, lambda t: b1c.c(t, t + 1), scr)
        b.wait("vector", "f_psE")
        vtt(V("gelub", 0, 4), V("gelub", 0, 4), b1c.c(0, 4), ALU.add, inc="v_fp")
        bar("vector")

        if "fa" not in _FSUB:
            ar.pop(); ar.pop()
            return
        XW = 1040
        xw = ar.alloc(4 * XW)
        xnT = ar.alloc(8 * TC)
        hT = ar.alloc(4 * TC)
        hb_s = ar.alloc(TC)
        te_s = ar.alloc(TC)
        logT = ar.alloc(TC, parts=64)
        logtok = ar.alloc(4 * NS)
        sin_s = ar.alloc(4 * D)
        tk_s = ar.alloc(2 * NS)
        b.wait("vector", "a_fp")
        for j in range(4):
            b.do("vector", lambda e, j=j:
                 e.memset(xw.c(j * XW + 1024, j * XW + 1025), 1.0), inc="v_fa")
        bar("vector")
        b.wait("tensor", "v_fa")

        for c in range(NCH):
            b.wait("sync", "a_ln")
            b.wait("sync", "mm_slot")
            b.wait("sync", "dbgw")
            for j in range(4):
                blk = c * 4 + j
                dma_in(in_hbm[blk * 128:(blk + 1) * 128, :], sin_s.c(j * D, (j + 1) * D))
            wait_dma("vector")
            wait_dma("scalar")
            for j in range(4):
                layernorm_block(lambda j=j: sin_s.c(j * D, (j + 1) * D),
                                lambda j=j: xw.c(j * XW, j * XW + 1024), sq_s)
                if "lnstats" in _DEBUG and cid == 0:
                    tapd("lnstats", [T, 16])
                    blk = c * 4 + j
                    b.wait("sync", "a_ln")
                    dma_out(g["dbg_lnstats"][blk * 128:(blk + 1) * 128, :],
                            V("msum", 0, 16), sem="dbgw")
            b.wait("tensor", "a_ln")
            for dt_i in range(8):
                b.wait("tensor", "f_psD")
                for j in range(4):
                    transpose(psD[0:128, j * 128:(j + 1) * 128],
                              xw.c(j * XW + dt_i * 128, j * XW + (dt_i + 1) * 128), 128,
                              inc="p_psD" if j == 3 else None)
                b.wait("scalar", "p_psD")
                act(xnT.c(dt_i * TC, (dt_i + 1) * TC), psD[0:128, 0:512], AF.Copy,
                    inc="f_psD")
            b.wait("tensor", "f_psD")
            for jt in range(4):
                ps, pss = (psA, "A") if jt % 2 == 0 else (psB, "B")
                b.wait("tensor", "f_psA")
                b.wait("tensor", "f_psB")
                for kt in range(8):
                    mm(ps[0:128, 0:512],
                       w1a.c(kt * DH + jt * 128, kt * DH + (jt + 1) * 128),
                       xnT.c(kt * TC, (kt + 1) * TC), kt == 0, kt == 7,
                       inc="p_ps" + pss if kt == 7 else None)
                b.wait("scalar", "p_ps" + pss)
                b.wait("scalar", "f_psA")
                b.wait("scalar", "f_psB")
                act(hb_s.ap(), ps[0:128, 0:512], AF.Copy, bias=V("gelub", jt), inc="a_fa")
                act(te_s.ap(), hb_s.ap(), AF.Erf, scale=0.7071067811865476, inc="a_fa")
                b.wait("vector", "a_fa")
                vstt(hT.c(jt * TC, (jt + 1) * TC), te_s.ap(), 1.0, ALU.add, ALU.mult,
                     hb_s.ap(), inc="f_ps" + pss)
            b.wait("tensor", "f_psA")
            b.wait("tensor", "f_psB")
            b.wait("tensor", "f_psC")
            for kt in range(4):
                mm(psC[0:64, 0:512], w2h.c(kt * NS, (kt + 1) * NS),
                   hT.c(kt * TC, (kt + 1) * TC), kt == 0, kt == 3,
                   inc="p_psC" if kt == 3 else None)
            b.wait("scalar", "p_psC")
            act(logT.ap(), psC[0:64, 0:512], AF.Copy, scale=V("tauc", parts=64),
                bias=V("taubc", parts=64), inc="f_psC")
            b.wait("tensor", "f_psC")
            b.wait("tensor", "f_psD")
            for j in range(4):
                transpose(psD[0:128, j * NS:(j + 1) * NS],
                          logT.c(j * 128, (j + 1) * 128, parts=64), 64,
                          inc="p_psD" if j == 3 else None)
            b.wait("scalar", "p_psD")
            act(logtok.ap(), psD[0:128, 0:256], AF.Copy, inc="f_psD")
            b.wait("vector", "f_psD")
            for j in range(4):
                blk = c * 4 + j
                b.do("vector", lambda e, j=j: e.max(out=V("v8", 0, 8),
                                                    in_=logtok.c(j * NS, (j + 1) * NS)),
                     inc="v_fa")
                bar("vector")
                vts(V("m1n"), V("v8", 0, 1), -1.0, ALU.mult, inc="v_fa")
                vts(tk_s.c(NS, 2 * NS), logtok.c(j * NS, (j + 1) * NS), V("v8", 7, 1),
                    ALU.is_ge, inc="v_fa")
                b.wait("scalar", "v_fa")
                act(V("v8", 0, 8), V("v8", 0, 8), AF.Exp, bias=V("m1n"), inc="a_fa")
                b.wait("vector", "a_fa")
                b.do("vector", lambda e: e.reduce_sum(V("Z"), V("v8", 0, 8), axis=AX.X),
                     inc="v_fa")
                bar("vector")
                b.do("vector", lambda e: e.reciprocal(V("Zr"), V("Z")), inc="v_fa")
                b.wait("scalar", "v_fa")
                act(tk_s.c(0, NS), logtok.c(j * NS, (j + 1) * NS), AF.Exp,
                    bias=V("m1n"), inc="a_fa")
                b.wait("vector", "a_fa")
                vstt(alpha.c(blk * NS, (blk + 1) * NS), tk_s.c(0, NS), V("Zr"),
                     ALU.mult, ALU.mult, tk_s.c(NS, 2 * NS), inc="v_fa")
                bar("vector")
                if tapd("logtok", [T, NS]) if False else ("logtok" in _DEBUG and cid == 0):
                    tapd("logtok", [T, NS])
                    b.wait("sync", "f_psD")
                    dma_out(g["dbg_logtok"][blk * 128:(blk + 1) * 128, :],
                            logtok.c(j * NS, (j + 1) * NS), sem="dbgw")
                if "xn" in _DEBUG and cid == 0:
                    tapd("xn", [T, D])
                    b.wait("sync", "a_ln")
                    dma_out(g["dbg_xn"][blk * 128:(blk + 1) * 128, :],
                            xw.c(j * XW, j * XW + 1024), sem="dbgw")
                if "alpha" in _DEBUG and cid == 0:
                    tapd("alpha", [T, NS])
                    b.wait("sync", "v_fa")
                    dma_out(g["dbg_alpha"][blk * 128:(blk + 1) * 128, :],
                            alpha.c(blk * NS, (blk + 1) * NS), sem="dbgw")
                b.wait("tensor", "v_fa")
                b.wait("tensor", "f_psE")
                transpose(psE[0:64, 0:128], alpha.c(blk * NS, (blk + 1) * NS), 128,
                          inc="p_psE")
                b.wait("scalar", "p_psE")
                act(alphaT.c(blk * 128, (blk + 1) * 128, parts=64), psE[0:64, 0:128],
                    AF.Copy, inc="f_psE")
                mm(ps_slot[0:64, 0:512], alpha.c(blk * NS, (blk + 1) * NS),
                   xw.c(j * XW, j * XW + 512), blk == 0, blk == NBLK - 1)
                mm(ps_slot[0:64, 512:1024], alpha.c(blk * NS, (blk + 1) * NS),
                   xw.c(j * XW + 512, j * XW + 1024), blk == 0, blk == NBLK - 1)
                mm(ps_slot[0:64, 1024:1025], alpha.c(blk * NS, (blk + 1) * NS),
                   xw.c(j * XW + 1024, j * XW + 1025), blk == 0, blk == NBLK - 1,
                   inc="mm_slot")

        b.wait("sync", "mm_slot")
        b.wait("vector", "mm_slot")
        b.wait("scalar", "mm_slot")
        ar.pop()
        if "fb" not in _FSUB:
            ar.pop()
            return
        ar.push()
        slot_in = ar.alloc(1040, parts=64)
        b.wait("scalar", "mm_slot")
        act(slot_in.c(0, 1025, parts=64), ps_slot[0:64, 0:1025], AF.Copy, inc="f_slot")
        b.wait("vector", "f_slot")
        vts(slot_in.c(1024, 1025, parts=64), slot_in.c(1024, 1025, parts=64), 1e-8,
            ALU.add, inc="v_fb")
        bar("vector")
        b.do("vector", lambda e: e.reciprocal(V("denr", parts=64),
                                              slot_in.c(1024, 1025, parts=64)),
             inc="v_fb")
        bar("vector")
        vts(slot_in.c(0, 1024, parts=64), slot_in.c(0, 1024, parts=64),
            V("denr", parts=64), ALU.mult, inc="v_fb")
        bar("vector")
        if tapd("si", [NS, 1025]):
            b.wait("sync", "v_fb")
            dma_out(g["dbg_si"], slot_in.c(0, 1025, parts=64), sem="dbgw")
        siT = ar.alloc(8 * NS)
        siniT = ar.alloc(8 * NS)
        b.wait("tensor", "v_fb")
        for (src, dstT) in ([(slot_in, siT), (si_sb, siniT)] if _FBL >= 1 else []):
            for j in range(8):
                b.wait("tensor", "f_psD")
                transpose(psD[0:128, 0:64], src.c(j * 128, (j + 1) * 128, parts=64), 64,
                          inc="p_psD")
                b.wait("scalar", "p_psD")
                act(dstT.c(j * NS, (j + 1) * NS), psD[0:128, 0:64], AF.Copy, inc="f_psD")
        load_cols(g["bih"], 3 * D, lambda t: V("bihc", t), scr)
        load_cols(g["bhh"], 3 * D, lambda t: V("bhhc", t), scr)
        ghT = ar.alloc(24 * NS)
        giT = ar.alloc(24 * NS)
        ar.push()
        wscr = ar.alloc(D)
        HW2 = 12 * 128
        WT = ar.alloc(8 * HW2)
        for (wname, dstT, bcol, srcT) in ([("Whh", ghT, "bhhc", siniT),
                                          ("Wih", giT, "bihc", siT)] if _FBL >= 2 else []):
            for hf in range(2):
                for rt2 in range(12):
                    rt = hf * 12 + rt2
                    b.wait("sync", "p_psD")
                    b.wait("sync", "f_psA")
                    b.wait("sync", "f_psB")
                    dma_in(g[wname][rt * 128:(rt + 1) * 128, :], wscr.ap())
                    wait_dma("tensor")
                    for dt_i in range(8):
                        b.wait("tensor", "f_psD")
                        transpose(psD[0:128, 0:128],
                                  wscr.c(dt_i * 128, (dt_i + 1) * 128),
                                  128, inc="p_psD")
                        b.wait("scalar", "p_psD")
                        act(WT.c(dt_i * HW2 + rt2 * 128, dt_i * HW2 + (rt2 + 1) * 128),
                            psD[0:128, 0:128], AF.Copy, inc="f_psD")
                b.wait("tensor", "f_psD")
                for jt2 in range(12):
                    jt = hf * 12 + jt2
                    ps, pss = (psA, "A") if jt % 2 == 0 else (psB, "B")
                    b.wait("tensor", "f_ps" + pss)
                    for kt in range(8):
                        mm(ps[0:128, 0:NS],
                           WT.c(kt * HW2 + jt2 * 128, kt * HW2 + (jt2 + 1) * 128),
                           srcT.c(kt * NS, (kt + 1) * NS), kt == 0, kt == 7,
                           inc="p_ps" + pss if kt == 7 else None)
                    b.wait("scalar", "p_ps" + pss)
                    act(dstT.c(jt * NS, (jt + 1) * NS), ps[0:128, 0:NS], AF.Copy,
                        bias=V(bcol, jt), inc="f_ps" + pss)
                b.wait("tensor", "f_psA")
                b.wait("tensor", "f_psB")
        ar.pop()
        SnT = ar.alloc(8 * NS)
        rb = ar.alloc(NS)
        b.wait("vector", "f_psA")
        b.wait("vector", "f_psB")
        for t8 in (range(8) if _FBL >= 3 else []):
            gi_r = lambda t8=t8: giT.c(t8 * NS, (t8 + 1) * NS)
            gh_r = lambda t8=t8: ghT.c(t8 * NS, (t8 + 1) * NS)
            gi_z = lambda t8=t8: giT.c((8 + t8) * NS, (9 + t8) * NS)
            gh_z = lambda t8=t8: ghT.c((8 + t8) * NS, (9 + t8) * NS)
            gi_n = lambda t8=t8: giT.c((16 + t8) * NS, (17 + t8) * NS)
            gh_n = lambda t8=t8: ghT.c((16 + t8) * NS, (17 + t8) * NS)
            b.wait("vector", "a_fb")
            vtt(rb.ap(), gi_r(), gh_r(), ALU.add, inc="v_fb")
            b.wait("scalar", "v_fb")
            act(rb.ap(), rb.ap(), AF.Sigmoid, inc="a_fb")
            b.wait("vector", "a_fb")
            vtt(gh_n(), rb.ap(), gh_n(), ALU.mult, inc="v_fb")
            bar("vector")
            vtt(gi_n(), gi_n(), gh_n(), ALU.add, inc="v_fb")
            b.wait("scalar", "v_fb")
            act(gi_n(), gi_n(), AF.Tanh, inc="a_fb")
            vtt(gi_z(), gi_z(), gh_z(), ALU.add, inc="v_fb")
            b.wait("scalar", "v_fb")
            act(gi_z(), gi_z(), AF.Sigmoid, inc="a_fb")
            b.wait("vector", "a_fb")
            vtt(rb.ap(), siniT.c(t8 * NS, (t8 + 1) * NS), gi_n(), ALU.subtract,
                inc="v_fb")
            bar("vector")
            vtt(rb.ap(), rb.ap(), gi_z(), ALU.mult, inc="v_fb")
            bar("vector")
            vtt(SnT.c(t8 * NS, (t8 + 1) * NS), gi_n(), rb.ap(), ALU.add, inc="v_fb")
        bar("vector")
        if tapd("SnT", [128, 8 * NS]):
            b.wait("sync", "v_fb")
            dma_out(g["dbg_SnT"], SnT.ap(), sem="dbgw")
        load_cols(g["bq"], DQ, lambda t: V("bqc", t), scr)
        load_cols(g["bk"], DQ, lambda t: V("bkc", t), scr)
        qT = ar.alloc(2 * NS)
        kT = ar.alloc(2 * NS)
        ar.push()
        wqt = ar.alloc(8 * DQ)
        wkt = ar.alloc(8 * DQ)
        load_tiles(g["wq"], D, DQ, wqt)
        load_tiles(g["wk"], D, DQ, wkt)
        wait_dma("tensor")
        b.wait("tensor", "v_fb")
        for (wt_, dT, bc) in ([(wqt, qT, "bqc"), (wkt, kT, "bkc")] if _FBL >= 4 else []):
            for qt in range(2):
                b.wait("tensor", "f_psA")
                for kt in range(8):
                    mm(psA[0:128, 0:NS],
                       wt_.c(kt * DQ + qt * 128, kt * DQ + (qt + 1) * 128),
                       SnT.c(kt * NS, (kt + 1) * NS), kt == 0, kt == 7,
                       inc="p_psA" if kt == 7 else None)
                b.wait("scalar", "p_psA")
                act(dT.c(qt * NS, (qt + 1) * NS), psA[0:128, 0:NS], AF.Copy,
                    bias=V(bc, qt), inc="f_psA")
        b.wait("sync", "p_psA")
        ar.pop()
        attn = ar.alloc(NS, parts=64)
        b.wait("tensor", "f_psA")
        b.wait("tensor", "f_psB")
        for kt in (range(2) if _FBL >= 5 else []):
            mm(psB[0:64, 0:NS], qT.c(kt * NS, (kt + 1) * NS),
               kT.c(kt * NS, (kt + 1) * NS), kt == 0, kt == 1,
               inc="p_psB" if kt == 1 else None)
        b.wait("scalar", "p_psB")
        if _FBL >= 5:
            act(attn.ap(), psB[0:64, 0:NS], AF.Copy, scale=DQ ** (-0.5), inc="f_psB")
        b.wait("vector", "f_psB")
        if _FBL < 6:
            b.wait("sync", "v_fb"); b.wait("sync", "a_fb")
            ar.pop(); ar.pop(); return
        b.do("vector", lambda e: e.reduce_max(V("att1", parts=64), attn.ap(), axis=AX.X),
             inc="v_fb")
        bar("vector")
        vts(V("att2", parts=64), V("att1", parts=64), -1.0, ALU.mult, inc="v_fb")
        b.wait("scalar", "v_fb")
        act(attn.ap(), attn.ap(), AF.Exp, bias=V("att2", parts=64), inc="a_fb")
        b.wait("vector", "a_fb")
        b.do("vector", lambda e: e.reduce_sum(V("att3", parts=64), attn.ap(), axis=AX.X),
             inc="v_fb")
        bar("vector")
        b.do("vector", lambda e: e.reciprocal(V("att1", parts=64), V("att3", parts=64)),
             inc="v_fb")
        bar("vector")
        vts(attn.ap(), attn.ap(), V("att1", parts=64), ALU.mult, inc="v_fb")
        bar("vector")
        attnT = ar.alloc(NS, parts=64)
        b.wait("tensor", "v_fb")
        b.wait("tensor", "f_psE")
        transpose(psE[0:64, 0:64], attn.ap(), 64, inc="p_psE")
        b.wait("scalar", "p_psE")
        act(attnT.ap(), psE[0:64, 0:64], AF.Copy, inc="f_psE")
        if _FBL < 7:
            b.wait("sync", "v_fb"); b.wait("sync", "a_fb")
            ar.pop(); ar.pop(); return
        attnT_ = None
        vsb = ar.alloc(D, parts=64)
        Snsm = ar.alloc(D, parts=64)
        ysb = ar.alloc(D, parts=64)
        bvrep = ar.alloc(D, parts=64)
        ar.push()
        wvt = ar.alloc(8 * D)
        load_tiles(g["wv"], D, D, wvt)
        wait_dma("tensor")
        for h in range(2):
            b.wait("tensor", "f_psA")
            for kt in range(8):
                mm(psA[0:64, 0:512], SnT.c(kt * NS, (kt + 1) * NS),
                   wvt.c(kt * D + h * 512, kt * D + (h + 1) * 512), kt == 0, kt == 7,
                   inc="p_psA" if kt == 7 else None)
            b.wait("scalar", "p_psA")
            act(vsb.c(h * 512, (h + 1) * 512, parts=64), psA[0:64, 0:512], AF.Copy,
                inc="f_psA")
        b.wait("tensor", "f_psA")
        for j in range(8):
            b.wait("tensor", "f_psD")
            transpose(psD[0:64, 0:128], SnT.c(j * NS, (j + 1) * NS), 128, inc="p_psD")
            b.wait("scalar", "p_psD")
            act(Snsm.c(j * 128, (j + 1) * 128, parts=64), psD[0:64, 0:128], AF.Copy,
                inc="f_psD")
        dma_in(g["bv"].unsqueeze(0).broadcast_to([64, D]), bvrep.ap())
        b.wait("vector", "f_psD")
        b.wait("tensor", "f_psE")
        for h in range(2):
            ps, pss = (psA, "A") if h == 0 else (psB, "B")
            b.wait("tensor", "f_ps" + pss)
            mm(ps[0:64, 0:512], attnT.ap(), vsb.c(h * 512, (h + 1) * 512, parts=64),
               True, True, inc="p_ps" + pss)
            b.wait("vector", "p_ps" + pss)
            vtt(ysb.c(h * 512, (h + 1) * 512, parts=64),
                Snsm.c(h * 512, (h + 1) * 512, parts=64), ps[0:64, 0:512], ALU.add,
                inc="f_ps" + pss)
        wait_dma("vector")
        bar("vector")
        vtt(ysb.ap(), ysb.ap(), bvrep.ap(), ALU.add, inc="v_fb")
        bar("vector")
        b.wait("sync", "v_fb")
        ar.pop()
        if _FBL < 8:
            b.wait("sync", "a_fb"); ar.pop(); ar.pop(); return
        lngr = ar.alloc(D, parts=64)
        lnbr = ar.alloc(D, parts=64)
        dma_in(g["ln_g"].unsqueeze(0).broadcast_to([64, D]), lngr.ap())
        dma_in(g["ln_b"].unsqueeze(0).broadcast_to([64, D]), lnbr.ap())
        b.do("vector", lambda e: e.reduce_sum(V("sc1", 0, 1, parts=64), ysb.ap(),
                                              axis=AX.X), inc="v_fb")
        bar("vector")
        vts(V("sc1", 1, 1, parts=64), V("sc1", 0, 1, parts=64), -1.0 / D, ALU.mult,
            inc="v_fb")
        b.wait("scalar", "v_fb")
        act(ysb.ap(), ysb.ap(), AF.Copy, bias=V("sc1", 1, 1, parts=64), inc="a_fb")
        act(sq_s.c(0, D, parts=64), ysb.ap(), AF.Square,
            accum=V("sc2", 0, 1, parts=64), inc="a_fb")
        bar("scalar")
        act(V("sc2", 1, 1, parts=64), V("sc2", 0, 1, parts=64), AF.Ln, scale=1.0 / D,
            bias=1e-5, inc="a_fb")
        bar("scalar")
        act(V("sc3", 0, 1, parts=64), V("sc2", 1, 1, parts=64), AF.Exp, scale=-0.5,
            inc="a_fb")
        bar("scalar")
        act(ysb.ap(), ysb.ap(), AF.Copy, scale=V("sc3", 0, 1, parts=64), inc="a_fb")
        wait_dma("vector")
        b.wait("vector", "a_fb")
        vtt(ysb.ap(), ysb.ap(), lngr.ap(), ALU.mult, inc="v_fb")
        bar("vector")
        vtt(ysb.ap(), ysb.ap(), lnbr.ap(), ALU.add, inc="v_fb")
        bar("vector")
        if _FBL < 9:
            b.wait("sync", "v_fb"); b.wait("sync", "a_fb")
            ar.pop(); ar.pop(); return
        yT = ar.alloc(8 * NS)
        svT = ar.alloc(8 * NS)
        ar.push()
        b.wait("tensor", "v_fb")
        for j in range(8):
            b.wait("tensor", "f_psD")
            transpose(psD[0:128, 0:64], ysb.c(j * 128, (j + 1) * 128, parts=64), 64,
                      inc="p_psD")
            b.wait("scalar", "p_psD")
            act(yT.c(j * NS, (j + 1) * NS), psD[0:128, 0:64], AF.Copy, inc="f_psD")
        wvpt = ar.alloc(8 * D)
        svsb = ar.alloc(D, parts=64)
        load_tiles(g["wvp"], D, D, wvpt)
        wait_dma("tensor")
        b.wait("tensor", "f_psD")
        for h in range(2):
            ps, pss = (psA, "A") if h == 0 else (psB, "B")
            b.wait("tensor", "f_ps" + pss)
            for kt in range(8):
                mm(ps[0:64, 0:512], yT.c(kt * NS, (kt + 1) * NS),
                   wvpt.c(kt * D + h * 512, kt * D + (h + 1) * 512), kt == 0, kt == 7,
                   inc="p_ps" + pss if kt == 7 else None)
            b.wait("scalar", "p_ps" + pss)
            act(svsb.c(h * 512, (h + 1) * 512, parts=64), ps[0:64, 0:512], AF.Copy,
                inc="f_ps" + pss)
        b.wait("tensor", "f_psA")
        b.wait("tensor", "f_psB")
        for j in range(8):
            b.wait("tensor", "f_psD")
            transpose(psD[0:128, 0:64], svsb.c(j * 128, (j + 1) * 128, parts=64), 64,
                      inc="p_psD")
            b.wait("scalar", "p_psD")
            act(svT.c(j * NS, (j + 1) * NS), psD[0:128, 0:64], AF.Copy, inc="f_psD")
        b.wait("sync", "f_psD")
        ar.pop()
        ar.push()
        wot = ar.alloc(8 * D)
        load_tiles(g["wo"], D, D, wot)
        bvpc = ar.alloc(8)
        load_cols(g["bvp"], D, lambda t: bvpc.c(t, t + 1), scr)
        c2row = ar.alloc(D, parts=1)
        bor = ar.alloc(D, parts=1)
        dma_in(g["bo"].unsqueeze(0), bor.ap())
        wait_dma("tensor")
        b.wait("tensor", "f_psD")
        b.wait("tensor", "f_psE")
        for h in range(2):
            b.wait("tensor", "f_psE")
            for kt in range(8):
                mm(psE[0:1, 0:512], bvpc.c(kt, kt + 1),
                   wot.c(kt * D + h * 512, kt * D + (h + 1) * 512), kt == 0, kt == 7,
                   inc="p_psE" if kt == 7 else None)
            b.wait("vector", "p_psE")
            b.wait("vector", SEM_DMA)
            vtt(c2row.c(h * 512, (h + 1) * 512, parts=1), psE[0:1, 0:512],
                bor.c(h * 512, (h + 1) * 512, parts=1), ALU.add, inc="f_psE")
        bar("vector")
        c2rep = ar.alloc(D, parts=64)
        for h in range(2):
            ps, pss = (psA, "A") if h == 0 else (psB, "B")
            b.wait("tensor", "f_ps" + pss)
            for kt in range(8):
                mm(ps[0:64, 0:512], svT.c(kt * NS, (kt + 1) * NS),
                   wot.c(kt * D + h * 512, kt * D + (h + 1) * 512), kt == 0, kt == 7,
                   inc="p_ps" + pss if kt == 7 else None)
            b.wait("tensor", "f_psE")
            b.wait("tensor", "v_fb")
            bcast(psE[0:64, 0:512], c2row.c(h * 512, (h + 1) * 512, parts=1), 64,
                  inc="p_psE")
            b.wait("scalar", "p_psE")
            act(c2rep.c(h * 512, (h + 1) * 512, parts=64), psE[0:64, 0:512], AF.Copy,
                inc="f_psE")
            b.wait("vector", "p_ps" + pss)
            b.wait("vector", "f_psE")
            vtt(svwo2.c(h * 512, (h + 1) * 512, parts=64), ps[0:64, 0:512],
                c2rep.c(h * 512, (h + 1) * 512, parts=64), ALU.add, inc="f_ps" + pss)
        bar("vector")
        if tapd("svwo2", [NS, D]):
            b.wait("sync", "f_psA")
            b.wait("sync", "f_psB")
            dma_out(g["dbg_svwo2"], svwo2.ap(), sem="dbgw")
        b.wait("sync", "f_psA")
        b.wait("sync", "f_psB")
        ar.pop()

        if "fc" not in _FSUB:
            ar.pop(); ar.pop()
            return
        sin_c = ar.alloc(4 * D)
        sout = ar.alloc(4 * D)
        for c in range(NCH):
            b.wait("sync", "f_psA")
            b.wait("sync", "f_psB")
            b.wait("sync", out_sem)
            for j in range(4):
                blk = c * 4 + j
                dma_in(in_hbm[blk * 128:(blk + 1) * 128, :], sin_c.c(j * D, (j + 1) * D))
            wait_dma("vector")
            for j in range(4):
                blk = c * 4 + j
                for h in range(2):
                    ps, pss = (psA, "A") if h == 0 else (psB, "B")
                    b.wait("tensor", "f_ps" + pss)
                    mm(ps[0:128, 0:512], alphaT.c(blk * 128, (blk + 1) * 128, parts=64),
                       svwo2.c(h * 512, (h + 1) * 512, parts=64), True, True,
                       inc="p_ps" + pss)
                    b.wait("vector", "p_ps" + pss)
                    vstt(sout.c(j * D + h * 512, j * D + (h + 1) * 512),
                         ps[0:128, 0:512], 0.5, ALU.mult, ALU.add,
                         sin_c.c(j * D + h * 512, j * D + (h + 1) * 512),
                         inc="f_ps" + pss)
            b.wait("sync", "f_psA")
            b.wait("sync", "f_psB")
            for j in range(4):
                blk = c * 4 + j
                dma_out(out_hbm[blk * 128:(blk + 1) * 128, :],
                        sout.c(j * D, (j + 1) * D), sem=out_sem)
        ar.pop()
        ar.pop()

    m_static = {
        "xdbl": ar.alloc(T, parts=96),
        "dtT": ar.alloc(E, parts=64),
        "xpT": ar.alloc(16 * 96),
    }

    def m_pre():
        ar.push()
        wscr = ar.alloc(2 * E)
        scr = ar.alloc(3 * D, parts=1)
        tbuf = ar.alloc(128)
        for rt in range(32):
            b.wait("sync", "p_psD")
            dma_in(g["in_proj_w"][rt * 128:(rt + 1) * 128, :], wscr.c(0, D))
            wait_dma("tensor")
            for dt_i in range(8):
                b.wait("tensor", "f_psD")
                transpose(psD[0:128, 0:128], wscr.c(dt_i * 128, (dt_i + 1) * 128), 128,
                          inc="p_psD")
                b.wait("scalar", "p_psD")
                b.wait("scalar", "mpw")
                act(tbuf.ap(), psD[0:128, 0:128], AF.Copy, inc="f_psD")
                b.wait("sync", "f_psD")
                dma_out(g["ipT_h"][dt_i, :, rt * 128:(rt + 1) * 128], tbuf.ap(),
                        sem="mpw")
        for mt in range(8):
            b.wait("sync", "p_psD")
            dma_in(g["out_proj_w"][mt * 128:(mt + 1) * 128, :], wscr.c(0, E))
            wait_dma("tensor")
            for kt in range(16):
                b.wait("tensor", "f_psD")
                transpose(psD[0:128, 0:128], wscr.c(kt * 128, (kt + 1) * 128), 128,
                          inc="p_psD")
                b.wait("scalar", "p_psD")
                b.wait("scalar", "mpw")
                act(tbuf.ap(), psD[0:128, 0:128], AF.Copy, inc="f_psD")
                b.wait("sync", "f_psD")
                dma_out(g["opT_h"][mt, :, kt * 128:(kt + 1) * 128], tbuf.ap(),
                        sem="mpw")
        dma_in(g["conv_w"].rearrange("(et p) k -> p et k", p=128),
               V("convw", 0, 64).rearrange("p (et k) -> p et k", k=4))
        dma_in(g["A_log"].rearrange("(et p) n -> p et n", p=128),
               V("Acols", 0, 256).rearrange("p (et n) -> p et n", n=N))
        load_cols(g["conv_b"], E, lambda t: V("convb", t), scr)
        load_cols(g["Dvec"], E, lambda t: V("Dc", t), scr)
        load_cols(g["dt_proj_b"], E, lambda t: V("dtb", t), scr)
        wait_dma("scalar")
        act(V("Acols", 0, 256), V("Acols", 0, 256), AF.Exp, inc="a_mp")
        b.wait("vector", "a_mp")
        vts(V("Acols", 0, 256), V("Acols", 0, 256), -1.0, ALU.mult, inc="v_mp")
        bar("vector")
        for rt in range(16):
            b.wait("sync", "p_psD")
            dma_in(g["dt_proj_w"][rt * 128:(rt + 1) * 128, :], wscr.c(0, RK))
            wait_dma("tensor")
            b.wait("tensor", "f_psD")
            transpose(psD[0:64, 0:128], wscr.c(0, RK), 128, inc="p_psD")
            b.wait("scalar", "p_psD")
            act(m_static["dtT"].c(rt * 128, (rt + 1) * 128, parts=64),
                psD[0:64, 0:128], AF.Copy, inc="f_psD")
        b.wait("sync", "p_psD")
        dma_in(g["x_proj_w"], wscr.c(0, E, parts=96))
        wait_dma("tensor")
        for et in range(16):
            b.wait("tensor", "f_psD")
            transpose(psD[0:128, 0:96], wscr.c(et * 128, (et + 1) * 128, parts=96), 96,
                      inc="p_psD")
            b.wait("scalar", "p_psD")
            act(m_static["xpT"].c(et * 96, (et + 1) * 96), psD[0:128, 0:96], AF.Copy,
                inc="f_psD")
        b.wait("vector", "f_psD")
        b.wait("sync", "mpw")
        ar.pop()

    def m1():
        ar.push()
        sin1 = ar.alloc(D)
        xn1 = ar.alloc(D)
        sq1 = ar.alloc(D)
        xnT = ar.alloc(8 * TC)
        wsl = ar.alloc(2 * 1024)
        CB = TC + 16
        cvb = ar.alloc(2 * CB)
        xce = ar.alloc(TC)
        tails = ar.alloc(64)
        b.do("vector", lambda e: e.memset(tails.ap(), 0.0), inc="v_m1")
        bar("vector")
        b.wait("sync", "v_m1")

        for c in range(NCH):
            for j in range(4):
                blk = c * 4 + j
                b.wait("sync", "f_psD")
                dma_in(g["s1"][blk * 128:(blk + 1) * 128, :], sin1.ap())
                wait_dma("vector")
                wait_dma("scalar")
                layernorm_block(lambda: sin1.ap(), lambda: xn1.ap(), sq1)
                b.wait("tensor", "a_ln")
                b.wait("tensor", "f_psD")
                for dt_i in range(8):
                    transpose(psD[0:128, (dt_i % 4) * 128:(dt_i % 4 + 1) * 128],
                              xn1.c(dt_i * 128, (dt_i + 1) * 128), 128,
                              inc="p_psD" if dt_i % 4 == 3 else None)
                    if dt_i % 4 == 3:
                        b.wait("scalar", "p_psD")
                        for q2 in range(4):
                            di = dt_i - 3 + q2
                            act(xnT.c(di * TC + j * 128, di * TC + (j + 1) * 128),
                                psD[0:128, q2 * 128:(q2 + 1) * 128], AF.Copy,
                                inc="f_psD" if q2 == 3 else None)
                        if dt_i == 3:
                            b.wait("tensor", "f_psD")
            b.wait("tensor", "f_psD")
            for et in range(32):
                wb = wsl.sub((et % 2) * 1024, (et % 2) * 1024 + 1024)
                b.wait("sync", "f_psA")
                b.wait("sync", "f_psB")
                for kt in range(8):
                    dma_in(g["ipT_h"][kt, :, et * 128:(et + 1) * 128],
                           wb.c(kt * 128, (kt + 1) * 128))
                wait_dma("tensor")
                ps, pss = (psA, "A") if et % 2 == 0 else (psB, "B")
                b.wait("tensor", "f_ps" + pss)
                for kt in range(8):
                    mm(ps[0:128, 0:512], wb.c(kt * 128, (kt + 1) * 128),
                       xnT.c(kt * TC, (kt + 1) * TC), kt == 0, kt == 7,
                       inc="p_ps" + pss if kt == 7 else None)
                b.wait("scalar", "p_ps" + pss)
                if et < 16:
                    cb = cvb.sub((et % 2) * CB, (et % 2) * CB + CB)
                    b.wait("scalar", "v_cv")
                    act(cb.c(3, 3 + TC), ps[0:128, 0:512], AF.Copy, inc="f_ps" + pss)
                    b.wait("vector", "f_ps" + pss)
                    b.wait("vector", "m1w")
                    b.wait("vector", "mm_xd")
                    vcopy(cb.c(0, 3), tails.c(et * 3, (et + 1) * 3), inc="v_cv")
                    bar("vector")
                    vts(xce.ap(), cb.c(0, TC), V("convw", et * 4 + 0), ALU.mult,
                        inc="v_cv")
                    bar("vector")
                    for k in range(1, 4):
                        vstt(xce.ap(), cb.c(k, k + TC), V("convw", et * 4 + k),
                             ALU.mult, ALU.add, xce.ap(), inc="v_cv")
                        bar("vector")
                    vcopy(tails.c(et * 3, (et + 1) * 3), cb.c(TC, TC + 3), inc="v_cv")
                    b.wait("scalar", "v_cv")
                    act(xce.ap(), xce.ap(), AF.Silu, bias=V("convb", et), inc="a_m1")
                    b.wait("sync", "a_m1")
                    dma_out(g["xc_h"][et * 128:(et + 1) * 128, c * TC:(c + 1) * TC],
                            xce.ap(), sem="m1w")
                    b.wait("tensor", "a_m1")
                    if et == 0 and c > 0:
                        b.wait("tensor", "f_psC")
                    mm(psC[0:96, 0:512], m_static["xpT"].c(et * 96, (et + 1) * 96),
                       xce.ap(), et == 0, et == 15, inc="mm_xd")
                else:
                    b.wait("scalar", "m1w")
                    b.wait("scalar", "mm_xd")
                    act(xce.ap(), ps[0:128, 0:512], AF.Silu, inc="f_ps" + pss)
                    b.wait("sync", "f_ps" + pss)
                    dma_out(g["zs_h"][(et - 16) * 128:(et - 15) * 128,
                                      c * TC:(c + 1) * TC], xce.ap(), sem="m1w")
            b.wait("scalar", "mm_xd")
            act(m_static["xdbl"].c(c * TC, (c + 1) * TC, parts=96), psC[0:96, 0:512],
                AF.Copy, inc="f_psC")
            b.wait("sync", "f_psC")
            dma_out(g["xdbl_h"][:, c * TC:(c + 1) * TC],
                    m_static["xdbl"].c(c * TC, (c + 1) * TC, parts=96), sem="m1w")
        b.wait("sync", "m1w")
        ar.pop()

    def m2():
        ar.push()
        Brep = ar.alloc(2 * TC)
        Crep = ar.alloc(2 * TC)
        s1b = ar.alloc(4 * D)
        yall = ar.alloc(16 * TC)
        xcq = ar.alloc(ETQ * TC)
        zsq = ar.alloc(ETQ * TC)
        dlq = ar.alloc(ETQ * TC)
        dxq = ar.alloc(ETQ * TC)
        abuf = ar.alloc(ETQ * TC)
        ubuf = ar.alloc(ETQ * TC)
        opb = ar.alloc(16 * 128)
        mo = ar.alloc(TC)
        hs_scr = ar.alloc(16)

        def a3(tile):
            return tile.ap().rearrange("p (e t) -> p e t", e=ETQ)

        def bc_load(c, n):
            p = n % 2
            b.wait("sync", "v_m2")
            dma_in(g["xdbl_h"][RK + n:RK + n + 1, c * TC:(c + 1) * TC]
                   .broadcast_to([128, TC]), Brep.c(p * TC, (p + 1) * TC))
            dma_in(g["xdbl_h"][RK + N + n:RK + N + n + 1, c * TC:(c + 1) * TC]
                   .broadcast_to([128, TC]), Crep.c(p * TC, (p + 1) * TC))
            bc_cnt[0] = b.cnt[SEM_DMA]

        bc_cnt = [0]
        for c in range(NCH):
            b.wait("sync", "m2w")
            b.wait("sync", "dbgw")
            for j in range(4):
                blk = c * 4 + j
                dma_in(g["s1"][blk * 128:(blk + 1) * 128, :], s1b.c(j * D, (j + 1) * D))
            for q in range(NEQ):
                b.wait("scalar", "v_m2")
                for i in range(ETQ):
                    et = q * ETQ + i
                    ps, pss = (psA, "A") if i % 2 == 0 else (psB, "B")
                    b.wait("tensor", "f_ps" + pss)
                    mm(ps[0:128, 0:512],
                       m_static["dtT"].c(et * 128, (et + 1) * 128, parts=64),
                       m_static["xdbl"].c(c * TC, (c + 1) * TC, parts=64),
                       True, True, inc="p_ps" + pss)
                    b.wait("scalar", "p_ps" + pss)
                    act(dlq.c(i * TC, (i + 1) * TC), ps[0:128, 0:512], AF.Exp,
                        bias=V("dtb", et), inc="f_ps" + pss)
                act(dlq.ap(), dlq.ap(), AF.Ln, bias=1.0, inc="a_m2")
                b.wait("sync", "v_m2")
                for i in range(ETQ):
                    et = q * ETQ + i
                    dma_in(g["xc_h"][et * 128:(et + 1) * 128, c * TC:(c + 1) * TC],
                           xcq.c(i * TC, (i + 1) * TC))
                    dma_in(g["zs_h"][et * 128:(et + 1) * 128, c * TC:(c + 1) * TC],
                           zsq.c(i * TC, (i + 1) * TC))
                wait_dma("vector")
                if "delta" in _DEBUG:
                    if "dbg_delta" not in g:
                        g["dbg_delta"] = nc.dram_tensor("dbg_delta", [E, T], F32,
                                                        kind="ExternalOutput").ap()
                    b.wait("sync", "a_m2")
                    for i2 in range(ETQ):
                        dma_out(g["dbg_delta"][(q * ETQ + i2) * 128:(q * ETQ + i2 + 1) * 128,
                                               c * TC:(c + 1) * TC],
                                dlq.c(i2 * TC, (i2 + 1) * TC), sem="dbgw")
                b.wait("vector", "a_m2")
                vtt(dxq.ap(), dlq.ap(), xcq.ap(), ALU.mult, inc="v_m2")
                bar("vector")
                b.wait("sync", "v_m2")
                bc_load(c, 0)
                bc_waits = [bc_cnt[0]]
                for n in range(N):
                    if n + 1 < N:
                        bc_load(c, n + 1)
                        bc_waits.append(bc_cnt[0])
                    b.wait("scalar", "v_m2")
                    for i in range(ETQ):
                        et = q * ETQ + i
                        act(abuf.c(i * TC, (i + 1) * TC), dlq.c(i * TC, (i + 1) * TC),
                            AF.Exp, scale=V("Acols", et * N + n), inc="a_m2")
                    b.wait_val("vector", SEM_DMA, bc_waits[n])
                    b.wait("vector", "a_m2")
                    vtt(a3(ubuf), a3(dxq),
                        Brep.c((n % 2) * TC, (n % 2 + 1) * TC).unsqueeze(1)
                        .broadcast_to([128, ETQ, TC]), ALU.mult, inc="v_m2")
                    bar("vector")
                    vtt(hs_scr.c(0, ETQ), a3(abuf)[:, :, 0:1].squeeze(2),
                        h_state.ap().rearrange("p (n e) -> p n e", n=N)
                        [:, n:n + 1, q * ETQ:(q + 1) * ETQ].squeeze(1),
                        ALU.mult, inc="v_m2")
                    bar("vector")
                    vtt(a3(ubuf)[:, :, 0:1].squeeze(2), a3(ubuf)[:, :, 0:1].squeeze(2),
                        hs_scr.c(0, ETQ), ALU.add, inc="v_m2")
                    b.do("vector", lambda e: e.memset(a3(abuf)[:, :, 0:1].squeeze(2),
                                                      0.0), inc="v_m2")
                    bar("vector")
                    b.do("vector", lambda e: e.tensor_tensor_scan(
                        abuf.ap(), abuf.ap(), ubuf.ap(), 0.0, ALU.mult, ALU.add),
                        inc="v_m2")
                    bar("vector")
                    b.do("vector", lambda e, n=n, q=q: e.tensor_copy(
                        h_state.ap().rearrange("p (n e) -> p n e", n=N)
                        [:, n:n + 1, q * ETQ:(q + 1) * ETQ].squeeze(1),
                        a3(abuf)[:, :, TC - 1:TC].squeeze(2)), inc="v_m2")
                    vtt(a3(abuf), a3(abuf),
                        Crep.c((n % 2) * TC, (n % 2 + 1) * TC).unsqueeze(1)
                        .broadcast_to([128, ETQ, TC]), ALU.mult, inc="v_m2")
                    bar("vector")
                    if n == 0:
                        vcopy(yall.c(q * ETQ * TC, (q + 1) * ETQ * TC), abuf.ap(),
                              inc="v_m2")
                    else:
                        vtt(yall.c(q * ETQ * TC, (q + 1) * ETQ * TC),
                            yall.c(q * ETQ * TC, (q + 1) * ETQ * TC), abuf.ap(),
                            ALU.add, inc="v_m2")
                    bar("vector")
                for i in range(ETQ):
                    et = q * ETQ + i
                    vstt(yall.c((q * ETQ + i) * TC, (q * ETQ + i + 1) * TC),
                         xcq.c(i * TC, (i + 1) * TC), V("Dc", et), ALU.mult, ALU.add,
                         yall.c((q * ETQ + i) * TC, (q * ETQ + i + 1) * TC), inc="v_m2")
                bar("vector")
                vtt(yall.c(q * ETQ * TC, (q + 1) * ETQ * TC),
                    yall.c(q * ETQ * TC, (q + 1) * ETQ * TC), zsq.ap(), ALU.mult,
                    inc="v_m2")
                bar("vector")
                if "yg" in _DEBUG:
                    if "dbg_yg" not in g:
                        g["dbg_yg"] = nc.dram_tensor("dbg_yg", [E, T], F32,
                                                     kind="ExternalOutput").ap()
                    b.wait("sync", "v_m2")
                    for i2 in range(ETQ):
                        dma_out(g["dbg_yg"][(q * ETQ + i2) * 128:(q * ETQ + i2 + 1) * 128,
                                            c * TC:(c + 1) * TC],
                                yall.c((q * ETQ + i2) * TC, (q * ETQ + i2 + 1) * TC),
                                sem="dbgw")
            b.wait("tensor", "v_m2")
            for mt in range(8):
                b.wait("sync", "f_psA")
                b.wait("sync", "f_psB")
                dma_in(g["opT_h"][mt], opb.ap())
                wait_dma("tensor")
                ps, pss = (psA, "A") if mt % 2 == 0 else (psB, "B")
                b.wait("tensor", "f_ps" + pss)
                for kt in range(16):
                    mm(ps[0:128, 0:512], opb.c(kt * 128, (kt + 1) * 128),
                       yall.c(kt * TC, (kt + 1) * TC), kt == 0, kt == 15,
                       inc="p_ps" + pss if kt == 15 else None)
                b.wait("scalar", "p_ps" + pss)
                b.wait("scalar", "p_psD")
                act(mo.ap(), ps[0:128, 0:512], AF.Copy, inc="f_ps" + pss)
                b.wait("tensor", "f_ps" + pss)
                b.wait("tensor", "f_psD")
                for j in range(4):
                    transpose(psD[0:128, j * 128:(j + 1) * 128],
                              mo.c(j * 128, (j + 1) * 128), 128,
                              inc="p_psD" if j == 3 else None)
                b.wait("vector", "p_psD")
                for j in range(4):
                    vtt(s1b.c(j * D + mt * 128, j * D + (mt + 1) * 128),
                        s1b.c(j * D + mt * 128, j * D + (mt + 1) * 128),
                        psD[0:128, j * 128:(j + 1) * 128], ALU.add,
                        inc="f_psD" if j == 3 else None)
            b.wait("sync", "f_psD")
            for j in range(4):
                blk = c * 4 + j
                dma_out(g["s2"][blk * 128:(blk + 1) * 128, :],
                        s1b.c(j * D, (j + 1) * D), sem="m2w")
        b.wait("sync", "m2w")
        ar.pop()

    if "mpre" in _PHASES:
        m_pre()
    if "ffn1" in _PHASES:
        ffn_call(g["x_in"], g["s1"], "s1w", cid=0)
        b.wait("sync", "s1w")
    if "m1" in _PHASES:
        m1()
    if "m2" in _PHASES:
        m2()
    if "ffn2" in _PHASES:
        ffn_call(g["s2"], g["out"], SEM_OUT, cid=1)
    else:
        b.do("sync", lambda e: e.dma_start(out=g["out"][0:128, :], in_=g["s1"][0:128, :]),
             inc=SEM_OUT, inc_n=16)

    for name, src in [("s1", g["s1"]), ("s2", g["s2"]), ("xdbl", g["xdbl_h"]),
                      ("xc", g["xc_h"]), ("zs", g["zs_h"])]:
        if name in dbg:
            b.do("sync", lambda e, d=dbg[name], s=src: e.dma_start(out=d, in_=s),
                 inc=SEM_OUT, inc_n=16)

    b.wait("sync", SEM_OUT)

    with nc.Block() as block:
        b.flush(block)
    return nc


_REN = {
    "L0_attn_in_proj_w": "in_proj_w", "L0_attn_conv_w": "conv_w",
    "L0_attn_conv_b": "conv_b", "L0_attn_x_proj_w": "x_proj_w",
    "L0_attn_dt_proj_w": "dt_proj_w", "L0_attn_dt_proj_b": "dt_proj_b",
    "L0_attn_A_log": "A_log", "L0_attn_D": "Dvec",
    "L0_attn_out_proj_w": "out_proj_w",
    "L0_ffn_slot_init": "slot_init", "L0_ffn_w1": "w1", "L0_ffn_b1": "b1",
    "L0_ffn_w2": "w2", "L0_ffn_b2": "b2", "L0_ffn_tau": "tau",
    "L0_ffn_Wih": "Wih", "L0_ffn_Whh": "Whh", "L0_ffn_bih": "bih",
    "L0_ffn_bhh": "bhh", "L0_ffn_wq": "wq", "L0_ffn_bq": "bq",
    "L0_ffn_wk": "wk", "L0_ffn_bk": "bk", "L0_ffn_wv": "wv",
    "L0_ffn_bv": "bv", "L0_ffn_ln_g": "ln_g", "L0_ffn_ln_b": "ln_b",
    "L0_ffn_wvp": "wvp", "L0_ffn_bvp": "bvp", "L0_ffn_wo": "wo",
    "L0_ffn_bo": "bo",
}


def kernel(**inputs):
    x = np.ascontiguousarray(np.asarray(inputs["L0_attn_hidden_states"],
                                        dtype=np.float32))
    weights = {v: np.ascontiguousarray(np.asarray(inputs[k], dtype=np.float32))
               for k, v in _REN.items()}
    nc = build_program()
    in_maps = []
    for c in range(8):
        m = {"x_in": x[c]}
        m.update(weights)
        in_maps.append(m)
    res = run_bass_kernel_spmd(nc, in_maps, core_ids=list(range(8)))
    return np.stack([res.results[c]["out"] for c in range(8)], axis=0)


# version 06
# speedup vs baseline: 1.0410x; 1.0410x over previous
import numpy as np
import concourse.bass as bass
import concourse.mybir as mybir
from concourse.bass_utils import run_bass_kernel_spmd

F32 = mybir.dt.float32
AF = mybir.ActivationFunctionType
ALU = mybir.AluOpType
AX = mybir.AxisListType

T = 4096
D = 1024
E = 2048
N = 16
RK = 64
NS = 64
DH = 512
DQ = 256
TC = 512
NCH = T // TC
NBLK = T // 128
EQ = 512
NEQ = E // EQ
ETQ = EQ // 128

ARENA_COLS = 49152

_DEBUG = []
_TAPCALL = 0
_PHASES = ["mpre", "ffn1", "m1", "m2", "ffn2"]
_FSUB = ["fpre", "fa", "fb", "fc"]
_FBL = 99


class Builder:
    def __init__(self, nc):
        self.nc = nc
        self.q = {"sync": [], "tensor": [], "scalar": [], "vector": [], "gpsimd": []}
        self.cnt = {}
        self.sems = {}
        self.last_inc = {}

    def sem(self, name):
        if name not in self.sems:
            self.sems[name] = self.nc.semaphore(name).__enter__()
            self.cnt[name] = 0
        return self.sems[name]

    def on(self, eng, fn):
        self.q[eng].append(fn)

    def do(self, eng, build, inc=None, inc_n=1):
        if inc is not None:
            self.sem(inc)
            self.cnt[inc] += inc_n
            s, n = self.sems[inc], inc_n

            def fn(e, build=build, s=s, n=n):
                build(e).then_inc(s, n)
            self.on(eng, fn)
            self.last_inc[eng] = inc
        else:
            self.on(eng, build)
            self.last_inc[eng] = None

    def wait(self, eng, sem):
        self.sem(sem)
        v = self.cnt[sem]
        if v == 0:
            return
        s = self.sems[sem]
        self.on(eng, lambda e, s=s, v=v: e.wait_ge(s, v))

    def wait_val(self, eng, sem, v):
        s = self.sem(sem)
        if v > 0:
            self.on(eng, lambda e, s=s, v=v: e.wait_ge(s, v))

    def fence(self):
        for eng in ["sync", "tensor", "scalar", "vector", "gpsimd"]:
            for sem in list(self.cnt):
                self.wait(eng, sem)

    def flush(self, block):
        for eng in ["sync", "tensor", "scalar", "vector", "gpsimd"]:
            fns = self.q[eng]
            if not fns:
                continue

            def runner(e, fns=fns):
                for f in fns:
                    f(e)
            getattr(block, eng)(runner)


class Arena:
    def __init__(self, handle):
        self.h = handle
        self.base = 0
        self.marks = []
        self.peak = 0

    def push(self):
        self.marks.append(self.base)

    def pop(self):
        self.base = self.marks.pop()

    def alloc(self, cols, parts=128):
        off = (self.base + 15) // 16 * 16
        self.base = off + cols
        self.peak = max(self.peak, self.base)
        assert self.base <= ARENA_COLS, f"arena overflow: {self.base}"
        return Tile(self.h, parts, off, cols)


class Tile:
    def __init__(self, h, parts, off, cols):
        self.h = h
        self.parts = parts
        self.off = off
        self.cols = cols

    def ap(self):
        return self.h[0:self.parts, self.off:self.off + self.cols]

    def c(self, a, b, parts=None):
        parts = self.parts if parts is None else parts
        assert 0 <= a <= b <= self.cols
        return self.h[0:parts, self.off + a:self.off + b]

    def sub(self, a, b):
        return Tile(self.h, self.parts, self.off + a, b - a)


def build_program():
    nc = bass.Bass("TRN2", target_bir_lowering=False)
    g = {}

    def din(name, shape):
        g[name] = nc.dram_tensor(name, shape, F32, kind="ExternalInput").ap()

    din("x_in", [T, D])
    din("in_proj_w", [2 * E, D]);  din("conv_w", [E, 4]);   din("conv_b", [E])
    din("x_proj_w", [96, E]);      din("dt_proj_w", [E, RK]); din("dt_proj_b", [E])
    din("A_log", [E, N]);          din("Dvec", [E]);        din("out_proj_w", [D, E])
    din("slot_init", [NS, D]);     din("w1", [2 * D, DH]);  din("b1", [DH])
    din("w2", [DH, NS]);           din("b2", [NS]);         din("tau", [1])
    din("Wih", [3 * D, D]);        din("Whh", [3 * D, D])
    din("bih", [3 * D]);           din("bhh", [3 * D])
    din("wq", [D, DQ]);            din("bq", [DQ])
    din("wk", [D, DQ]);            din("bk", [DQ])
    din("wv", [D, D]);             din("bv", [D])
    din("ln_g", [D]);              din("ln_b", [D])
    din("wvp", [D, D]);            din("bvp", [D])
    din("wo", [D, D]);             din("bo", [D])

    g["out"] = nc.dram_tensor("out", [T, D], F32, kind="ExternalOutput").ap()
    g["s1"] = nc.dram_tensor("s1_scr", [T, D], F32).ap()
    g["s2"] = nc.dram_tensor("s2_scr", [T, D], F32).ap()
    g["xc_h"] = nc.dram_tensor("xc_scr", [E, T], F32).ap()
    g["zs_h"] = nc.dram_tensor("zs_scr", [E, T], F32).ap()
    g["xdbl_h"] = nc.dram_tensor("xdbl_scr", [96, T], F32).ap()
    g["ipT_h"] = nc.dram_tensor("ipT_scr", [8, 128, 2 * E], F32).ap()
    g["opT_h"] = nc.dram_tensor("opT_scr", [8, 128, E], F32).ap()

    dbg = {}
    for name, shape in [("s1", [T, D]), ("s2", [T, D]), ("xdbl", [96, T]),
                        ("xc", [E, T]), ("zs", [E, T])]:
        if name in _DEBUG:
            dbg[name] = nc.dram_tensor("dbg_" + name, shape, F32,
                                       kind="ExternalOutput").ap()

    arena_h = nc.alloc_sbuf_tensor("arena", [128, ARENA_COLS], F32)
    ar = Arena(arena_h)
    b = Builder(nc)

    ident = ar.alloc(128)
    ones_row = ar.alloc(128, parts=1)
    h_state = ar.alloc(256)
    vecs = ar.alloc(1024)

    VE = {}
    _vo = [0]

    def valloc(n):
        o = _vo[0]; _vo[0] += n
        assert _vo[0] <= 1024
        return o

    for nm, n in [("msum", 1), ("mneg", 1), ("lv", 1), ("rs", 1),
                  ("v8", 8), ("m1n", 1), ("Z", 1), ("Zr", 1),
                  ("gelub", 4), ("b2c", 1), ("tauc", 1), ("taubc", 1),
                  ("denr", 1), ("att1", 1), ("att2", 1), ("att3", 1),
                  ("sc1", 2), ("sc2", 2), ("sc3", 2),
                  ("bihc", 24), ("bhhc", 24), ("bqc", 2), ("bkc", 2),
                  ("convw", 64), ("convb", 16), ("Dc", 16), ("dtb", 16),
                  ("Acols", 256)]:
        VE[nm] = valloc(n)

    def V(nm, k=0, w=1, parts=128):
        o = vecs.off + VE[nm] + k
        return arena_h[0:parts, o:o + w]

    psA = nc.alloc_psum_tensor("psA", [128, 512], F32)
    psB = nc.alloc_psum_tensor("psB", [128, 512], F32)
    psC = nc.alloc_psum_tensor("psC", [128, 512], F32)
    psD = nc.alloc_psum_tensor("psD", [128, 512], F32)
    psE = nc.alloc_psum_tensor("psE", [128, 512], F32)
    ps_slot = nc.alloc_psum_tensor("ps_slot", [64, 1025], F32)

    SEM_DMA = "dq"
    SEM_OUT = "dout"

    def dma_in(src_ap, dst_ap):
        b.do("sync", lambda e, s=src_ap, d=dst_ap: e.dma_start(out=d, in_=s),
             inc=SEM_DMA, inc_n=16)

    def dma_out(dst_ap, src_ap, sem=SEM_OUT):
        b.do("sync", lambda e, s=src_ap, d=dst_ap: e.dma_start(out=d, in_=s),
             inc=sem, inc_n=16)

    def wait_dma(eng):
        b.wait(eng, SEM_DMA)

    def bar(eng):
        li = b.last_inc.get(eng)
        assert li is not None, f"bar({eng}) after inc-less instruction"
        b.wait(eng, li)

    def mm(ps_ap, s_ap, m_ap, start, stop, inc=None):
        b.do("tensor",
             lambda e, o=ps_ap, s=s_ap, m=m_ap, st=start, sp=stop:
             e.matmul(o, s, m, start=st, stop=sp, skip_group_check=True), inc=inc)

    def transpose(ps_ap, in_ap, rows, inc=None):
        b.do("tensor",
             lambda e, o=ps_ap, i=in_ap, r=rows:
             e.transpose(o, i, ident.c(0, r, parts=r)), inc=inc)

    def act(out_ap, in_ap, func, bias=0.0, scale=1.0, accum=None, inc=None):
        if func == AF.Copy and not isinstance(bias, float):
            func = AF.Identity
        b.do("scalar",
             lambda e, o=out_ap, i=in_ap, f=func, bi=bias, sc=scale, a=accum:
             e.activation(o, i, f, bias=bi, scale=sc, accum_out=a), inc=inc)

    def vtt(out_ap, a_ap, b_ap, op, inc=None):
        b.do("vector", lambda e, o=out_ap, x=a_ap, y=b_ap, p=op:
             e.tensor_tensor(o, x, y, p), inc=inc)

    def vts(out_ap, a_ap, s, op, inc=None):
        b.do("vector", lambda e, o=out_ap, x=a_ap, sc=s, p=op:
             e.tensor_scalar(o, x, sc, None, p), inc=inc)

    def vstt(out_ap, in0_ap, s, op0, op1, in1_ap, inc=None):
        b.do("vector", lambda e, o=out_ap, x=in0_ap, sc=s, p0=op0, p1=op1, y=in1_ap:
             e.scalar_tensor_tensor(o, x, sc, y, p0, p1), inc=inc)

    def vcopy(out_ap, in_ap, inc=None):
        b.do("vector", lambda e, o=out_ap, i=in_ap: e.tensor_copy(o, i), inc=inc)

    def bcast(ps_ap, row_ap, parts, inc):
        b.do("tensor", lambda e, o=ps_ap, r=row_ap, p=parts:
             e.matmul(o, ones_row.c(0, p, parts=1), r, start=True, stop=True,
                      skip_group_check=True), inc=inc)

    def layernorm_block(x_f, out_f, sq_tile):
        xa = x_f()
        b.wait("vector", "a_ln")
        b.do("vector", lambda e, a=xa: e.reduce_sum(V("msum"), a, axis=AX.X),
             inc="v_ln")
        bar("vector")
        vts(V("mneg"), V("msum"), -1.0 / D, ALU.mult, inc="v_ln")
        b.wait("scalar", "v_ln")
        act(out_f(), x_f(), AF.Copy, bias=V("mneg"), inc="a_ln")
        act(sq_tile.ap(), out_f(), AF.Square, accum=V("lv"), inc="a_ln")
        bar("scalar")
        act(V("lv"), V("lv"), AF.Ln, scale=1.0 / D, bias=1e-5, inc="a_ln")
        bar("scalar")
        act(V("rs"), V("lv"), AF.Exp, scale=-0.5, inc="a_ln")
        bar("scalar")
        act(out_f(), out_f(), AF.Copy, scale=V("rs"), inc="a_ln")

    def load_tiles(src_full, rows, cols, dst_tile):
        nrt = (rows + 127) // 128
        for rt in range(nrt):
            r0, r1 = rt * 128, min(rows, rt * 128 + 128)
            dma_in(src_full[r0:r1, :], dst_tile.c(rt * cols, rt * cols + cols,
                                                  parts=r1 - r0))

    def load_cols(src_vec, n, dst_f, scratch_row):
        b.wait("sync", "p_psE")
        dma_in(src_vec.unsqueeze(0), scratch_row.c(0, n, parts=1))
        wait_dma("tensor")
        for t in range((n + 127) // 128):
            w = min(128, n - t * 128)
            b.wait("tensor", "f_psE")
            transpose(psE[0:w, 0:1], scratch_row.c(t * 128, t * 128 + w, parts=1), 1,
                      inc="p_psE")
            b.wait("scalar", "p_psE")
            act(dst_f(t), psE[0:w, 0:1], AF.Copy, inc="f_psE")

    CONSTS = [1e-5, 0.1, 1e-8]
    cst = ar.alloc(16)
    for i, v in enumerate(CONSTS):
        b.do("vector", lambda e, i=i, v=v: e.memset(cst.c(i, i + 1), v), inc="v_pre")
        nc.const_aps.register(F32, v, cst.c(i, i + 1))

    b.do("gpsimd", lambda e: e.iota(ident.ap(), [[-1, 128]], channel_multiplier=1,
                                    allow_small_or_imprecise_dtypes=True), inc="g_pre")
    b.wait("vector", "g_pre")
    vts(ident.ap(), ident.ap(), 0.0, ALU.is_equal, inc="v_pre")
    b.do("vector", lambda e: e.memset(ones_row.ap(), 1.0), inc="v_pre")
    b.do("vector", lambda e: e.memset(h_state.ap(), 0.0), inc="v_pre")
    bar("vector")
    b.wait("tensor", "v_pre")
    b.wait("scalar", "v_pre")
    b.wait("sync", "v_pre")

    def ffn_call(in_hbm, out_hbm, out_sem, cid=0):
        def tapd(name, shape):
            if cid == _TAPCALL and name in _DEBUG:
                if "dbg_" + name not in g:
                    g["dbg_" + name] = nc.dram_tensor("dbg_" + name, shape, F32,
                                                      kind="ExternalOutput").ap()
                return True
            return False
        ar.push()
        alpha = ar.alloc(NBLK * NS)
        alphaT = ar.alloc(T, parts=64)
        svwo2 = ar.alloc(D, parts=64)
        scr = ar.alloc(3 * D, parts=1)
        sq_s = ar.alloc(D)
        si_sb = ar.alloc(D, parts=64)

        ar.push()
        if "fpre" not in _FSUB:
            ar.pop(); ar.pop()
            return
        w1a = ar.alloc(8 * DH)
        w2h = ar.alloc(4 * NS)
        load_tiles(g["w1"][0:D, :], D, DH, w1a)
        w2r = ar.alloc(4 * NS)
        load_tiles(g["w2"], DH, NS, w2r)
        wait_dma("scalar")
        for kt in range(4):
            act(w2h.c(kt * NS, (kt + 1) * NS), w2r.c(kt * NS, (kt + 1) * NS),
                AF.Copy, scale=0.5, inc="a_fp")
        tcol = ar.alloc(16, parts=1)
        dma_in(g["tau"].unsqueeze(0), tcol.c(0, 1, parts=1))
        wait_dma("scalar")
        act(tcol.c(1, 2, parts=1), tcol.c(0, 1, parts=1), AF.Abs, inc="a_fp")
        bar("scalar")
        act(tcol.c(2, 3, parts=1), tcol.c(1, 2, parts=1), AF.Copy, bias=0.1, inc="a_fp")
        b.wait("vector", "a_fp")
        b.do("vector", lambda e: e.reciprocal(tcol.c(3, 4, parts=1),
                                              tcol.c(2, 3, parts=1)), inc="v_fp")
        bar("vector")
        b.wait("tensor", "v_fp")
        b.wait("tensor", "f_psE")
        bcast(psE[0:64, 0:1], tcol.c(3, 4, parts=1), 64, inc="p_psE")
        b.wait("scalar", "p_psE")
        act(V("tauc", parts=64), psE[0:64, 0:1], AF.Copy, inc="f_psE")
        load_cols(g["b2"], NS, lambda t: V("b2c", parts=64), scr)
        b.wait("vector", "f_psE")
        vtt(V("taubc", parts=64), V("tauc", parts=64), V("b2c", parts=64), ALU.mult,
            inc="v_fp")
        dma_in(g["slot_init"], si_sb.ap())
        onames = ar.alloc(16, parts=64)
        b.do("vector", lambda e: e.memset(onames.c(0, 1, parts=64), 1.0 / 64), inc="v_fp")
        bar("vector")
        wait_dma("tensor")
        b.wait("tensor", "v_fp")
        smean = ar.alloc(D, parts=1)
        for h in range(2):
            b.wait("tensor", "f_psE")
            mm(psE[0:1, 0:512], onames.c(0, 1, parts=64),
               si_sb.c(h * 512, (h + 1) * 512, parts=64), True, True, inc="p_psE")
            b.wait("scalar", "p_psE")
            act(smean.c(h * 512, (h + 1) * 512, parts=1), psE[0:1, 0:512], AF.Copy,
                inc="f_psE")
        smc = ar.alloc(8)
        b.wait("sync", "f_psE")
        for t in range(8):
            b.wait("tensor", "f_psE")
            transpose(psE[0:128, 0:1], smean.c(t * 128, (t + 1) * 128, parts=1), 1,
                      inc="p_psE")
            b.wait("scalar", "p_psE")
            act(smc.c(t, t + 1), psE[0:128, 0:1], AF.Copy, inc="f_psE")
        w1b = ar.alloc(8 * DH)
        load_tiles(g["w1"][D:2 * D, :], D, DH, w1b)
        wait_dma("tensor")
        b.wait("tensor", "f_psE")
        for jt in range(4):
            b.wait("tensor", "f_psE")
            for kt in range(8):
                mm(psE[0:128, 0:1],
                   w1b.c(kt * DH + jt * 128, kt * DH + (jt + 1) * 128),
                   smc.c(kt, kt + 1), kt == 0, kt == 7,
                   inc="p_psE" if kt == 7 else None)
            b.wait("scalar", "p_psE")
            act(V("gelub", jt), psE[0:128, 0:1], AF.Copy, inc="f_psE")
        b1c = ar.alloc(4)
        load_cols(g["b1"], DH, lambda t: b1c.c(t, t + 1), scr)
        b.wait("vector", "f_psE")
        vtt(V("gelub", 0, 4), V("gelub", 0, 4), b1c.c(0, 4), ALU.add, inc="v_fp")
        bar("vector")

        b.fence()
        if "fa" not in _FSUB:
            ar.pop(); ar.pop()
            return
        XW = 1040
        xw = ar.alloc(4 * XW)
        xnT = ar.alloc(8 * TC)
        hT = ar.alloc(4 * TC)
        hb_s = ar.alloc(TC)
        te_s = ar.alloc(TC)
        logT = ar.alloc(TC, parts=64)
        logtok = ar.alloc(4 * NS)
        sin_s = ar.alloc(4 * D)
        tk_s = ar.alloc(2 * NS)
        b.wait("vector", "a_fp")
        for j in range(4):
            b.do("vector", lambda e, j=j:
                 e.memset(xw.c(j * XW + 1024, j * XW + 1025), 1.0), inc="v_fa")
        bar("vector")
        b.wait("tensor", "v_fa")

        for c in range(NCH):
            b.wait("sync", "a_ln")
            b.wait("sync", "mm_slot")
            b.wait("sync", "dbgw")
            for j in range(4):
                blk = c * 4 + j
                dma_in(in_hbm[blk * 128:(blk + 1) * 128, :], sin_s.c(j * D, (j + 1) * D))
            wait_dma("vector")
            wait_dma("scalar")
            for j in range(4):
                layernorm_block(lambda j=j: sin_s.c(j * D, (j + 1) * D),
                                lambda j=j: xw.c(j * XW, j * XW + 1024), sq_s)
                if "lnstats" in _DEBUG and cid == _TAPCALL:
                    tapd("lnstats", [T, 16])
                    blk = c * 4 + j
                    b.wait("sync", "a_ln")
                    dma_out(g["dbg_lnstats"][blk * 128:(blk + 1) * 128, :],
                            V("msum", 0, 16), sem="dbgw")
            b.wait("tensor", "a_ln")
            for dt_i in range(8):
                b.wait("tensor", "f_psD")
                for j in range(4):
                    transpose(psD[0:128, j * 128:(j + 1) * 128],
                              xw.c(j * XW + dt_i * 128, j * XW + (dt_i + 1) * 128), 128,
                              inc="p_psD" if j == 3 else None)
                b.wait("scalar", "p_psD")
                act(xnT.c(dt_i * TC, (dt_i + 1) * TC), psD[0:128, 0:512], AF.Copy,
                    inc="f_psD")
            b.wait("tensor", "f_psD")
            for jt in range(4):
                ps, pss = (psA, "A") if jt % 2 == 0 else (psB, "B")
                b.wait("tensor", "f_psA")
                b.wait("tensor", "f_psB")
                for kt in range(8):
                    mm(ps[0:128, 0:512],
                       w1a.c(kt * DH + jt * 128, kt * DH + (jt + 1) * 128),
                       xnT.c(kt * TC, (kt + 1) * TC), kt == 0, kt == 7,
                       inc="p_ps" + pss if kt == 7 else None)
                b.wait("scalar", "p_ps" + pss)
                b.wait("scalar", "f_psA")
                b.wait("scalar", "f_psB")
                act(hb_s.ap(), ps[0:128, 0:512], AF.Copy, bias=V("gelub", jt), inc="a_fa")
                act(te_s.ap(), hb_s.ap(), AF.Erf, scale=0.7071067811865476, inc="a_fa")
                b.wait("vector", "a_fa")
                vstt(hT.c(jt * TC, (jt + 1) * TC), te_s.ap(), 1.0, ALU.add, ALU.mult,
                     hb_s.ap(), inc="f_ps" + pss)
            b.wait("tensor", "f_psA")
            b.wait("tensor", "f_psB")
            b.wait("tensor", "f_psC")
            for kt in range(4):
                mm(psC[0:64, 0:512], w2h.c(kt * NS, (kt + 1) * NS),
                   hT.c(kt * TC, (kt + 1) * TC), kt == 0, kt == 3,
                   inc="p_psC" if kt == 3 else None)
            b.wait("scalar", "p_psC")
            act(logT.ap(), psC[0:64, 0:512], AF.Copy, scale=V("tauc", parts=64),
                bias=V("taubc", parts=64), inc="f_psC")
            b.wait("tensor", "f_psC")
            b.wait("tensor", "f_psD")
            for j in range(4):
                transpose(psD[0:128, j * NS:(j + 1) * NS],
                          logT.c(j * 128, (j + 1) * 128, parts=64), 64,
                          inc="p_psD" if j == 3 else None)
            b.wait("scalar", "p_psD")
            act(logtok.ap(), psD[0:128, 0:256], AF.Copy, inc="f_psD")
            b.wait("vector", "f_psD")
            for j in range(4):
                blk = c * 4 + j
                b.do("vector", lambda e, j=j: e.max(out=V("v8", 0, 8),
                                                    in_=logtok.c(j * NS, (j + 1) * NS)),
                     inc="v_fa")
                bar("vector")
                vts(V("m1n"), V("v8", 0, 1), -1.0, ALU.mult, inc="v_fa")
                vts(tk_s.c(NS, 2 * NS), logtok.c(j * NS, (j + 1) * NS), V("v8", 7, 1),
                    ALU.is_ge, inc="v_fa")
                b.wait("scalar", "v_fa")
                act(V("v8", 0, 8), V("v8", 0, 8), AF.Exp, bias=V("m1n"), inc="a_fa")
                b.wait("vector", "a_fa")
                b.do("vector", lambda e: e.reduce_sum(V("Z"), V("v8", 0, 8), axis=AX.X),
                     inc="v_fa")
                bar("vector")
                b.do("vector", lambda e: e.reciprocal(V("Zr"), V("Z")), inc="v_fa")
                b.wait("scalar", "v_fa")
                act(tk_s.c(0, NS), logtok.c(j * NS, (j + 1) * NS), AF.Exp,
                    bias=V("m1n"), inc="a_fa")
                b.wait("vector", "a_fa")
                vstt(alpha.c(blk * NS, (blk + 1) * NS), tk_s.c(0, NS), V("Zr"),
                     ALU.mult, ALU.mult, tk_s.c(NS, 2 * NS), inc="v_fa")
                bar("vector")
                if tapd("logtok", [T, NS]) if False else ("logtok" in _DEBUG and cid == _TAPCALL):
                    tapd("logtok", [T, NS])
                    b.wait("sync", "f_psD")
                    dma_out(g["dbg_logtok"][blk * 128:(blk + 1) * 128, :],
                            logtok.c(j * NS, (j + 1) * NS), sem="dbgw")
                if "xn" in _DEBUG and cid == _TAPCALL:
                    tapd("xn", [T, D])
                    b.wait("sync", "a_ln")
                    dma_out(g["dbg_xn"][blk * 128:(blk + 1) * 128, :],
                            xw.c(j * XW, j * XW + 1024), sem="dbgw")
                if "alpha" in _DEBUG and cid == _TAPCALL:
                    tapd("alpha", [T, NS])
                    b.wait("sync", "v_fa")
                    dma_out(g["dbg_alpha"][blk * 128:(blk + 1) * 128, :],
                            alpha.c(blk * NS, (blk + 1) * NS), sem="dbgw")
                b.wait("tensor", "v_fa")
                b.wait("tensor", "f_psE")
                transpose(psE[0:64, 0:128], alpha.c(blk * NS, (blk + 1) * NS), 128,
                          inc="p_psE")
                b.wait("scalar", "p_psE")
                act(alphaT.c(blk * 128, (blk + 1) * 128, parts=64), psE[0:64, 0:128],
                    AF.Copy, inc="f_psE")
                mm(ps_slot[0:64, 0:512], alpha.c(blk * NS, (blk + 1) * NS),
                   xw.c(j * XW, j * XW + 512), blk == 0, blk == NBLK - 1)
                mm(ps_slot[0:64, 512:1024], alpha.c(blk * NS, (blk + 1) * NS),
                   xw.c(j * XW + 512, j * XW + 1024), blk == 0, blk == NBLK - 1)
                mm(ps_slot[0:64, 1024:1025], alpha.c(blk * NS, (blk + 1) * NS),
                   xw.c(j * XW + 1024, j * XW + 1025), blk == 0, blk == NBLK - 1,
                   inc="mm_slot")

        b.wait("sync", "mm_slot")
        b.wait("vector", "mm_slot")
        b.wait("scalar", "mm_slot")
        ar.pop()
        if "fb" not in _FSUB:
            ar.pop()
            return
        b.fence()
        ar.push()
        slot_in = ar.alloc(1040, parts=64)
        b.wait("scalar", "mm_slot")
        act(slot_in.c(0, 1025, parts=64), ps_slot[0:64, 0:1025], AF.Copy, inc="f_slot")
        b.wait("vector", "f_slot")
        vts(slot_in.c(1024, 1025, parts=64), slot_in.c(1024, 1025, parts=64), 1e-8,
            ALU.add, inc="v_fb")
        bar("vector")
        b.do("vector", lambda e: e.reciprocal(V("denr", parts=64),
                                              slot_in.c(1024, 1025, parts=64)),
             inc="v_fb")
        bar("vector")
        vts(slot_in.c(0, 1024, parts=64), slot_in.c(0, 1024, parts=64),
            V("denr", parts=64), ALU.mult, inc="v_fb")
        bar("vector")
        if tapd("si", [NS, 1025]):
            b.wait("sync", "v_fb")
            dma_out(g["dbg_si"], slot_in.c(0, 1025, parts=64), sem="dbgw")
        siT = ar.alloc(8 * NS)
        siniT = ar.alloc(8 * NS)
        b.wait("tensor", "v_fb")
        for (src, dstT) in ([(slot_in, siT), (si_sb, siniT)] if _FBL >= 1 else []):
            for j in range(8):
                b.wait("tensor", "f_psD")
                transpose(psD[0:128, 0:64], src.c(j * 128, (j + 1) * 128, parts=64), 64,
                          inc="p_psD")
                b.wait("scalar", "p_psD")
                act(dstT.c(j * NS, (j + 1) * NS), psD[0:128, 0:64], AF.Copy, inc="f_psD")
        load_cols(g["bih"], 3 * D, lambda t: V("bihc", t), scr)
        load_cols(g["bhh"], 3 * D, lambda t: V("bhhc", t), scr)
        ghT = ar.alloc(24 * NS)
        giT = ar.alloc(24 * NS)
        ar.push()
        wscr = ar.alloc(D)
        HW2 = 12 * 128
        WT = ar.alloc(8 * HW2)
        for (wname, dstT, bcol, srcT) in ([("Whh", ghT, "bhhc", siniT),
                                          ("Wih", giT, "bihc", siT)] if _FBL >= 2 else []):
            for hf in range(2):
                for rt2 in range(12):
                    rt = hf * 12 + rt2
                    b.wait("sync", "p_psD")
                    b.wait("sync", "f_psA")
                    b.wait("sync", "f_psB")
                    dma_in(g[wname][rt * 128:(rt + 1) * 128, :], wscr.ap())
                    wait_dma("tensor")
                    for dt_i in range(8):
                        b.wait("tensor", "f_psD")
                        transpose(psD[0:128, 0:128],
                                  wscr.c(dt_i * 128, (dt_i + 1) * 128),
                                  128, inc="p_psD")
                        b.wait("scalar", "p_psD")
                        act(WT.c(dt_i * HW2 + rt2 * 128, dt_i * HW2 + (rt2 + 1) * 128),
                            psD[0:128, 0:128], AF.Copy, inc="f_psD")
                b.wait("tensor", "f_psD")
                for jt2 in range(12):
                    jt = hf * 12 + jt2
                    ps, pss = (psA, "A") if jt % 2 == 0 else (psB, "B")
                    b.wait("tensor", "f_ps" + pss)
                    for kt in range(8):
                        mm(ps[0:128, 0:NS],
                           WT.c(kt * HW2 + jt2 * 128, kt * HW2 + (jt2 + 1) * 128),
                           srcT.c(kt * NS, (kt + 1) * NS), kt == 0, kt == 7,
                           inc="p_ps" + pss if kt == 7 else None)
                    b.wait("scalar", "p_ps" + pss)
                    act(dstT.c(jt * NS, (jt + 1) * NS), ps[0:128, 0:NS], AF.Copy,
                        bias=V(bcol, jt), inc="f_ps" + pss)
                b.wait("tensor", "f_psA")
                b.wait("tensor", "f_psB")
        ar.pop()
        SnT = ar.alloc(8 * NS)
        rb = ar.alloc(NS)
        b.wait("vector", "f_psA")
        b.wait("vector", "f_psB")
        for t8 in (range(8) if _FBL >= 3 else []):
            gi_r = lambda t8=t8: giT.c(t8 * NS, (t8 + 1) * NS)
            gh_r = lambda t8=t8: ghT.c(t8 * NS, (t8 + 1) * NS)
            gi_z = lambda t8=t8: giT.c((8 + t8) * NS, (9 + t8) * NS)
            gh_z = lambda t8=t8: ghT.c((8 + t8) * NS, (9 + t8) * NS)
            gi_n = lambda t8=t8: giT.c((16 + t8) * NS, (17 + t8) * NS)
            gh_n = lambda t8=t8: ghT.c((16 + t8) * NS, (17 + t8) * NS)
            b.wait("vector", "a_fb")
            vtt(rb.ap(), gi_r(), gh_r(), ALU.add, inc="v_fb")
            b.wait("scalar", "v_fb")
            act(rb.ap(), rb.ap(), AF.Sigmoid, inc="a_fb")
            b.wait("vector", "a_fb")
            vtt(gh_n(), rb.ap(), gh_n(), ALU.mult, inc="v_fb")
            bar("vector")
            vtt(gi_n(), gi_n(), gh_n(), ALU.add, inc="v_fb")
            b.wait("scalar", "v_fb")
            act(gi_n(), gi_n(), AF.Tanh, inc="a_fb")
            vtt(gi_z(), gi_z(), gh_z(), ALU.add, inc="v_fb")
            b.wait("scalar", "v_fb")
            act(gi_z(), gi_z(), AF.Sigmoid, inc="a_fb")
            b.wait("vector", "a_fb")
            vtt(rb.ap(), siniT.c(t8 * NS, (t8 + 1) * NS), gi_n(), ALU.subtract,
                inc="v_fb")
            bar("vector")
            vtt(rb.ap(), rb.ap(), gi_z(), ALU.mult, inc="v_fb")
            bar("vector")
            vtt(SnT.c(t8 * NS, (t8 + 1) * NS), gi_n(), rb.ap(), ALU.add, inc="v_fb")
        bar("vector")
        if tapd("SnT", [128, 8 * NS]):
            b.wait("sync", "v_fb")
            dma_out(g["dbg_SnT"], SnT.ap(), sem="dbgw")
        load_cols(g["bq"], DQ, lambda t: V("bqc", t), scr)
        load_cols(g["bk"], DQ, lambda t: V("bkc", t), scr)
        qT = ar.alloc(2 * NS)
        kT = ar.alloc(2 * NS)
        ar.push()
        wqt = ar.alloc(8 * DQ)
        wkt = ar.alloc(8 * DQ)
        load_tiles(g["wq"], D, DQ, wqt)
        load_tiles(g["wk"], D, DQ, wkt)
        wait_dma("tensor")
        b.wait("tensor", "v_fb")
        for (wt_, dT, bc) in ([(wqt, qT, "bqc"), (wkt, kT, "bkc")] if _FBL >= 4 else []):
            for qt in range(2):
                b.wait("tensor", "f_psA")
                for kt in range(8):
                    mm(psA[0:128, 0:NS],
                       wt_.c(kt * DQ + qt * 128, kt * DQ + (qt + 1) * 128),
                       SnT.c(kt * NS, (kt + 1) * NS), kt == 0, kt == 7,
                       inc="p_psA" if kt == 7 else None)
                b.wait("scalar", "p_psA")
                act(dT.c(qt * NS, (qt + 1) * NS), psA[0:128, 0:NS], AF.Copy,
                    bias=V(bc, qt), inc="f_psA")
        b.wait("sync", "p_psA")
        ar.pop()
        attn = ar.alloc(NS, parts=64)
        b.wait("tensor", "f_psA")
        b.wait("tensor", "f_psB")
        for kt in (range(2) if _FBL >= 5 else []):
            mm(psB[0:64, 0:NS], qT.c(kt * NS, (kt + 1) * NS),
               kT.c(kt * NS, (kt + 1) * NS), kt == 0, kt == 1,
               inc="p_psB" if kt == 1 else None)
        b.wait("scalar", "p_psB")
        if _FBL >= 5:
            act(attn.ap(), psB[0:64, 0:NS], AF.Copy, scale=DQ ** (-0.5), inc="f_psB")
        b.wait("vector", "f_psB")
        if _FBL < 6:
            b.wait("sync", "v_fb"); b.wait("sync", "a_fb")
            ar.pop(); ar.pop(); return
        b.do("vector", lambda e: e.reduce_max(V("att1", parts=64), attn.ap(), axis=AX.X),
             inc="v_fb")
        bar("vector")
        vts(V("att2", parts=64), V("att1", parts=64), -1.0, ALU.mult, inc="v_fb")
        b.wait("scalar", "v_fb")
        act(attn.ap(), attn.ap(), AF.Exp, bias=V("att2", parts=64), inc="a_fb")
        b.wait("vector", "a_fb")
        b.do("vector", lambda e: e.reduce_sum(V("att3", parts=64), attn.ap(), axis=AX.X),
             inc="v_fb")
        bar("vector")
        b.do("vector", lambda e: e.reciprocal(V("att1", parts=64), V("att3", parts=64)),
             inc="v_fb")
        bar("vector")
        vts(attn.ap(), attn.ap(), V("att1", parts=64), ALU.mult, inc="v_fb")
        bar("vector")
        attnT = ar.alloc(NS, parts=64)
        b.wait("tensor", "v_fb")
        b.wait("tensor", "f_psE")
        transpose(psE[0:64, 0:64], attn.ap(), 64, inc="p_psE")
        b.wait("scalar", "p_psE")
        act(attnT.ap(), psE[0:64, 0:64], AF.Copy, inc="f_psE")
        if _FBL < 7:
            b.wait("sync", "v_fb"); b.wait("sync", "a_fb")
            ar.pop(); ar.pop(); return
        attnT_ = None
        vsb = ar.alloc(D, parts=64)
        Snsm = ar.alloc(D, parts=64)
        ysb = ar.alloc(D, parts=64)
        bvrep = ar.alloc(D, parts=64)
        ar.push()
        wvt = ar.alloc(8 * D)
        load_tiles(g["wv"], D, D, wvt)
        wait_dma("tensor")
        for h in range(2):
            b.wait("tensor", "f_psA")
            for kt in range(8):
                mm(psA[0:64, 0:512], SnT.c(kt * NS, (kt + 1) * NS),
                   wvt.c(kt * D + h * 512, kt * D + (h + 1) * 512), kt == 0, kt == 7,
                   inc="p_psA" if kt == 7 else None)
            b.wait("scalar", "p_psA")
            act(vsb.c(h * 512, (h + 1) * 512, parts=64), psA[0:64, 0:512], AF.Copy,
                inc="f_psA")
        b.wait("tensor", "f_psA")
        for j in range(8):
            b.wait("tensor", "f_psD")
            transpose(psD[0:64, 0:128], SnT.c(j * NS, (j + 1) * NS), 128, inc="p_psD")
            b.wait("scalar", "p_psD")
            act(Snsm.c(j * 128, (j + 1) * 128, parts=64), psD[0:64, 0:128], AF.Copy,
                inc="f_psD")
        dma_in(g["bv"].unsqueeze(0).broadcast_to([64, D]), bvrep.ap())
        b.wait("vector", "f_psD")
        b.wait("tensor", "f_psE")
        for h in range(2):
            ps, pss = (psA, "A") if h == 0 else (psB, "B")
            b.wait("tensor", "f_ps" + pss)
            mm(ps[0:64, 0:512], attnT.ap(), vsb.c(h * 512, (h + 1) * 512, parts=64),
               True, True, inc="p_ps" + pss)
            b.wait("vector", "p_ps" + pss)
            vtt(ysb.c(h * 512, (h + 1) * 512, parts=64),
                Snsm.c(h * 512, (h + 1) * 512, parts=64), ps[0:64, 0:512], ALU.add,
                inc="f_ps" + pss)
        wait_dma("vector")
        bar("vector")
        vtt(ysb.ap(), ysb.ap(), bvrep.ap(), ALU.add, inc="v_fb")
        bar("vector")
        b.wait("sync", "v_fb")
        ar.pop()
        if _FBL < 8:
            b.wait("sync", "a_fb"); ar.pop(); ar.pop(); return
        lngr = ar.alloc(D, parts=64)
        lnbr = ar.alloc(D, parts=64)
        dma_in(g["ln_g"].unsqueeze(0).broadcast_to([64, D]), lngr.ap())
        dma_in(g["ln_b"].unsqueeze(0).broadcast_to([64, D]), lnbr.ap())
        b.do("vector", lambda e: e.reduce_sum(V("sc1", 0, 1, parts=64), ysb.ap(),
                                              axis=AX.X), inc="v_fb")
        bar("vector")
        vts(V("sc1", 1, 1, parts=64), V("sc1", 0, 1, parts=64), -1.0 / D, ALU.mult,
            inc="v_fb")
        b.wait("scalar", "v_fb")
        act(ysb.ap(), ysb.ap(), AF.Copy, bias=V("sc1", 1, 1, parts=64), inc="a_fb")
        act(sq_s.c(0, D, parts=64), ysb.ap(), AF.Square,
            accum=V("sc2", 0, 1, parts=64), inc="a_fb")
        bar("scalar")
        act(V("sc2", 1, 1, parts=64), V("sc2", 0, 1, parts=64), AF.Ln, scale=1.0 / D,
            bias=1e-5, inc="a_fb")
        bar("scalar")
        act(V("sc3", 0, 1, parts=64), V("sc2", 1, 1, parts=64), AF.Exp, scale=-0.5,
            inc="a_fb")
        bar("scalar")
        act(ysb.ap(), ysb.ap(), AF.Copy, scale=V("sc3", 0, 1, parts=64), inc="a_fb")
        wait_dma("vector")
        b.wait("vector", "a_fb")
        vtt(ysb.ap(), ysb.ap(), lngr.ap(), ALU.mult, inc="v_fb")
        bar("vector")
        vtt(ysb.ap(), ysb.ap(), lnbr.ap(), ALU.add, inc="v_fb")
        bar("vector")
        if _FBL < 9:
            b.wait("sync", "v_fb"); b.wait("sync", "a_fb")
            ar.pop(); ar.pop(); return
        yT = ar.alloc(8 * NS)
        svT = ar.alloc(8 * NS)
        ar.push()
        b.wait("tensor", "v_fb")
        for j in range(8):
            b.wait("tensor", "f_psD")
            transpose(psD[0:128, 0:64], ysb.c(j * 128, (j + 1) * 128, parts=64), 64,
                      inc="p_psD")
            b.wait("scalar", "p_psD")
            act(yT.c(j * NS, (j + 1) * NS), psD[0:128, 0:64], AF.Copy, inc="f_psD")
        wvpt = ar.alloc(8 * D)
        svsb = ar.alloc(D, parts=64)
        load_tiles(g["wvp"], D, D, wvpt)
        wait_dma("tensor")
        b.wait("tensor", "f_psD")
        for h in range(2):
            ps, pss = (psA, "A") if h == 0 else (psB, "B")
            b.wait("tensor", "f_ps" + pss)
            for kt in range(8):
                mm(ps[0:64, 0:512], yT.c(kt * NS, (kt + 1) * NS),
                   wvpt.c(kt * D + h * 512, kt * D + (h + 1) * 512), kt == 0, kt == 7,
                   inc="p_ps" + pss if kt == 7 else None)
            b.wait("scalar", "p_ps" + pss)
            act(svsb.c(h * 512, (h + 1) * 512, parts=64), ps[0:64, 0:512], AF.Copy,
                inc="f_ps" + pss)
        b.wait("tensor", "f_psA")
        b.wait("tensor", "f_psB")
        for j in range(8):
            b.wait("tensor", "f_psD")
            transpose(psD[0:128, 0:64], svsb.c(j * 128, (j + 1) * 128, parts=64), 64,
                      inc="p_psD")
            b.wait("scalar", "p_psD")
            act(svT.c(j * NS, (j + 1) * NS), psD[0:128, 0:64], AF.Copy, inc="f_psD")
        b.wait("sync", "f_psD")
        ar.pop()
        ar.push()
        wot = ar.alloc(8 * D)
        load_tiles(g["wo"], D, D, wot)
        bvpc = ar.alloc(8)
        load_cols(g["bvp"], D, lambda t: bvpc.c(t, t + 1), scr)
        c2row = ar.alloc(D, parts=1)
        bor = ar.alloc(D, parts=1)
        dma_in(g["bo"].unsqueeze(0), bor.ap())
        wait_dma("tensor")
        b.wait("tensor", "f_psD")
        b.wait("tensor", "f_psE")
        for h in range(2):
            b.wait("tensor", "f_psE")
            for kt in range(8):
                mm(psE[0:1, 0:512], bvpc.c(kt, kt + 1),
                   wot.c(kt * D + h * 512, kt * D + (h + 1) * 512), kt == 0, kt == 7,
                   inc="p_psE" if kt == 7 else None)
            b.wait("vector", "p_psE")
            b.wait("vector", SEM_DMA)
            vtt(c2row.c(h * 512, (h + 1) * 512, parts=1), psE[0:1, 0:512],
                bor.c(h * 512, (h + 1) * 512, parts=1), ALU.add, inc="f_psE")
        bar("vector")
        c2rep = ar.alloc(D, parts=64)
        for h in range(2):
            ps, pss = (psA, "A") if h == 0 else (psB, "B")
            b.wait("tensor", "f_ps" + pss)
            for kt in range(8):
                mm(ps[0:64, 0:512], svT.c(kt * NS, (kt + 1) * NS),
                   wot.c(kt * D + h * 512, kt * D + (h + 1) * 512), kt == 0, kt == 7,
                   inc="p_ps" + pss if kt == 7 else None)
            b.wait("tensor", "f_psE")
            b.wait("tensor", "v_fb")
            bcast(psE[0:64, 0:512], c2row.c(h * 512, (h + 1) * 512, parts=1), 64,
                  inc="p_psE")
            b.wait("scalar", "p_psE")
            act(c2rep.c(h * 512, (h + 1) * 512, parts=64), psE[0:64, 0:512], AF.Copy,
                inc="f_psE")
            b.wait("vector", "p_ps" + pss)
            b.wait("vector", "f_psE")
            vtt(svwo2.c(h * 512, (h + 1) * 512, parts=64), ps[0:64, 0:512],
                c2rep.c(h * 512, (h + 1) * 512, parts=64), ALU.add, inc="f_ps" + pss)
        bar("vector")
        if tapd("svwo2", [NS, D]):
            b.wait("sync", "f_psA")
            b.wait("sync", "f_psB")
            dma_out(g["dbg_svwo2"], svwo2.ap(), sem="dbgw")
        b.wait("sync", "f_psA")
        b.wait("sync", "f_psB")
        ar.pop()

        if "fc" not in _FSUB:
            ar.pop(); ar.pop()
            return
        b.fence()
        sin_c = ar.alloc(4 * D)
        sout = ar.alloc(4 * D)
        for c in range(NCH):
            b.wait("sync", "f_psA")
            b.wait("sync", "f_psB")
            b.wait("sync", out_sem)
            for j in range(4):
                blk = c * 4 + j
                dma_in(in_hbm[blk * 128:(blk + 1) * 128, :], sin_c.c(j * D, (j + 1) * D))
            wait_dma("vector")
            for j in range(4):
                blk = c * 4 + j
                for h in range(2):
                    ps, pss = (psA, "A") if h == 0 else (psB, "B")
                    b.wait("tensor", "f_ps" + pss)
                    mm(ps[0:128, 0:512], alphaT.c(blk * 128, (blk + 1) * 128, parts=64),
                       svwo2.c(h * 512, (h + 1) * 512, parts=64), True, True,
                       inc="p_ps" + pss)
                    b.wait("vector", "p_ps" + pss)
                    vstt(sout.c(j * D + h * 512, j * D + (h + 1) * 512),
                         ps[0:128, 0:512], 0.5, ALU.mult, ALU.add,
                         sin_c.c(j * D + h * 512, j * D + (h + 1) * 512),
                         inc="f_ps" + pss)
            b.wait("sync", "f_psA")
            b.wait("sync", "f_psB")
            for j in range(4):
                blk = c * 4 + j
                dma_out(out_hbm[blk * 128:(blk + 1) * 128, :],
                        sout.c(j * D, (j + 1) * D), sem=out_sem)
        ar.pop()
        ar.pop()

    m_static = {
        "xdbl": ar.alloc(T, parts=96),
        "dtT": ar.alloc(E, parts=64),
        "xpT": ar.alloc(16 * 96),
    }

    def m_pre():
        ar.push()
        wscr = ar.alloc(2 * E)
        scr = ar.alloc(3 * D, parts=1)
        tbuf = ar.alloc(128)
        for rt in range(32):
            b.wait("sync", "p_psD")
            dma_in(g["in_proj_w"][rt * 128:(rt + 1) * 128, :], wscr.c(0, D))
            wait_dma("tensor")
            for dt_i in range(8):
                b.wait("tensor", "f_psD")
                transpose(psD[0:128, 0:128], wscr.c(dt_i * 128, (dt_i + 1) * 128), 128,
                          inc="p_psD")
                b.wait("scalar", "p_psD")
                b.wait("scalar", "mpw")
                act(tbuf.ap(), psD[0:128, 0:128], AF.Copy, inc="f_psD")
                b.wait("sync", "f_psD")
                dma_out(g["ipT_h"][dt_i, :, rt * 128:(rt + 1) * 128], tbuf.ap(),
                        sem="mpw")
        for mt in range(8):
            b.wait("sync", "p_psD")
            dma_in(g["out_proj_w"][mt * 128:(mt + 1) * 128, :], wscr.c(0, E))
            wait_dma("tensor")
            for kt in range(16):
                b.wait("tensor", "f_psD")
                transpose(psD[0:128, 0:128], wscr.c(kt * 128, (kt + 1) * 128), 128,
                          inc="p_psD")
                b.wait("scalar", "p_psD")
                b.wait("scalar", "mpw")
                act(tbuf.ap(), psD[0:128, 0:128], AF.Copy, inc="f_psD")
                b.wait("sync", "f_psD")
                dma_out(g["opT_h"][mt, :, kt * 128:(kt + 1) * 128], tbuf.ap(),
                        sem="mpw")
        dma_in(g["conv_w"].rearrange("(et p) k -> p et k", p=128),
               V("convw", 0, 64).rearrange("p (et k) -> p et k", k=4))
        dma_in(g["A_log"].rearrange("(et p) n -> p et n", p=128),
               V("Acols", 0, 256).rearrange("p (et n) -> p et n", n=N))
        load_cols(g["conv_b"], E, lambda t: V("convb", t), scr)
        load_cols(g["Dvec"], E, lambda t: V("Dc", t), scr)
        load_cols(g["dt_proj_b"], E, lambda t: V("dtb", t), scr)
        wait_dma("scalar")
        act(V("Acols", 0, 256), V("Acols", 0, 256), AF.Exp, inc="a_mp")
        b.wait("vector", "a_mp")
        vts(V("Acols", 0, 256), V("Acols", 0, 256), -1.0, ALU.mult, inc="v_mp")
        bar("vector")
        for rt in range(16):
            b.wait("sync", "p_psD")
            dma_in(g["dt_proj_w"][rt * 128:(rt + 1) * 128, :], wscr.c(0, RK))
            wait_dma("tensor")
            b.wait("tensor", "f_psD")
            transpose(psD[0:64, 0:128], wscr.c(0, RK), 128, inc="p_psD")
            b.wait("scalar", "p_psD")
            act(m_static["dtT"].c(rt * 128, (rt + 1) * 128, parts=64),
                psD[0:64, 0:128], AF.Copy, inc="f_psD")
        b.wait("sync", "p_psD")
        dma_in(g["x_proj_w"], wscr.c(0, E, parts=96))
        wait_dma("tensor")
        for et in range(16):
            b.wait("tensor", "f_psD")
            transpose(psD[0:128, 0:96], wscr.c(et * 128, (et + 1) * 128, parts=96), 96,
                      inc="p_psD")
            b.wait("scalar", "p_psD")
            act(m_static["xpT"].c(et * 96, (et + 1) * 96), psD[0:128, 0:96], AF.Copy,
                inc="f_psD")
        b.wait("vector", "f_psD")
        b.wait("sync", "mpw")
        ar.pop()

    def m1():
        b.fence()
        ar.push()
        sin1 = ar.alloc(D)
        xn1 = ar.alloc(D)
        sq1 = ar.alloc(D)
        xnT = ar.alloc(8 * TC)
        wsl = ar.alloc(2 * 1024)
        CB = TC + 16
        cvb = ar.alloc(2 * CB)
        xce = ar.alloc(TC)
        tails = ar.alloc(64)
        b.do("vector", lambda e: e.memset(tails.ap(), 0.0), inc="v_m1")
        bar("vector")
        b.wait("sync", "v_m1")

        for c in range(NCH):
            for j in range(4):
                blk = c * 4 + j
                b.wait("sync", "f_psD")
                dma_in(g["s1"][blk * 128:(blk + 1) * 128, :], sin1.ap())
                wait_dma("vector")
                wait_dma("scalar")
                layernorm_block(lambda: sin1.ap(), lambda: xn1.ap(), sq1)
                b.wait("tensor", "a_ln")
                b.wait("tensor", "f_psD")
                for dt_i in range(8):
                    transpose(psD[0:128, (dt_i % 4) * 128:(dt_i % 4 + 1) * 128],
                              xn1.c(dt_i * 128, (dt_i + 1) * 128), 128,
                              inc="p_psD" if dt_i % 4 == 3 else None)
                    if dt_i % 4 == 3:
                        b.wait("scalar", "p_psD")
                        for q2 in range(4):
                            di = dt_i - 3 + q2
                            act(xnT.c(di * TC + j * 128, di * TC + (j + 1) * 128),
                                psD[0:128, q2 * 128:(q2 + 1) * 128], AF.Copy,
                                inc="f_psD" if q2 == 3 else None)
                        if dt_i == 3:
                            b.wait("tensor", "f_psD")
            b.wait("tensor", "f_psD")
            for et in range(32):
                wb = wsl.sub((et % 2) * 1024, (et % 2) * 1024 + 1024)
                b.wait("sync", "f_psA")
                b.wait("sync", "f_psB")
                for kt in range(8):
                    dma_in(g["ipT_h"][kt, :, et * 128:(et + 1) * 128],
                           wb.c(kt * 128, (kt + 1) * 128))
                wait_dma("tensor")
                ps, pss = (psA, "A") if et % 2 == 0 else (psB, "B")
                b.wait("tensor", "f_ps" + pss)
                for kt in range(8):
                    mm(ps[0:128, 0:512], wb.c(kt * 128, (kt + 1) * 128),
                       xnT.c(kt * TC, (kt + 1) * TC), kt == 0, kt == 7,
                       inc="p_ps" + pss if kt == 7 else None)
                b.wait("scalar", "p_ps" + pss)
                if et < 16:
                    cb = cvb.sub((et % 2) * CB, (et % 2) * CB + CB)
                    b.wait("scalar", "v_cv")
                    act(cb.c(3, 3 + TC), ps[0:128, 0:512], AF.Copy, inc="f_ps" + pss)
                    b.wait("vector", "f_ps" + pss)
                    b.wait("vector", "m1w")
                    b.wait("vector", "mm_xd")
                    vcopy(cb.c(0, 3), tails.c(et * 3, (et + 1) * 3), inc="v_cv")
                    bar("vector")
                    vts(xce.ap(), cb.c(0, TC), V("convw", et * 4 + 0), ALU.mult,
                        inc="v_cv")
                    bar("vector")
                    for k in range(1, 4):
                        vstt(xce.ap(), cb.c(k, k + TC), V("convw", et * 4 + k),
                             ALU.mult, ALU.add, xce.ap(), inc="v_cv")
                        bar("vector")
                    vcopy(tails.c(et * 3, (et + 1) * 3), cb.c(TC, TC + 3), inc="v_cv")
                    b.wait("scalar", "v_cv")
                    act(xce.ap(), xce.ap(), AF.Silu, bias=V("convb", et), inc="a_m1")
                    b.wait("sync", "a_m1")
                    dma_out(g["xc_h"][et * 128:(et + 1) * 128, c * TC:(c + 1) * TC],
                            xce.ap(), sem="m1w")
                    b.wait("tensor", "a_m1")
                    if et == 0 and c > 0:
                        b.wait("tensor", "f_psC")
                    mm(psC[0:96, 0:512], m_static["xpT"].c(et * 96, (et + 1) * 96),
                       xce.ap(), et == 0, et == 15, inc="mm_xd")
                else:
                    b.wait("scalar", "m1w")
                    b.wait("scalar", "mm_xd")
                    act(xce.ap(), ps[0:128, 0:512], AF.Silu, inc="f_ps" + pss)
                    b.wait("sync", "f_ps" + pss)
                    dma_out(g["zs_h"][(et - 16) * 128:(et - 15) * 128,
                                      c * TC:(c + 1) * TC], xce.ap(), sem="m1w")
            b.wait("scalar", "mm_xd")
            act(m_static["xdbl"].c(c * TC, (c + 1) * TC, parts=96), psC[0:96, 0:512],
                AF.Copy, inc="f_psC")
            b.wait("sync", "f_psC")
            dma_out(g["xdbl_h"][:, c * TC:(c + 1) * TC],
                    m_static["xdbl"].c(c * TC, (c + 1) * TC, parts=96), sem="m1w")
        b.wait("sync", "m1w")
        ar.pop()

    def m2():
        b.fence()
        ar.push()
        Brep = ar.alloc(2 * TC)
        Crep = ar.alloc(2 * TC)
        s1b = ar.alloc(4 * D)
        yall = ar.alloc(16 * TC)
        xcq = ar.alloc(ETQ * TC)
        zsq = ar.alloc(ETQ * TC)
        dlq = ar.alloc(ETQ * TC)
        dxq = ar.alloc(ETQ * TC)
        abuf = ar.alloc(2 * ETQ * TC)
        ubuf = ar.alloc(2 * ETQ * TC)
        opb = ar.alloc(16 * 128)
        mo = ar.alloc(TC)
        hs_scr = ar.alloc(2 * ETQ)
        QT = ETQ * TC

        def a3(tile, p):
            return tile.c(p * QT, (p + 1) * QT).rearrange("p (e t) -> p e t", e=ETQ)

        def a2(tile, p):
            return tile.c(p * QT, (p + 1) * QT)

        prev_hc = [0, 0]
        prev_acc = [0, 0]
        for c in range(NCH):
            b.wait("sync", "m2w")
            b.wait("sync", "dbgw")
            for j in range(4):
                blk = c * 4 + j
                dma_in(g["s1"][blk * 128:(blk + 1) * 128, :], s1b.c(j * D, (j + 1) * D))
            b.wait("gpsimd", "f_psA")
            b.wait("gpsimd", "f_psB")
            for q in range(NEQ):
                b.wait("scalar", "v_m2")
                for i in range(ETQ):
                    et = q * ETQ + i
                    ps, pss = (psA, "A") if i % 2 == 0 else (psB, "B")
                    b.wait("tensor", "f_ps" + pss)
                    mm(ps[0:128, 0:512],
                       m_static["dtT"].c(et * 128, (et + 1) * 128, parts=64),
                       m_static["xdbl"].c(c * TC, (c + 1) * TC, parts=64),
                       True, True, inc="p_ps" + pss)
                    b.wait("scalar", "p_ps" + pss)
                    act(dlq.c(i * TC, (i + 1) * TC), ps[0:128, 0:512], AF.Exp,
                        bias=V("dtb", et), inc="f_ps" + pss)
                act(dlq.ap(), dlq.ap(), AF.Ln, bias=1.0, inc="a_m2")
                b.wait("sync", "v_m2")
                b.wait("sync", "g_m2")
                for i in range(ETQ):
                    et = q * ETQ + i
                    dma_in(g["xc_h"][et * 128:(et + 1) * 128, c * TC:(c + 1) * TC],
                           xcq.c(i * TC, (i + 1) * TC))
                    dma_in(g["zs_h"][et * 128:(et + 1) * 128, c * TC:(c + 1) * TC],
                           zsq.c(i * TC, (i + 1) * TC))
                wait_dma("vector")
                b.wait("vector", "a_m2")
                vtt(dxq.ap(), dlq.ap(), xcq.ap(), ALU.mult, inc="v_m2")
                bar("vector")

                b.sem("a_m2"); b.sem("v_m2"); b.sem("g_m2"); b.sem(SEM_DMA)
                exp_cnt = [0] * N
                hc_cnt = [0] * N
                acc_cnt = [0] * N
                bc_cnt = [0] * N

                def bc_load(n):
                    p = n % 2
                    if n >= 2:
                        b.wait_val("sync", "v_m2", hc_cnt[n - 2])
                    else:
                        b.wait_val("sync", "v_m2", prev_hc[p])
                    dma_in(g["xdbl_h"][RK + n:RK + n + 1, c * TC:(c + 1) * TC]
                           .broadcast_to([128, TC]), Brep.c(p * TC, (p + 1) * TC))
                    dma_in(g["xdbl_h"][RK + N + n:RK + N + n + 1, c * TC:(c + 1) * TC]
                           .broadcast_to([128, TC]), Crep.c(p * TC, (p + 1) * TC))
                    bc_cnt[n] = b.cnt[SEM_DMA]

                bc_load(0)
                bc_load(1)
                for n in range(N):
                    p = n % 2
                    if n >= 2:
                        b.wait_val("scalar", "g_m2", acc_cnt[n - 2])
                    else:
                        b.wait_val("scalar", "g_m2", prev_acc[p])
                    for i in range(ETQ):
                        et = q * ETQ + i
                        act(a2(abuf, p).tensor[0:128,
                            abuf.off + p * QT + i * TC:abuf.off + p * QT + (i + 1) * TC]
                            if False else
                            abuf.c(p * QT + i * TC, p * QT + (i + 1) * TC),
                            dlq.c(i * TC, (i + 1) * TC),
                            AF.Exp, scale=V("Acols", et * N + n), inc="a_m2")
                    exp_cnt[n] = b.cnt["a_m2"]
                    b.wait_val("vector", SEM_DMA, bc_cnt[n])
                    vtt(a3(ubuf, p), dxq.ap().rearrange("p (e t) -> p e t", e=ETQ),
                        Brep.c(p * TC, (p + 1) * TC).unsqueeze(1)
                        .broadcast_to([128, ETQ, TC]), ALU.mult, inc="v_m2")
                    b.wait_val("vector", "a_m2", exp_cnt[n])
                    bar("vector")
                    vtt(hs_scr.c(p * ETQ, (p + 1) * ETQ),
                        a3(abuf, p)[:, :, 0:1].squeeze(2),
                        h_state.ap().rearrange("p (n e) -> p n e", n=N)
                        [:, n:n + 1, q * ETQ:(q + 1) * ETQ].squeeze(1),
                        ALU.mult, inc="v_m2")
                    bar("vector")
                    vtt(a3(ubuf, p)[:, :, 0:1].squeeze(2),
                        a3(ubuf, p)[:, :, 0:1].squeeze(2),
                        hs_scr.c(p * ETQ, (p + 1) * ETQ), ALU.add, inc="v_m2")
                    b.do("vector", lambda e, ap_=a3(abuf, p)[:, :, 0:1].squeeze(2):
                         e.memset(ap_, 0.0), inc="v_m2")
                    bar("vector")
                    b.do("vector", lambda e, o=a2(abuf, p), i0=a2(abuf, p),
                         i1=a2(ubuf, p): e.tensor_tensor_scan(o, i0, i1, 0.0,
                                                              ALU.mult, ALU.add),
                         inc="v_m2")
                    bar("vector")
                    b.do("vector", lambda e, n=n, q=q, src=a3(abuf, p)[:, :, TC - 1:TC]
                         .squeeze(2): e.tensor_copy(
                        h_state.ap().rearrange("p (n e) -> p n e", n=N)
                        [:, n:n + 1, q * ETQ:(q + 1) * ETQ].squeeze(1), src),
                        inc="v_m2")
                    vtt(a3(abuf, p), a3(abuf, p),
                        Crep.c(p * TC, (p + 1) * TC).unsqueeze(1)
                        .broadcast_to([128, ETQ, TC]), ALU.mult, inc="v_m2")
                    hc_cnt[n] = b.cnt["v_m2"]
                    b.wait_val("gpsimd", "v_m2", hc_cnt[n])
                    if n == 0:
                        b.do("gpsimd", lambda e, src=a2(abuf, p),
                             dst=yall.c(q * QT, (q + 1) * QT):
                             e.tensor_copy(dst, src), inc="g_m2")
                    else:
                        b.do("gpsimd", lambda e, src=a2(abuf, p),
                             dst=yall.c(q * QT, (q + 1) * QT),
                             dst2=yall.c(q * QT, (q + 1) * QT):
                             e.tensor_tensor(dst, dst2, src, ALU.add), inc="g_m2")
                    acc_cnt[n] = b.cnt["g_m2"]
                    if n + 2 < N:
                        bc_load(n + 2)
                prev_hc = [hc_cnt[N - 2], hc_cnt[N - 1]]
                prev_acc = [acc_cnt[N - 2], acc_cnt[N - 1]]
                b.wait_val("vector", "g_m2", acc_cnt[N - 1])
                for i in range(ETQ):
                    et = q * ETQ + i
                    vstt(yall.c((q * ETQ + i) * TC, (q * ETQ + i + 1) * TC),
                         xcq.c(i * TC, (i + 1) * TC), V("Dc", et), ALU.mult, ALU.add,
                         yall.c((q * ETQ + i) * TC, (q * ETQ + i + 1) * TC), inc="v_m2")
                bar("vector")
                vtt(yall.c(q * QT, (q + 1) * QT), yall.c(q * QT, (q + 1) * QT),
                    zsq.ap(), ALU.mult, inc="v_m2")
                bar("vector")
            b.wait("tensor", "v_m2")
            for mt in range(8):
                b.wait("sync", "f_psA")
                b.wait("sync", "f_psB")
                dma_in(g["opT_h"][mt], opb.ap())
                wait_dma("tensor")
                ps, pss = (psA, "A") if mt % 2 == 0 else (psB, "B")
                b.wait("tensor", "f_ps" + pss)
                for kt in range(16):
                    mm(ps[0:128, 0:512], opb.c(kt * 128, (kt + 1) * 128),
                       yall.c(kt * TC, (kt + 1) * TC), kt == 0, kt == 15,
                       inc="p_ps" + pss if kt == 15 else None)
                b.wait("scalar", "p_ps" + pss)
                b.wait("scalar", "p_psD")
                act(mo.ap(), ps[0:128, 0:512], AF.Copy, inc="f_ps" + pss)
                b.wait("tensor", "f_ps" + pss)
                b.wait("tensor", "f_psD")
                for j in range(4):
                    transpose(psD[0:128, j * 128:(j + 1) * 128],
                              mo.c(j * 128, (j + 1) * 128), 128,
                              inc="p_psD" if j == 3 else None)
                b.wait("vector", "p_psD")
                for j in range(4):
                    vtt(s1b.c(j * D + mt * 128, j * D + (mt + 1) * 128),
                        s1b.c(j * D + mt * 128, j * D + (mt + 1) * 128),
                        psD[0:128, j * 128:(j + 1) * 128], ALU.add,
                        inc="f_psD" if j == 3 else None)
            b.wait("sync", "f_psD")
            for j in range(4):
                blk = c * 4 + j
                dma_out(g["s2"][blk * 128:(blk + 1) * 128, :],
                        s1b.c(j * D, (j + 1) * D), sem="m2w")
        b.wait("sync", "m2w")
        ar.pop()

    if "mpre" in _PHASES:
        m_pre()
    if "ffn1" in _PHASES:
        ffn_call(g["x_in"], g["s1"], "s1w", cid=0)
        b.wait("sync", "s1w")
    if "m1" in _PHASES:
        m1()
    if "m2" in _PHASES:
        m2()
    if "ffn2" in _PHASES:
        b.fence()
        ffn_call(g["s2"], g["out"], SEM_OUT, cid=1)
    else:
        b.do("sync", lambda e: e.dma_start(out=g["out"][0:128, :], in_=g["s1"][0:128, :]),
             inc=SEM_OUT, inc_n=16)

    for name, src in [("s1", g["s1"]), ("s2", g["s2"]), ("xdbl", g["xdbl_h"]),
                      ("xc", g["xc_h"]), ("zs", g["zs_h"])]:
        if name in dbg:
            b.do("sync", lambda e, d=dbg[name], s=src: e.dma_start(out=d, in_=s),
                 inc=SEM_OUT, inc_n=16)

    b.wait("sync", SEM_OUT)

    with nc.Block() as block:
        b.flush(block)
    return nc


_REN = {
    "L0_attn_in_proj_w": "in_proj_w", "L0_attn_conv_w": "conv_w",
    "L0_attn_conv_b": "conv_b", "L0_attn_x_proj_w": "x_proj_w",
    "L0_attn_dt_proj_w": "dt_proj_w", "L0_attn_dt_proj_b": "dt_proj_b",
    "L0_attn_A_log": "A_log", "L0_attn_D": "Dvec",
    "L0_attn_out_proj_w": "out_proj_w",
    "L0_ffn_slot_init": "slot_init", "L0_ffn_w1": "w1", "L0_ffn_b1": "b1",
    "L0_ffn_w2": "w2", "L0_ffn_b2": "b2", "L0_ffn_tau": "tau",
    "L0_ffn_Wih": "Wih", "L0_ffn_Whh": "Whh", "L0_ffn_bih": "bih",
    "L0_ffn_bhh": "bhh", "L0_ffn_wq": "wq", "L0_ffn_bq": "bq",
    "L0_ffn_wk": "wk", "L0_ffn_bk": "bk", "L0_ffn_wv": "wv",
    "L0_ffn_bv": "bv", "L0_ffn_ln_g": "ln_g", "L0_ffn_ln_b": "ln_b",
    "L0_ffn_wvp": "wvp", "L0_ffn_bvp": "bvp", "L0_ffn_wo": "wo",
    "L0_ffn_bo": "bo",
}


def kernel(**inputs):
    x = np.ascontiguousarray(np.asarray(inputs["L0_attn_hidden_states"],
                                        dtype=np.float32))
    weights = {v: np.ascontiguousarray(np.asarray(inputs[k], dtype=np.float32))
               for k, v in _REN.items()}
    nc = build_program()
    in_maps = []
    for c in range(8):
        m = {"x_in": x[c]}
        m.update(weights)
        in_maps.append(m)
    res = run_bass_kernel_spmd(nc, in_maps, core_ids=list(range(8)))
    return np.stack([res.results[c]["out"] for c in range(8)], axis=0)


# version 13
# speedup vs baseline: 1.2212x; 1.1730x over previous
import numpy as np
import concourse.bass as bass
import concourse.mybir as mybir
from concourse.bass_utils import run_bass_kernel_spmd

F32 = mybir.dt.float32
AF = mybir.ActivationFunctionType
ALU = mybir.AluOpType
AX = mybir.AxisListType

T = 4096
D = 1024
E = 2048
N = 16
RK = 64
NS = 64
DH = 512
DQ = 256
TC = 512
NCH = T // TC
NBLK = T // 128
EQ = 512
NEQ = E // EQ
ETQ = EQ // 128

ARENA_COLS = 49152

_DEBUG = []
_TAPCALL = 0
_PHASES = ["mpre", "ffn1", "m1", "m2", "ffn2"]
_FSUB = ["fpre", "fa", "fb", "fc"]
_FBL = 99


class Builder:
    def __init__(self, nc):
        self.nc = nc
        self.q = {"sync": [], "tensor": [], "scalar": [], "vector": [], "gpsimd": []}
        self.cnt = {}
        self.sems = {}
        self.last_inc = {}

    def sem(self, name):
        if name not in self.sems:
            self.sems[name] = self.nc.semaphore(name).__enter__()
            self.cnt[name] = 0
        return self.sems[name]

    def on(self, eng, fn):
        self.q[eng].append(fn)

    def do(self, eng, build, inc=None, inc_n=1):
        if inc is not None:
            self.sem(inc)
            self.cnt[inc] += inc_n
            s, n = self.sems[inc], inc_n

            def fn(e, build=build, s=s, n=n):
                build(e).then_inc(s, n)
            self.on(eng, fn)
            self.last_inc[eng] = inc
        else:
            self.on(eng, build)
            self.last_inc[eng] = None

    def wait(self, eng, sem):
        self.sem(sem)
        v = self.cnt[sem]
        if v == 0:
            return
        s = self.sems[sem]
        self.on(eng, lambda e, s=s, v=v: e.wait_ge(s, v))

    def wait_val(self, eng, sem, v):
        s = self.sem(sem)
        if v > 0:
            self.on(eng, lambda e, s=s, v=v: e.wait_ge(s, v))

    def fence(self):
        for eng in ["sync", "tensor", "scalar", "vector", "gpsimd"]:
            for sem in list(self.cnt):
                self.wait(eng, sem)

    def flush(self, block):
        for eng in ["sync", "tensor", "scalar", "vector", "gpsimd"]:
            fns = self.q[eng]
            if not fns:
                continue

            def runner(e, fns=fns):
                for f in fns:
                    f(e)
            getattr(block, eng)(runner)


class Arena:
    def __init__(self, handle):
        self.h = handle
        self.base = 0
        self.marks = []
        self.peak = 0

    def push(self):
        self.marks.append(self.base)

    def pop(self):
        self.base = self.marks.pop()

    def alloc(self, cols, parts=128):
        off = (self.base + 15) // 16 * 16
        self.base = off + cols
        self.peak = max(self.peak, self.base)
        assert self.base <= ARENA_COLS, f"arena overflow: {self.base}"
        return Tile(self.h, parts, off, cols)


class Tile:
    def __init__(self, h, parts, off, cols):
        self.h = h
        self.parts = parts
        self.off = off
        self.cols = cols

    def ap(self):
        return self.h[0:self.parts, self.off:self.off + self.cols]

    def c(self, a, b, parts=None):
        parts = self.parts if parts is None else parts
        assert 0 <= a <= b <= self.cols
        return self.h[0:parts, self.off + a:self.off + b]

    def sub(self, a, b):
        return Tile(self.h, self.parts, self.off + a, b - a)


def build_program():
    nc = bass.Bass("TRN2", target_bir_lowering=False)
    g = {}

    def din(name, shape):
        g[name] = nc.dram_tensor(name, shape, F32, kind="ExternalInput").ap()

    din("x_in", [T, D])
    din("in_proj_w", [2 * E, D]);  din("conv_w", [E, 4]);   din("conv_b", [E])
    din("x_proj_w", [96, E]);      din("dt_proj_w", [E, RK]); din("dt_proj_b", [E])
    din("A_log", [E, N]);          din("Dvec", [E]);        din("out_proj_w", [D, E])
    din("slot_init", [NS, D]);     din("w1", [2 * D, DH]);  din("b1", [DH])
    din("w2", [DH, NS]);           din("b2", [NS]);         din("tau", [1])
    din("Wih", [3 * D, D]);        din("Whh", [3 * D, D])
    din("bih", [3 * D]);           din("bhh", [3 * D])
    din("wq", [D, DQ]);            din("bq", [DQ])
    din("wk", [D, DQ]);            din("bk", [DQ])
    din("wv", [D, D]);             din("bv", [D])
    din("ln_g", [D]);              din("ln_b", [D])
    din("wvp", [D, D]);            din("bvp", [D])
    din("wo", [D, D]);             din("bo", [D])

    g["out"] = nc.dram_tensor("out", [T, D], F32, kind="ExternalOutput").ap()
    g["s1"] = nc.dram_tensor("s1_scr", [T, D], F32).ap()
    g["s2"] = nc.dram_tensor("s2_scr", [T, D], F32).ap()
    g["xc_h"] = nc.dram_tensor("xc_scr", [E, T], F32).ap()
    g["zs_h"] = nc.dram_tensor("zs_scr", [E, T], F32).ap()
    g["xdbl_h"] = nc.dram_tensor("xdbl_scr", [96, T], F32).ap()
    g["ipT_h"] = nc.dram_tensor("ipT_scr", [8, 128, 2 * E], F32).ap()
    g["opT_h"] = nc.dram_tensor("opT_scr", [8, 128, E], F32).ap()

    dbg = {}
    for name, shape in [("s1", [T, D]), ("s2", [T, D]), ("xdbl", [96, T]),
                        ("xc", [E, T]), ("zs", [E, T])]:
        if name in _DEBUG:
            dbg[name] = nc.dram_tensor("dbg_" + name, shape, F32,
                                       kind="ExternalOutput").ap()

    arena_h = nc.alloc_sbuf_tensor("arena", [128, ARENA_COLS], F32)
    ar = Arena(arena_h)
    b = Builder(nc)

    ident = ar.alloc(128)
    ones_row = ar.alloc(128, parts=1)
    ones2d = ar.alloc(128)
    h_state = ar.alloc(256)
    vecs = ar.alloc(1024)

    VE = {}
    _vo = [0]

    def valloc(n):
        o = _vo[0]; _vo[0] += n
        assert _vo[0] <= 1024
        return o

    for nm, n in [("msum", 1), ("mneg", 1), ("lv", 1), ("rs", 1),
                  ("v8", 8), ("m1n", 1), ("Z", 1), ("Zr", 1),
                  ("gelub", 4), ("b2c", 1), ("tauc", 1), ("taubc", 1),
                  ("denr", 1), ("att1", 1), ("att2", 1), ("att3", 1),
                  ("sc1", 2), ("sc2", 2), ("sc3", 2),
                  ("bihc", 24), ("bhhc", 24), ("bqc", 2), ("bkc", 2),
                  ("convw", 64), ("convb", 16), ("Dc", 16), ("dtb", 16),
                  ("Acols", 256)]:
        VE[nm] = valloc(n)

    def V(nm, k=0, w=1, parts=128):
        o = vecs.off + VE[nm] + k
        return arena_h[0:parts, o:o + w]

    psA = nc.alloc_psum_tensor("psA", [128, 512], F32)
    psB = nc.alloc_psum_tensor("psB", [128, 512], F32)
    psC = nc.alloc_psum_tensor("psC", [128, 512], F32)
    psD = nc.alloc_psum_tensor("psD", [128, 512], F32)
    psE = nc.alloc_psum_tensor("psE", [128, 512], F32)
    ps_slot = nc.alloc_psum_tensor("ps_slot", [64, 1025], F32)

    SEM_DMA = "dq"
    SEM_OUT = "dout"

    def dma_in(src_ap, dst_ap):
        b.do("sync", lambda e, s=src_ap, d=dst_ap: e.dma_start(out=d, in_=s),
             inc=SEM_DMA, inc_n=16)

    def dma_out(dst_ap, src_ap, sem=SEM_OUT):
        b.do("sync", lambda e, s=src_ap, d=dst_ap: e.dma_start(out=d, in_=s),
             inc=sem, inc_n=16)

    def wait_dma(eng):
        b.wait(eng, SEM_DMA)

    def bar(eng):
        li = b.last_inc.get(eng)
        assert li is not None, f"bar({eng}) after inc-less instruction"
        b.wait(eng, li)

    def mm(ps_ap, s_ap, m_ap, start, stop, inc=None):
        b.do("tensor",
             lambda e, o=ps_ap, s=s_ap, m=m_ap, st=start, sp=stop:
             e.matmul(o, s, m, start=st, stop=sp, skip_group_check=True), inc=inc)

    def transpose(ps_ap, in_ap, rows, inc=None):
        b.do("tensor",
             lambda e, o=ps_ap, i=in_ap, r=rows:
             e.transpose(o, i, ident.c(0, r, parts=r)), inc=inc)

    def act(out_ap, in_ap, func, bias=0.0, scale=1.0, accum=None, inc=None):
        if func == AF.Copy and not isinstance(bias, float):
            func = AF.Identity
        b.do("scalar",
             lambda e, o=out_ap, i=in_ap, f=func, bi=bias, sc=scale, a=accum:
             e.activation(o, i, f, bias=bi, scale=sc, accum_out=a), inc=inc)

    def vtt(out_ap, a_ap, b_ap, op, inc=None):
        b.do("vector", lambda e, o=out_ap, x=a_ap, y=b_ap, p=op:
             e.tensor_tensor(o, x, y, p), inc=inc)

    def vts(out_ap, a_ap, s, op, inc=None):
        b.do("vector", lambda e, o=out_ap, x=a_ap, sc=s, p=op:
             e.tensor_scalar(o, x, sc, None, p), inc=inc)

    def vstt(out_ap, in0_ap, s, op0, op1, in1_ap, inc=None):
        b.do("vector", lambda e, o=out_ap, x=in0_ap, sc=s, p0=op0, p1=op1, y=in1_ap:
             e.scalar_tensor_tensor(o, x, sc, y, p0, p1), inc=inc)

    def vcopy(out_ap, in_ap, inc=None):
        b.do("vector", lambda e, o=out_ap, i=in_ap: e.tensor_copy(o, i), inc=inc)

    def bcast(ps_ap, row_ap, parts, inc):
        b.do("tensor", lambda e, o=ps_ap, r=row_ap, p=parts:
             e.matmul(o, ones_row.c(0, p, parts=1), r, start=True, stop=True,
                      skip_group_check=True), inc=inc)

    def layernorm_block(x_f, out_f, sq_tile):
        xa = x_f()
        b.wait("vector", "a_ln")
        b.do("vector", lambda e, a=xa: e.reduce_sum(V("msum"), a, axis=AX.X),
             inc="v_ln")
        bar("vector")
        vts(V("mneg"), V("msum"), -1.0 / D, ALU.mult, inc="v_ln")
        b.wait("scalar", "v_ln")
        act(out_f(), x_f(), AF.Copy, bias=V("mneg"), inc="a_ln")
        act(sq_tile.ap(), out_f(), AF.Square, accum=V("lv"), inc="a_ln")
        bar("scalar")
        act(V("lv"), V("lv"), AF.Ln, scale=1.0 / D, bias=1e-5, inc="a_ln")
        bar("scalar")
        act(V("rs"), V("lv"), AF.Exp, scale=-0.5, inc="a_ln")
        bar("scalar")
        act(out_f(), out_f(), AF.Copy, scale=V("rs"), inc="a_ln")

    def load_tiles(src_full, rows, cols, dst_tile):
        nrt = (rows + 127) // 128
        for rt in range(nrt):
            r0, r1 = rt * 128, min(rows, rt * 128 + 128)
            dma_in(src_full[r0:r1, :], dst_tile.c(rt * cols, rt * cols + cols,
                                                  parts=r1 - r0))

    def load_cols(src_vec, n, dst_f, scratch_row):
        b.wait("sync", "p_psE")
        dma_in(src_vec.unsqueeze(0), scratch_row.c(0, n, parts=1))
        wait_dma("tensor")
        for t in range((n + 127) // 128):
            w = min(128, n - t * 128)
            b.wait("tensor", "f_psE")
            transpose(psE[0:w, 0:1], scratch_row.c(t * 128, t * 128 + w, parts=1), 1,
                      inc="p_psE")
            b.wait("scalar", "p_psE")
            act(dst_f(t), psE[0:w, 0:1], AF.Copy, inc="f_psE")

    CONSTS = [1e-5, 0.1, 1e-8]
    cst = ar.alloc(16)
    for i, v in enumerate(CONSTS):
        b.do("vector", lambda e, i=i, v=v: e.memset(cst.c(i, i + 1), v), inc="v_pre")
        nc.const_aps.register(F32, v, cst.c(i, i + 1))

    b.do("gpsimd", lambda e: e.iota(ident.ap(), [[-1, 128]], channel_multiplier=1,
                                    allow_small_or_imprecise_dtypes=True), inc="g_pre")
    b.wait("vector", "g_pre")
    vts(ident.ap(), ident.ap(), 0.0, ALU.is_equal, inc="v_pre")
    b.do("vector", lambda e: e.memset(ones_row.ap(), 1.0), inc="v_pre")
    b.do("vector", lambda e: e.memset(ones2d.ap(), 1.0), inc="v_pre")
    b.do("vector", lambda e: e.memset(h_state.ap(), 0.0), inc="v_pre")
    bar("vector")
    b.wait("tensor", "v_pre")
    b.wait("scalar", "v_pre")
    b.wait("sync", "v_pre")

    def ffn_call(in_hbm, out_hbm, out_sem, cid=0):
        def tapd(name, shape):
            if cid == _TAPCALL and name in _DEBUG:
                if "dbg_" + name not in g:
                    g["dbg_" + name] = nc.dram_tensor("dbg_" + name, shape, F32,
                                                      kind="ExternalOutput").ap()
                return True
            return False
        ar.push()
        alpha = ar.alloc(NBLK * NS)
        alphaT = ar.alloc(T, parts=64)
        svwo2 = ar.alloc(D, parts=64)
        scr = ar.alloc(3 * D, parts=1)
        sq_s = ar.alloc(D)
        si_sb = ar.alloc(D, parts=64)

        ar.push()
        if "fpre" not in _FSUB:
            ar.pop(); ar.pop()
            return
        w1a = ar.alloc(8 * DH)
        w2h = ar.alloc(4 * NS)
        load_tiles(g["w1"][0:D, :], D, DH, w1a)
        w2r = ar.alloc(4 * NS)
        load_tiles(g["w2"], DH, NS, w2r)
        wait_dma("scalar")
        for kt in range(4):
            act(w2h.c(kt * NS, (kt + 1) * NS), w2r.c(kt * NS, (kt + 1) * NS),
                AF.Copy, scale=0.5, inc="a_fp")
        tcol = ar.alloc(16, parts=1)
        dma_in(g["tau"].unsqueeze(0), tcol.c(0, 1, parts=1))
        wait_dma("scalar")
        act(tcol.c(1, 2, parts=1), tcol.c(0, 1, parts=1), AF.Abs, inc="a_fp")
        bar("scalar")
        act(tcol.c(2, 3, parts=1), tcol.c(1, 2, parts=1), AF.Copy, bias=0.1, inc="a_fp")
        b.wait("vector", "a_fp")
        b.do("vector", lambda e: e.reciprocal(tcol.c(3, 4, parts=1),
                                              tcol.c(2, 3, parts=1)), inc="v_fp")
        bar("vector")
        b.wait("tensor", "v_fp")
        b.wait("tensor", "f_psE")
        bcast(psE[0:64, 0:1], tcol.c(3, 4, parts=1), 64, inc="p_psE")
        b.wait("scalar", "p_psE")
        act(V("tauc", parts=64), psE[0:64, 0:1], AF.Copy, inc="f_psE")
        load_cols(g["b2"], NS, lambda t: V("b2c", parts=64), scr)
        b.wait("vector", "f_psE")
        vtt(V("taubc", parts=64), V("tauc", parts=64), V("b2c", parts=64), ALU.mult,
            inc="v_fp")
        dma_in(g["slot_init"], si_sb.ap())
        onames = ar.alloc(16, parts=64)
        b.do("vector", lambda e: e.memset(onames.c(0, 1, parts=64), 1.0 / 64), inc="v_fp")
        bar("vector")
        wait_dma("tensor")
        b.wait("tensor", "v_fp")
        smean = ar.alloc(D, parts=1)
        for h in range(2):
            b.wait("tensor", "f_psE")
            mm(psE[0:1, 0:512], onames.c(0, 1, parts=64),
               si_sb.c(h * 512, (h + 1) * 512, parts=64), True, True, inc="p_psE")
            b.wait("scalar", "p_psE")
            act(smean.c(h * 512, (h + 1) * 512, parts=1), psE[0:1, 0:512], AF.Copy,
                inc="f_psE")
        smc = ar.alloc(8)
        b.wait("sync", "f_psE")
        for t in range(8):
            b.wait("tensor", "f_psE")
            transpose(psE[0:128, 0:1], smean.c(t * 128, (t + 1) * 128, parts=1), 1,
                      inc="p_psE")
            b.wait("scalar", "p_psE")
            act(smc.c(t, t + 1), psE[0:128, 0:1], AF.Copy, inc="f_psE")
        w1b = ar.alloc(8 * DH)
        load_tiles(g["w1"][D:2 * D, :], D, DH, w1b)
        wait_dma("tensor")
        b.wait("tensor", "f_psE")
        for jt in range(4):
            b.wait("tensor", "f_psE")
            for kt in range(8):
                mm(psE[0:128, 0:1],
                   w1b.c(kt * DH + jt * 128, kt * DH + (jt + 1) * 128),
                   smc.c(kt, kt + 1), kt == 0, kt == 7,
                   inc="p_psE" if kt == 7 else None)
            b.wait("scalar", "p_psE")
            act(V("gelub", jt), psE[0:128, 0:1], AF.Copy, inc="f_psE")
        b1c = ar.alloc(4)
        load_cols(g["b1"], DH, lambda t: b1c.c(t, t + 1), scr)
        b.wait("vector", "f_psE")
        vtt(V("gelub", 0, 4), V("gelub", 0, 4), b1c.c(0, 4), ALU.add, inc="v_fp")
        bar("vector")

        b.fence()
        if "fa" not in _FSUB:
            ar.pop(); ar.pop()
            return
        XW = 1040
        xw = ar.alloc(4 * XW)
        xnT = ar.alloc(8 * TC)
        hT = ar.alloc(4 * TC)
        hb_s = ar.alloc(TC)
        te_s = ar.alloc(TC)
        logT = ar.alloc(TC, parts=64)
        logtok = ar.alloc(4 * NS)
        sin_s = ar.alloc(4 * D)
        tk_s = ar.alloc(2 * NS)
        b.wait("vector", "a_fp")
        for j in range(4):
            b.do("vector", lambda e, j=j:
                 e.memset(xw.c(j * XW + 1024, j * XW + 1025), 1.0), inc="v_fa")
        bar("vector")
        b.wait("tensor", "v_fa")

        for c in range(NCH):
            b.wait("sync", "a_ln")
            b.wait("sync", "mm_slot")
            b.wait("sync", "dbgw")
            for j in range(4):
                blk = c * 4 + j
                dma_in(in_hbm[blk * 128:(blk + 1) * 128, :], sin_s.c(j * D, (j + 1) * D))
            wait_dma("vector")
            wait_dma("scalar")
            for j in range(4):
                layernorm_block(lambda j=j: sin_s.c(j * D, (j + 1) * D),
                                lambda j=j: xw.c(j * XW, j * XW + 1024), sq_s)
                if "lnstats" in _DEBUG and cid == _TAPCALL:
                    tapd("lnstats", [T, 16])
                    blk = c * 4 + j
                    b.wait("sync", "a_ln")
                    dma_out(g["dbg_lnstats"][blk * 128:(blk + 1) * 128, :],
                            V("msum", 0, 16), sem="dbgw")
            b.wait("tensor", "a_ln")
            for dt_i in range(8):
                b.wait("tensor", "f_psD")
                for j in range(4):
                    transpose(psD[0:128, j * 128:(j + 1) * 128],
                              xw.c(j * XW + dt_i * 128, j * XW + (dt_i + 1) * 128), 128,
                              inc="p_psD" if j == 3 else None)
                b.wait("scalar", "p_psD")
                act(xnT.c(dt_i * TC, (dt_i + 1) * TC), psD[0:128, 0:512], AF.Copy,
                    inc="f_psD")
            b.wait("tensor", "f_psD")
            for jt in range(4):
                ps, pss = (psA, "A") if jt % 2 == 0 else (psB, "B")
                b.wait("tensor", "f_psA")
                b.wait("tensor", "f_psB")
                for kt in range(8):
                    mm(ps[0:128, 0:512],
                       w1a.c(kt * DH + jt * 128, kt * DH + (jt + 1) * 128),
                       xnT.c(kt * TC, (kt + 1) * TC), kt == 0, kt == 7,
                       inc="p_ps" + pss if kt == 7 else None)
                b.wait("scalar", "p_ps" + pss)
                b.wait("scalar", "f_psA")
                b.wait("scalar", "f_psB")
                act(hb_s.ap(), ps[0:128, 0:512], AF.Copy, bias=V("gelub", jt), inc="a_fa")
                act(te_s.ap(), hb_s.ap(), AF.Erf, scale=0.7071067811865476, inc="a_fa")
                b.wait("vector", "a_fa")
                vstt(hT.c(jt * TC, (jt + 1) * TC), te_s.ap(), 1.0, ALU.add, ALU.mult,
                     hb_s.ap(), inc="f_ps" + pss)
            b.wait("tensor", "f_psA")
            b.wait("tensor", "f_psB")
            b.wait("tensor", "f_psC")
            for kt in range(4):
                mm(psC[0:64, 0:512], w2h.c(kt * NS, (kt + 1) * NS),
                   hT.c(kt * TC, (kt + 1) * TC), kt == 0, kt == 3,
                   inc="p_psC" if kt == 3 else None)
            b.wait("scalar", "p_psC")
            act(logT.ap(), psC[0:64, 0:512], AF.Copy, scale=V("tauc", parts=64),
                bias=V("taubc", parts=64), inc="f_psC")
            b.wait("tensor", "f_psC")
            b.wait("tensor", "f_psD")
            for j in range(4):
                transpose(psD[0:128, j * NS:(j + 1) * NS],
                          logT.c(j * 128, (j + 1) * 128, parts=64), 64,
                          inc="p_psD" if j == 3 else None)
            b.wait("scalar", "p_psD")
            act(logtok.ap(), psD[0:128, 0:256], AF.Copy, inc="f_psD")
            b.wait("vector", "f_psD")
            for j in range(4):
                blk = c * 4 + j
                b.do("vector", lambda e, j=j: e.max(out=V("v8", 0, 8),
                                                    in_=logtok.c(j * NS, (j + 1) * NS)),
                     inc="v_fa")
                bar("vector")
                vts(V("m1n"), V("v8", 0, 1), -1.0, ALU.mult, inc="v_fa")
                vts(tk_s.c(NS, 2 * NS), logtok.c(j * NS, (j + 1) * NS), V("v8", 7, 1),
                    ALU.is_ge, inc="v_fa")
                b.wait("scalar", "v_fa")
                act(V("v8", 0, 8), V("v8", 0, 8), AF.Exp, bias=V("m1n"), inc="a_fa")
                b.wait("vector", "a_fa")
                b.do("vector", lambda e: e.reduce_sum(V("Z"), V("v8", 0, 8), axis=AX.X),
                     inc="v_fa")
                bar("vector")
                b.do("vector", lambda e: e.reciprocal(V("Zr"), V("Z")), inc="v_fa")
                b.wait("scalar", "v_fa")
                act(tk_s.c(0, NS), logtok.c(j * NS, (j + 1) * NS), AF.Exp,
                    bias=V("m1n"), inc="a_fa")
                b.wait("vector", "a_fa")
                vstt(alpha.c(blk * NS, (blk + 1) * NS), tk_s.c(0, NS), V("Zr"),
                     ALU.mult, ALU.mult, tk_s.c(NS, 2 * NS), inc="v_fa")
                bar("vector")
                if tapd("logtok", [T, NS]) if False else ("logtok" in _DEBUG and cid == _TAPCALL):
                    tapd("logtok", [T, NS])
                    b.wait("sync", "f_psD")
                    dma_out(g["dbg_logtok"][blk * 128:(blk + 1) * 128, :],
                            logtok.c(j * NS, (j + 1) * NS), sem="dbgw")
                if "xn" in _DEBUG and cid == _TAPCALL:
                    tapd("xn", [T, D])
                    b.wait("sync", "a_ln")
                    dma_out(g["dbg_xn"][blk * 128:(blk + 1) * 128, :],
                            xw.c(j * XW, j * XW + 1024), sem="dbgw")
                if "alpha" in _DEBUG and cid == _TAPCALL:
                    tapd("alpha", [T, NS])
                    b.wait("sync", "v_fa")
                    dma_out(g["dbg_alpha"][blk * 128:(blk + 1) * 128, :],
                            alpha.c(blk * NS, (blk + 1) * NS), sem="dbgw")
                b.wait("tensor", "v_fa")
                b.wait("tensor", "f_psE")
                transpose(psE[0:64, 0:128], alpha.c(blk * NS, (blk + 1) * NS), 128,
                          inc="p_psE")
                b.wait("scalar", "p_psE")
                act(alphaT.c(blk * 128, (blk + 1) * 128, parts=64), psE[0:64, 0:128],
                    AF.Copy, inc="f_psE")
                mm(ps_slot[0:64, 0:512], alpha.c(blk * NS, (blk + 1) * NS),
                   xw.c(j * XW, j * XW + 512), blk == 0, blk == NBLK - 1)
                mm(ps_slot[0:64, 512:1024], alpha.c(blk * NS, (blk + 1) * NS),
                   xw.c(j * XW + 512, j * XW + 1024), blk == 0, blk == NBLK - 1)
                mm(ps_slot[0:64, 1024:1025], alpha.c(blk * NS, (blk + 1) * NS),
                   xw.c(j * XW + 1024, j * XW + 1025), blk == 0, blk == NBLK - 1,
                   inc="mm_slot")

        b.wait("sync", "mm_slot")
        b.wait("vector", "mm_slot")
        b.wait("scalar", "mm_slot")
        ar.pop()
        if "fb" not in _FSUB:
            ar.pop()
            return
        b.fence()
        ar.push()
        slot_in = ar.alloc(1040, parts=64)
        b.wait("scalar", "mm_slot")
        act(slot_in.c(0, 1025, parts=64), ps_slot[0:64, 0:1025], AF.Copy, inc="f_slot")
        b.wait("vector", "f_slot")
        vts(slot_in.c(1024, 1025, parts=64), slot_in.c(1024, 1025, parts=64), 1e-8,
            ALU.add, inc="v_fb")
        bar("vector")
        b.do("vector", lambda e: e.reciprocal(V("denr", parts=64),
                                              slot_in.c(1024, 1025, parts=64)),
             inc="v_fb")
        bar("vector")
        vts(slot_in.c(0, 1024, parts=64), slot_in.c(0, 1024, parts=64),
            V("denr", parts=64), ALU.mult, inc="v_fb")
        bar("vector")
        if tapd("si", [NS, 1025]):
            b.wait("sync", "v_fb")
            dma_out(g["dbg_si"], slot_in.c(0, 1025, parts=64), sem="dbgw")
        siT = ar.alloc(8 * NS)
        siniT = ar.alloc(8 * NS)
        b.wait("tensor", "v_fb")
        for (src, dstT) in ([(slot_in, siT), (si_sb, siniT)] if _FBL >= 1 else []):
            for j in range(8):
                b.wait("tensor", "f_psD")
                transpose(psD[0:128, 0:64], src.c(j * 128, (j + 1) * 128, parts=64), 64,
                          inc="p_psD")
                b.wait("scalar", "p_psD")
                act(dstT.c(j * NS, (j + 1) * NS), psD[0:128, 0:64], AF.Copy, inc="f_psD")
        load_cols(g["bih"], 3 * D, lambda t: V("bihc", t), scr)
        load_cols(g["bhh"], 3 * D, lambda t: V("bhhc", t), scr)
        ghT = ar.alloc(24 * NS)
        giT = ar.alloc(24 * NS)
        ar.push()
        wscr = ar.alloc(D)
        HW2 = 12 * 128
        WT = ar.alloc(8 * HW2)
        for (wname, dstT, bcol, srcT) in ([("Whh", ghT, "bhhc", siniT),
                                          ("Wih", giT, "bihc", siT)] if _FBL >= 2 else []):
            for hf in range(2):
                for rt2 in range(12):
                    rt = hf * 12 + rt2
                    b.wait("sync", "p_psD")
                    b.wait("sync", "f_psA")
                    b.wait("sync", "f_psB")
                    dma_in(g[wname][rt * 128:(rt + 1) * 128, :], wscr.ap())
                    wait_dma("tensor")
                    for dt_i in range(8):
                        b.wait("tensor", "f_psD")
                        transpose(psD[0:128, 0:128],
                                  wscr.c(dt_i * 128, (dt_i + 1) * 128),
                                  128, inc="p_psD")
                        b.wait("scalar", "p_psD")
                        act(WT.c(dt_i * HW2 + rt2 * 128, dt_i * HW2 + (rt2 + 1) * 128),
                            psD[0:128, 0:128], AF.Copy, inc="f_psD")
                b.wait("tensor", "f_psD")
                for jt2 in range(12):
                    jt = hf * 12 + jt2
                    ps, pss = (psA, "A") if jt % 2 == 0 else (psB, "B")
                    b.wait("tensor", "f_ps" + pss)
                    for kt in range(8):
                        mm(ps[0:128, 0:NS],
                           WT.c(kt * HW2 + jt2 * 128, kt * HW2 + (jt2 + 1) * 128),
                           srcT.c(kt * NS, (kt + 1) * NS), kt == 0, kt == 7,
                           inc="p_ps" + pss if kt == 7 else None)
                    b.wait("scalar", "p_ps" + pss)
                    act(dstT.c(jt * NS, (jt + 1) * NS), ps[0:128, 0:NS], AF.Copy,
                        bias=V(bcol, jt), inc="f_ps" + pss)
                b.wait("tensor", "f_psA")
                b.wait("tensor", "f_psB")
        ar.pop()
        SnT = ar.alloc(8 * NS)
        rb = ar.alloc(NS)
        b.wait("vector", "f_psA")
        b.wait("vector", "f_psB")
        for t8 in (range(8) if _FBL >= 3 else []):
            gi_r = lambda t8=t8: giT.c(t8 * NS, (t8 + 1) * NS)
            gh_r = lambda t8=t8: ghT.c(t8 * NS, (t8 + 1) * NS)
            gi_z = lambda t8=t8: giT.c((8 + t8) * NS, (9 + t8) * NS)
            gh_z = lambda t8=t8: ghT.c((8 + t8) * NS, (9 + t8) * NS)
            gi_n = lambda t8=t8: giT.c((16 + t8) * NS, (17 + t8) * NS)
            gh_n = lambda t8=t8: ghT.c((16 + t8) * NS, (17 + t8) * NS)
            b.wait("vector", "a_fb")
            vtt(rb.ap(), gi_r(), gh_r(), ALU.add, inc="v_fb")
            b.wait("scalar", "v_fb")
            act(rb.ap(), rb.ap(), AF.Sigmoid, inc="a_fb")
            b.wait("vector", "a_fb")
            vtt(gh_n(), rb.ap(), gh_n(), ALU.mult, inc="v_fb")
            bar("vector")
            vtt(gi_n(), gi_n(), gh_n(), ALU.add, inc="v_fb")
            b.wait("scalar", "v_fb")
            act(gi_n(), gi_n(), AF.Tanh, inc="a_fb")
            vtt(gi_z(), gi_z(), gh_z(), ALU.add, inc="v_fb")
            b.wait("scalar", "v_fb")
            act(gi_z(), gi_z(), AF.Sigmoid, inc="a_fb")
            b.wait("vector", "a_fb")
            vtt(rb.ap(), siniT.c(t8 * NS, (t8 + 1) * NS), gi_n(), ALU.subtract,
                inc="v_fb")
            bar("vector")
            vtt(rb.ap(), rb.ap(), gi_z(), ALU.mult, inc="v_fb")
            bar("vector")
            vtt(SnT.c(t8 * NS, (t8 + 1) * NS), gi_n(), rb.ap(), ALU.add, inc="v_fb")
        bar("vector")
        if tapd("SnT", [128, 8 * NS]):
            b.wait("sync", "v_fb")
            dma_out(g["dbg_SnT"], SnT.ap(), sem="dbgw")
        load_cols(g["bq"], DQ, lambda t: V("bqc", t), scr)
        load_cols(g["bk"], DQ, lambda t: V("bkc", t), scr)
        qT = ar.alloc(2 * NS)
        kT = ar.alloc(2 * NS)
        ar.push()
        wqt = ar.alloc(8 * DQ)
        wkt = ar.alloc(8 * DQ)
        load_tiles(g["wq"], D, DQ, wqt)
        load_tiles(g["wk"], D, DQ, wkt)
        wait_dma("tensor")
        b.wait("tensor", "v_fb")
        for (wt_, dT, bc) in ([(wqt, qT, "bqc"), (wkt, kT, "bkc")] if _FBL >= 4 else []):
            for qt in range(2):
                b.wait("tensor", "f_psA")
                for kt in range(8):
                    mm(psA[0:128, 0:NS],
                       wt_.c(kt * DQ + qt * 128, kt * DQ + (qt + 1) * 128),
                       SnT.c(kt * NS, (kt + 1) * NS), kt == 0, kt == 7,
                       inc="p_psA" if kt == 7 else None)
                b.wait("scalar", "p_psA")
                act(dT.c(qt * NS, (qt + 1) * NS), psA[0:128, 0:NS], AF.Copy,
                    bias=V(bc, qt), inc="f_psA")
        b.wait("sync", "p_psA")
        ar.pop()
        attn = ar.alloc(NS, parts=64)
        b.wait("tensor", "f_psA")
        b.wait("tensor", "f_psB")
        for kt in (range(2) if _FBL >= 5 else []):
            mm(psB[0:64, 0:NS], qT.c(kt * NS, (kt + 1) * NS),
               kT.c(kt * NS, (kt + 1) * NS), kt == 0, kt == 1,
               inc="p_psB" if kt == 1 else None)
        b.wait("scalar", "p_psB")
        if _FBL >= 5:
            act(attn.ap(), psB[0:64, 0:NS], AF.Copy, scale=DQ ** (-0.5), inc="f_psB")
        b.wait("vector", "f_psB")
        if _FBL < 6:
            b.wait("sync", "v_fb"); b.wait("sync", "a_fb")
            ar.pop(); ar.pop(); return
        b.do("vector", lambda e: e.reduce_max(V("att1", parts=64), attn.ap(), axis=AX.X),
             inc="v_fb")
        bar("vector")
        vts(V("att2", parts=64), V("att1", parts=64), -1.0, ALU.mult, inc="v_fb")
        b.wait("scalar", "v_fb")
        act(attn.ap(), attn.ap(), AF.Exp, bias=V("att2", parts=64), inc="a_fb")
        b.wait("vector", "a_fb")
        b.do("vector", lambda e: e.reduce_sum(V("att3", parts=64), attn.ap(), axis=AX.X),
             inc="v_fb")
        bar("vector")
        b.do("vector", lambda e: e.reciprocal(V("att1", parts=64), V("att3", parts=64)),
             inc="v_fb")
        bar("vector")
        vts(attn.ap(), attn.ap(), V("att1", parts=64), ALU.mult, inc="v_fb")
        bar("vector")
        attnT = ar.alloc(NS, parts=64)
        b.wait("tensor", "v_fb")
        b.wait("tensor", "f_psE")
        transpose(psE[0:64, 0:64], attn.ap(), 64, inc="p_psE")
        b.wait("scalar", "p_psE")
        act(attnT.ap(), psE[0:64, 0:64], AF.Copy, inc="f_psE")
        if _FBL < 7:
            b.wait("sync", "v_fb"); b.wait("sync", "a_fb")
            ar.pop(); ar.pop(); return
        attnT_ = None
        vsb = ar.alloc(D, parts=64)
        Snsm = ar.alloc(D, parts=64)
        ysb = ar.alloc(D, parts=64)
        bvrep = ar.alloc(D, parts=64)
        ar.push()
        wvt = ar.alloc(8 * D)
        load_tiles(g["wv"], D, D, wvt)
        wait_dma("tensor")
        for h in range(2):
            b.wait("tensor", "f_psA")
            for kt in range(8):
                mm(psA[0:64, 0:512], SnT.c(kt * NS, (kt + 1) * NS),
                   wvt.c(kt * D + h * 512, kt * D + (h + 1) * 512), kt == 0, kt == 7,
                   inc="p_psA" if kt == 7 else None)
            b.wait("scalar", "p_psA")
            act(vsb.c(h * 512, (h + 1) * 512, parts=64), psA[0:64, 0:512], AF.Copy,
                inc="f_psA")
        b.wait("tensor", "f_psA")
        for j in range(8):
            b.wait("tensor", "f_psD")
            transpose(psD[0:64, 0:128], SnT.c(j * NS, (j + 1) * NS), 128, inc="p_psD")
            b.wait("scalar", "p_psD")
            act(Snsm.c(j * 128, (j + 1) * 128, parts=64), psD[0:64, 0:128], AF.Copy,
                inc="f_psD")
        dma_in(g["bv"].unsqueeze(0).broadcast_to([64, D]), bvrep.ap())
        b.wait("vector", "f_psD")
        b.wait("tensor", "f_psE")
        for h in range(2):
            ps, pss = (psA, "A") if h == 0 else (psB, "B")
            b.wait("tensor", "f_ps" + pss)
            mm(ps[0:64, 0:512], attnT.ap(), vsb.c(h * 512, (h + 1) * 512, parts=64),
               True, True, inc="p_ps" + pss)
            b.wait("vector", "p_ps" + pss)
            vtt(ysb.c(h * 512, (h + 1) * 512, parts=64),
                Snsm.c(h * 512, (h + 1) * 512, parts=64), ps[0:64, 0:512], ALU.add,
                inc="f_ps" + pss)
        wait_dma("vector")
        bar("vector")
        vtt(ysb.ap(), ysb.ap(), bvrep.ap(), ALU.add, inc="v_fb")
        bar("vector")
        b.wait("sync", "v_fb")
        ar.pop()
        if _FBL < 8:
            b.wait("sync", "a_fb"); ar.pop(); ar.pop(); return
        lngr = ar.alloc(D, parts=64)
        lnbr = ar.alloc(D, parts=64)
        dma_in(g["ln_g"].unsqueeze(0).broadcast_to([64, D]), lngr.ap())
        dma_in(g["ln_b"].unsqueeze(0).broadcast_to([64, D]), lnbr.ap())
        b.do("vector", lambda e: e.reduce_sum(V("sc1", 0, 1, parts=64), ysb.ap(),
                                              axis=AX.X), inc="v_fb")
        bar("vector")
        vts(V("sc1", 1, 1, parts=64), V("sc1", 0, 1, parts=64), -1.0 / D, ALU.mult,
            inc="v_fb")
        b.wait("scalar", "v_fb")
        act(ysb.ap(), ysb.ap(), AF.Copy, bias=V("sc1", 1, 1, parts=64), inc="a_fb")
        act(sq_s.c(0, D, parts=64), ysb.ap(), AF.Square,
            accum=V("sc2", 0, 1, parts=64), inc="a_fb")
        bar("scalar")
        act(V("sc2", 1, 1, parts=64), V("sc2", 0, 1, parts=64), AF.Ln, scale=1.0 / D,
            bias=1e-5, inc="a_fb")
        bar("scalar")
        act(V("sc3", 0, 1, parts=64), V("sc2", 1, 1, parts=64), AF.Exp, scale=-0.5,
            inc="a_fb")
        bar("scalar")
        act(ysb.ap(), ysb.ap(), AF.Copy, scale=V("sc3", 0, 1, parts=64), inc="a_fb")
        wait_dma("vector")
        b.wait("vector", "a_fb")
        vtt(ysb.ap(), ysb.ap(), lngr.ap(), ALU.mult, inc="v_fb")
        bar("vector")
        vtt(ysb.ap(), ysb.ap(), lnbr.ap(), ALU.add, inc="v_fb")
        bar("vector")
        if _FBL < 9:
            b.wait("sync", "v_fb"); b.wait("sync", "a_fb")
            ar.pop(); ar.pop(); return
        yT = ar.alloc(8 * NS)
        svT = ar.alloc(8 * NS)
        ar.push()
        b.wait("tensor", "v_fb")
        for j in range(8):
            b.wait("tensor", "f_psD")
            transpose(psD[0:128, 0:64], ysb.c(j * 128, (j + 1) * 128, parts=64), 64,
                      inc="p_psD")
            b.wait("scalar", "p_psD")
            act(yT.c(j * NS, (j + 1) * NS), psD[0:128, 0:64], AF.Copy, inc="f_psD")
        wvpt = ar.alloc(8 * D)
        svsb = ar.alloc(D, parts=64)
        load_tiles(g["wvp"], D, D, wvpt)
        wait_dma("tensor")
        b.wait("tensor", "f_psD")
        for h in range(2):
            ps, pss = (psA, "A") if h == 0 else (psB, "B")
            b.wait("tensor", "f_ps" + pss)
            for kt in range(8):
                mm(ps[0:64, 0:512], yT.c(kt * NS, (kt + 1) * NS),
                   wvpt.c(kt * D + h * 512, kt * D + (h + 1) * 512), kt == 0, kt == 7,
                   inc="p_ps" + pss if kt == 7 else None)
            b.wait("scalar", "p_ps" + pss)
            act(svsb.c(h * 512, (h + 1) * 512, parts=64), ps[0:64, 0:512], AF.Copy,
                inc="f_ps" + pss)
        b.wait("tensor", "f_psA")
        b.wait("tensor", "f_psB")
        for j in range(8):
            b.wait("tensor", "f_psD")
            transpose(psD[0:128, 0:64], svsb.c(j * 128, (j + 1) * 128, parts=64), 64,
                      inc="p_psD")
            b.wait("scalar", "p_psD")
            act(svT.c(j * NS, (j + 1) * NS), psD[0:128, 0:64], AF.Copy, inc="f_psD")
        b.wait("sync", "f_psD")
        ar.pop()
        ar.push()
        wot = ar.alloc(8 * D)
        load_tiles(g["wo"], D, D, wot)
        bvpc = ar.alloc(8)
        load_cols(g["bvp"], D, lambda t: bvpc.c(t, t + 1), scr)
        c2row = ar.alloc(D, parts=1)
        bor = ar.alloc(D, parts=1)
        dma_in(g["bo"].unsqueeze(0), bor.ap())
        wait_dma("tensor")
        b.wait("tensor", "f_psD")
        b.wait("tensor", "f_psE")
        for h in range(2):
            b.wait("tensor", "f_psE")
            for kt in range(8):
                mm(psE[0:1, 0:512], bvpc.c(kt, kt + 1),
                   wot.c(kt * D + h * 512, kt * D + (h + 1) * 512), kt == 0, kt == 7,
                   inc="p_psE" if kt == 7 else None)
            b.wait("vector", "p_psE")
            b.wait("vector", SEM_DMA)
            vtt(c2row.c(h * 512, (h + 1) * 512, parts=1), psE[0:1, 0:512],
                bor.c(h * 512, (h + 1) * 512, parts=1), ALU.add, inc="f_psE")
        bar("vector")
        c2rep = ar.alloc(D, parts=64)
        for h in range(2):
            ps, pss = (psA, "A") if h == 0 else (psB, "B")
            b.wait("tensor", "f_ps" + pss)
            for kt in range(8):
                mm(ps[0:64, 0:512], svT.c(kt * NS, (kt + 1) * NS),
                   wot.c(kt * D + h * 512, kt * D + (h + 1) * 512), kt == 0, kt == 7,
                   inc="p_ps" + pss if kt == 7 else None)
            b.wait("tensor", "f_psE")
            b.wait("tensor", "v_fb")
            bcast(psE[0:64, 0:512], c2row.c(h * 512, (h + 1) * 512, parts=1), 64,
                  inc="p_psE")
            b.wait("scalar", "p_psE")
            act(c2rep.c(h * 512, (h + 1) * 512, parts=64), psE[0:64, 0:512], AF.Copy,
                inc="f_psE")
            b.wait("vector", "p_ps" + pss)
            b.wait("vector", "f_psE")
            vtt(svwo2.c(h * 512, (h + 1) * 512, parts=64), ps[0:64, 0:512],
                c2rep.c(h * 512, (h + 1) * 512, parts=64), ALU.add, inc="f_ps" + pss)
        bar("vector")
        if tapd("svwo2", [NS, D]):
            b.wait("sync", "f_psA")
            b.wait("sync", "f_psB")
            dma_out(g["dbg_svwo2"], svwo2.ap(), sem="dbgw")
        b.wait("sync", "f_psA")
        b.wait("sync", "f_psB")
        ar.pop()

        if "fc" not in _FSUB:
            ar.pop(); ar.pop()
            return
        b.fence()
        sin_c = ar.alloc(4 * D)
        sout = ar.alloc(4 * D)
        for c in range(NCH):
            b.wait("sync", "f_psA")
            b.wait("sync", "f_psB")
            b.wait("sync", out_sem)
            for j in range(4):
                blk = c * 4 + j
                dma_in(in_hbm[blk * 128:(blk + 1) * 128, :], sin_c.c(j * D, (j + 1) * D))
            wait_dma("vector")
            for j in range(4):
                blk = c * 4 + j
                for h in range(2):
                    ps, pss = (psA, "A") if h == 0 else (psB, "B")
                    b.wait("tensor", "f_ps" + pss)
                    mm(ps[0:128, 0:512], alphaT.c(blk * 128, (blk + 1) * 128, parts=64),
                       svwo2.c(h * 512, (h + 1) * 512, parts=64), True, True,
                       inc="p_ps" + pss)
                    b.wait("vector", "p_ps" + pss)
                    vstt(sout.c(j * D + h * 512, j * D + (h + 1) * 512),
                         ps[0:128, 0:512], 0.5, ALU.mult, ALU.add,
                         sin_c.c(j * D + h * 512, j * D + (h + 1) * 512),
                         inc="f_ps" + pss)
            b.wait("sync", "f_psA")
            b.wait("sync", "f_psB")
            for j in range(4):
                blk = c * 4 + j
                dma_out(out_hbm[blk * 128:(blk + 1) * 128, :],
                        sout.c(j * D, (j + 1) * D), sem=out_sem)
        ar.pop()
        ar.pop()

    m_static = {
        "xdbl": ar.alloc(T, parts=96),
        "dtT": ar.alloc(E, parts=64),
        "xpT": ar.alloc(16 * 96),
    }

    def m_pre():
        ar.push()
        wscr = ar.alloc(2 * E)
        scr = ar.alloc(3 * D, parts=1)
        tbuf = ar.alloc(128)
        for rt in range(32):
            b.wait("sync", "p_psD")
            dma_in(g["in_proj_w"][rt * 128:(rt + 1) * 128, :], wscr.c(0, D))
            wait_dma("tensor")
            for dt_i in range(8):
                b.wait("tensor", "f_psD")
                transpose(psD[0:128, 0:128], wscr.c(dt_i * 128, (dt_i + 1) * 128), 128,
                          inc="p_psD")
                b.wait("scalar", "p_psD")
                b.wait("scalar", "mpw")
                act(tbuf.ap(), psD[0:128, 0:128], AF.Copy, inc="f_psD")
                b.wait("sync", "f_psD")
                dma_out(g["ipT_h"][dt_i, :, rt * 128:(rt + 1) * 128], tbuf.ap(),
                        sem="mpw")
        for mt in range(8):
            b.wait("sync", "p_psD")
            dma_in(g["out_proj_w"][mt * 128:(mt + 1) * 128, :], wscr.c(0, E))
            wait_dma("tensor")
            for kt in range(16):
                b.wait("tensor", "f_psD")
                transpose(psD[0:128, 0:128], wscr.c(kt * 128, (kt + 1) * 128), 128,
                          inc="p_psD")
                b.wait("scalar", "p_psD")
                b.wait("scalar", "mpw")
                act(tbuf.ap(), psD[0:128, 0:128], AF.Copy, inc="f_psD")
                b.wait("sync", "f_psD")
                dma_out(g["opT_h"][mt, :, kt * 128:(kt + 1) * 128], tbuf.ap(),
                        sem="mpw")
        dma_in(g["conv_w"].rearrange("(et p) k -> p et k", p=128),
               V("convw", 0, 64).rearrange("p (et k) -> p et k", k=4))
        dma_in(g["A_log"].rearrange("(et p) n -> p et n", p=128),
               V("Acols", 0, 256).rearrange("p (et n) -> p et n", n=N))
        load_cols(g["conv_b"], E, lambda t: V("convb", t), scr)
        load_cols(g["Dvec"], E, lambda t: V("Dc", t), scr)
        load_cols(g["dt_proj_b"], E, lambda t: V("dtb", t), scr)
        wait_dma("scalar")
        act(V("Acols", 0, 256), V("Acols", 0, 256), AF.Exp, inc="a_mp")
        b.wait("vector", "a_mp")
        vts(V("Acols", 0, 256), V("Acols", 0, 256), -1.0, ALU.mult, inc="v_mp")
        bar("vector")
        for rt in range(16):
            b.wait("sync", "p_psD")
            dma_in(g["dt_proj_w"][rt * 128:(rt + 1) * 128, :], wscr.c(0, RK))
            wait_dma("tensor")
            b.wait("tensor", "f_psD")
            transpose(psD[0:64, 0:128], wscr.c(0, RK), 128, inc="p_psD")
            b.wait("scalar", "p_psD")
            act(m_static["dtT"].c(rt * 128, (rt + 1) * 128, parts=64),
                psD[0:64, 0:128], AF.Copy, inc="f_psD")
        b.wait("sync", "p_psD")
        dma_in(g["x_proj_w"], wscr.c(0, E, parts=96))
        wait_dma("tensor")
        for et in range(16):
            b.wait("tensor", "f_psD")
            transpose(psD[0:128, 0:96], wscr.c(et * 128, (et + 1) * 128, parts=96), 96,
                      inc="p_psD")
            b.wait("scalar", "p_psD")
            act(m_static["xpT"].c(et * 96, (et + 1) * 96), psD[0:128, 0:96], AF.Copy,
                inc="f_psD")
        b.wait("vector", "f_psD")
        b.wait("sync", "mpw")
        ar.pop()

    def m1():
        b.fence()
        ar.push()
        sin1 = ar.alloc(D)
        xn1 = ar.alloc(D)
        sq1 = ar.alloc(D)
        xnT = ar.alloc(8 * TC)
        wsl = ar.alloc(2 * 1024)
        CB = TC + 16
        cvb = ar.alloc(2 * CB)
        xce = ar.alloc(TC)
        tails = ar.alloc(64)
        b.do("vector", lambda e: e.memset(tails.ap(), 0.0), inc="v_m1")
        bar("vector")
        b.wait("sync", "v_m1")

        for c in range(NCH):
            for j in range(4):
                blk = c * 4 + j
                b.wait("sync", "f_psD")
                dma_in(g["s1"][blk * 128:(blk + 1) * 128, :], sin1.ap())
                wait_dma("vector")
                wait_dma("scalar")
                layernorm_block(lambda: sin1.ap(), lambda: xn1.ap(), sq1)
                b.wait("tensor", "a_ln")
                b.wait("tensor", "f_psD")
                for dt_i in range(8):
                    transpose(psD[0:128, (dt_i % 4) * 128:(dt_i % 4 + 1) * 128],
                              xn1.c(dt_i * 128, (dt_i + 1) * 128), 128,
                              inc="p_psD" if dt_i % 4 == 3 else None)
                    if dt_i % 4 == 3:
                        b.wait("scalar", "p_psD")
                        for q2 in range(4):
                            di = dt_i - 3 + q2
                            act(xnT.c(di * TC + j * 128, di * TC + (j + 1) * 128),
                                psD[0:128, q2 * 128:(q2 + 1) * 128], AF.Copy,
                                inc="f_psD" if q2 == 3 else None)
                        if dt_i == 3:
                            b.wait("tensor", "f_psD")
            b.wait("tensor", "f_psD")

            def ip_load(et):
                wb = wsl.sub((et % 2) * 1024, (et % 2) * 1024 + 1024)
                b.wait("sync", "f_psA")
                b.wait("sync", "f_psB")
                for kt in range(8):
                    dma_in(g["ipT_h"][kt, :, et * 128:(et + 1) * 128],
                           wb.c(kt * 128, (kt + 1) * 128))
                ipw_cnt[et % 2] = b.cnt[SEM_DMA]

            ipw_cnt = [0, 0]
            ip_load(0)
            for et in range(32):
                wb = wsl.sub((et % 2) * 1024, (et % 2) * 1024 + 1024)
                if et + 1 < 32:
                    ip_load(et + 1)
                b.wait_val("tensor", SEM_DMA, ipw_cnt[et % 2])
                ps, pss = (psA, "A") if et % 2 == 0 else (psB, "B")
                b.wait("tensor", "f_ps" + pss)
                for kt in range(8):
                    mm(ps[0:128, 0:512], wb.c(kt * 128, (kt + 1) * 128),
                       xnT.c(kt * TC, (kt + 1) * TC), kt == 0, kt == 7,
                       inc="p_ps" + pss if kt == 7 else None)
                b.wait("scalar", "p_ps" + pss)
                if et < 16:
                    cb = cvb.sub((et % 2) * CB, (et % 2) * CB + CB)
                    b.wait("scalar", "v_cv")
                    act(cb.c(3, 3 + TC), ps[0:128, 0:512], AF.Copy, inc="f_ps" + pss)
                    b.wait("vector", "f_ps" + pss)
                    b.wait("vector", "m1w")
                    b.wait("vector", "mm_xd")
                    vcopy(cb.c(0, 3), tails.c(et * 3, (et + 1) * 3), inc="v_cv")
                    bar("vector")
                    vts(xce.ap(), cb.c(0, TC), V("convw", et * 4 + 0), ALU.mult,
                        inc="v_cv")
                    bar("vector")
                    for k in range(1, 4):
                        vstt(xce.ap(), cb.c(k, k + TC), V("convw", et * 4 + k),
                             ALU.mult, ALU.add, xce.ap(), inc="v_cv")
                        bar("vector")
                    vcopy(tails.c(et * 3, (et + 1) * 3), cb.c(TC, TC + 3), inc="v_cv")
                    b.wait("scalar", "v_cv")
                    act(xce.ap(), xce.ap(), AF.Silu, bias=V("convb", et), inc="a_m1")
                    b.wait("sync", "a_m1")
                    dma_out(g["xc_h"][et * 128:(et + 1) * 128, c * TC:(c + 1) * TC],
                            xce.ap(), sem="m1w")
                    b.wait("tensor", "a_m1")
                    if et == 0 and c > 0:
                        b.wait("tensor", "f_psC")
                    mm(psC[0:96, 0:512], m_static["xpT"].c(et * 96, (et + 1) * 96),
                       xce.ap(), et == 0, et == 15, inc="mm_xd")
                else:
                    b.wait("scalar", "m1w")
                    b.wait("scalar", "mm_xd")
                    act(xce.ap(), ps[0:128, 0:512], AF.Silu, inc="f_ps" + pss)
                    b.wait("sync", "f_ps" + pss)
                    dma_out(g["zs_h"][(et - 16) * 128:(et - 15) * 128,
                                      c * TC:(c + 1) * TC], xce.ap(), sem="m1w")
            b.wait("scalar", "mm_xd")
            act(m_static["xdbl"].c(c * TC, (c + 1) * TC, parts=96), psC[0:96, 0:512],
                AF.Copy, inc="f_psC")
            b.wait("sync", "f_psC")
            dma_out(g["xdbl_h"][:, c * TC:(c + 1) * TC],
                    m_static["xdbl"].c(c * TC, (c + 1) * TC, parts=96), sem="m1w")
        b.wait("sync", "m1w")
        ar.pop()

    def m2():
        b.fence()
        ar.push()
        s1b = ar.alloc(4 * D)
        yall = ar.alloc(16 * TC)
        xcq = ar.alloc(ETQ * TC)
        zsq = ar.alloc(ETQ * TC)
        dlq = ar.alloc(ETQ * TC)
        dxq = ar.alloc(ETQ * TC)
        abuf = ar.alloc(2 * ETQ * TC)
        ubuf = ar.alloc(2 * ETQ * TC)
        opb = ar.alloc(16 * 128)
        mo = ar.alloc(TC)
        hs_scr = ar.alloc(2 * ETQ)
        bcrow = ar.alloc(4 * TC, parts=1)
        QT = ETQ * TC

        def a3(tile, p):
            return tile.c(p * QT, (p + 1) * QT).rearrange("p (e t) -> p e t", e=ETQ)

        def a2(tile, p):
            return tile.c(p * QT, (p + 1) * QT)

        prev_hc = [0, 0]
        prev_acc = [0, 0]
        prev_bc = [0, 0]
        for c in range(NCH):
            b.wait("sync", "m2w")
            b.wait("sync", "dbgw")
            for j in range(4):
                blk = c * 4 + j
                dma_in(g["s1"][blk * 128:(blk + 1) * 128, :], s1b.c(j * D, (j + 1) * D))
            s1b_cnt = b.cnt[SEM_DMA]
            b.wait("gpsimd", "f_psA")
            b.wait("gpsimd", "f_psB")
            for q in range(NEQ):
                b.wait("scalar", "v_m2")
                b.wait("tensor", "v_m2")
                for i in range(ETQ):
                    et = q * ETQ + i
                    ps, pss = (psA, "A") if i % 2 == 0 else (psB, "B")
                    b.wait("tensor", "f_ps" + pss)
                    mm(ps[0:128, 0:512],
                       m_static["dtT"].c(et * 128, (et + 1) * 128, parts=64),
                       m_static["xdbl"].c(c * TC, (c + 1) * TC, parts=64),
                       True, True, inc="p_ps" + pss)
                    b.wait("scalar", "p_ps" + pss)
                    act(dlq.c(i * TC, (i + 1) * TC), ps[0:128, 0:512], AF.Exp,
                        bias=V("dtb", et), inc="f_ps" + pss)
                act(dlq.ap(), dlq.ap(), AF.Ln, bias=1.0, inc="a_m2")
                b.wait("sync", "v_m2")
                b.wait("sync", "g_m2")
                for i in range(ETQ):
                    et = q * ETQ + i
                    dma_in(g["xc_h"][et * 128:(et + 1) * 128, c * TC:(c + 1) * TC],
                           xcq.c(i * TC, (i + 1) * TC))
                    dma_in(g["zs_h"][et * 128:(et + 1) * 128, c * TC:(c + 1) * TC],
                           zsq.c(i * TC, (i + 1) * TC))
                wait_dma("vector")
                b.wait("vector", "a_m2")
                vtt(dxq.ap(), dlq.ap(), xcq.ap(), ALU.mult, inc="v_m2")
                bar("vector")

                b.sem("a_m2"); b.sem("v_m2"); b.sem("g_m2"); b.sem(SEM_DMA)
                exp_cnt = [0] * N
                hc_cnt = [0] * N
                acc_cnt = [0] * N
                bc_cnt = [0] * N

                BBANK = [psC, psD]
                CBANK = [psE, psB]

                def bc_load(n):
                    p = n % 2
                    b.wait_val("sync", "p_bc", bc_cnt[n - 2] if n >= 2 else prev_bc[p])
                    dma_in(g["xdbl_h"][RK + n:RK + n + 1, c * TC:(c + 1) * TC],
                           bcrow.c(p * 2 * TC, p * 2 * TC + TC, parts=1))
                    dma_in(g["xdbl_h"][RK + N + n:RK + N + n + 1, c * TC:(c + 1) * TC],
                           bcrow.c(p * 2 * TC + TC, p * 2 * TC + 2 * TC, parts=1))
                    dcnt = b.cnt[SEM_DMA]
                    if n >= 2:
                        b.wait_val("tensor", "v_m2", hc_cnt[n - 2])
                    else:
                        b.wait_val("tensor", "v_m2", prev_hc[p])
                    b.wait_val("tensor", SEM_DMA, dcnt)
                    b.do("tensor", lambda e, o=BBANK[p][0:128, 0:512],
                         m=bcrow.c(p * 2 * TC, p * 2 * TC + TC, parts=1),
                         s=ones_row.c(0, 128, parts=1):
                         e.matmul(o, s, m, start=True, stop=True,
                                  skip_group_check=True), inc="p_bc")
                    b.do("tensor", lambda e, o=CBANK[p][0:128, 0:512],
                         m=bcrow.c(p * 2 * TC + TC, p * 2 * TC + 2 * TC, parts=1),
                         s=ones_row.c(0, 128, parts=1):
                         e.matmul(o, s, m, start=True, stop=True,
                                  skip_group_check=True), inc="p_bc")
                    bc_cnt[n] = b.cnt["p_bc"]

                bc_load(0)
                bc_load(1)
                for n in range(N):
                    p = n % 2
                    if n >= 2:
                        b.wait_val("scalar", "g_m2", acc_cnt[n - 2])
                    else:
                        b.wait_val("scalar", "g_m2", prev_acc[p])
                    for i in range(ETQ):
                        et = q * ETQ + i
                        act(a2(abuf, p).tensor[0:128,
                            abuf.off + p * QT + i * TC:abuf.off + p * QT + (i + 1) * TC]
                            if False else
                            abuf.c(p * QT + i * TC, p * QT + (i + 1) * TC),
                            dlq.c(i * TC, (i + 1) * TC),
                            AF.Exp, scale=V("Acols", et * N + n), inc="a_m2")
                    exp_cnt[n] = b.cnt["a_m2"]
                    b.wait_val("vector", "p_bc", bc_cnt[n])
                    vtt(a3(ubuf, p), dxq.ap().rearrange("p (e t) -> p e t", e=ETQ),
                        BBANK[p][0:128, 0:512].unsqueeze(1)
                        .broadcast_to([128, ETQ, TC]), ALU.mult, inc="v_m2")
                    b.wait_val("vector", "a_m2", exp_cnt[n])
                    bar("vector")
                    for i in range(ETQ):
                        init_ap = h_state.ap().rearrange("p (n e) -> p n e", n=N)[
                            :, n:n + 1, q * ETQ + i:q * ETQ + i + 1].squeeze(2)
                        b.do("vector", lambda e,
                             o=abuf.c(p * QT + i * TC, p * QT + (i + 1) * TC),
                             i0=abuf.c(p * QT + i * TC, p * QT + (i + 1) * TC),
                             i1=ubuf.c(p * QT + i * TC, p * QT + (i + 1) * TC),
                             ini=init_ap: e.tensor_tensor_scan(o, i0, i1, ini,
                                                               ALU.mult, ALU.add),
                             inc="v_m2")
                    bar("vector")
                    b.do("vector", lambda e, n=n, q=q, src=a3(abuf, p)[:, :, TC - 1:TC]
                         .squeeze(2): e.tensor_copy(
                        h_state.ap().rearrange("p (n e) -> p n e", n=N)
                        [:, n:n + 1, q * ETQ:(q + 1) * ETQ].squeeze(1), src),
                        inc="v_m2")
                    vtt(a3(abuf, p), a3(abuf, p),
                        CBANK[p][0:128, 0:512].unsqueeze(1)
                        .broadcast_to([128, ETQ, TC]), ALU.mult, inc="v_m2")
                    hc_cnt[n] = b.cnt["v_m2"]
                    b.wait_val("gpsimd", "v_m2", hc_cnt[n])
                    if n == 0:
                        b.do("gpsimd", lambda e, src=a2(abuf, p),
                             dst=yall.c(q * QT, (q + 1) * QT):
                             e.tensor_copy(dst, src), inc="g_m2")
                    else:
                        b.do("gpsimd", lambda e, src=a2(abuf, p),
                             dst=yall.c(q * QT, (q + 1) * QT),
                             dst2=yall.c(q * QT, (q + 1) * QT):
                             e.tensor_tensor(dst, dst2, src, ALU.add), inc="g_m2")
                    acc_cnt[n] = b.cnt["g_m2"]
                    if n + 2 < N:
                        bc_load(n + 2)
                prev_hc = [hc_cnt[N - 2], hc_cnt[N - 1]]
                prev_acc = [acc_cnt[N - 2], acc_cnt[N - 1]]
                prev_bc = [bc_cnt[N - 2], bc_cnt[N - 1]]
                b.wait_val("vector", "g_m2", acc_cnt[N - 1])
                for i in range(ETQ):
                    et = q * ETQ + i
                    vstt(yall.c((q * ETQ + i) * TC, (q * ETQ + i + 1) * TC),
                         xcq.c(i * TC, (i + 1) * TC), V("Dc", et), ALU.mult, ALU.add,
                         yall.c((q * ETQ + i) * TC, (q * ETQ + i + 1) * TC), inc="v_m2")
                bar("vector")
                vtt(yall.c(q * QT, (q + 1) * QT), yall.c(q * QT, (q + 1) * QT),
                    zsq.ap(), ALU.mult, inc="v_m2")
                bar("vector")
            b.wait("tensor", "v_m2")
            for mt in range(8):
                b.wait("sync", "f_psA")
                b.wait("sync", "f_psB")
                dma_in(g["opT_h"][mt], opb.ap())
                wait_dma("tensor")
                ps, pss = (psA, "A") if mt % 2 == 0 else (psB, "B")
                b.wait("tensor", "f_ps" + pss)
                for kt in range(16):
                    mm(ps[0:128, 0:512], opb.c(kt * 128, (kt + 1) * 128),
                       yall.c(kt * TC, (kt + 1) * TC), kt == 0, kt == 15,
                       inc="p_ps" + pss if kt == 15 else None)
                b.wait("scalar", "p_ps" + pss)
                b.wait("scalar", "p_psD")
                act(mo.ap(), ps[0:128, 0:512], AF.Copy, inc="f_ps" + pss)
                b.wait("tensor", "f_ps" + pss)
                b.wait("tensor", "f_psD")
                for j in range(4):
                    transpose(psD[0:128, j * 128:(j + 1) * 128],
                              mo.c(j * 128, (j + 1) * 128), 128,
                              inc="p_psD" if j == 3 else None)
                b.wait("vector", "p_psD")
                b.wait_val("vector", SEM_DMA, s1b_cnt)
                for j in range(4):
                    vtt(s1b.c(j * D + mt * 128, j * D + (mt + 1) * 128),
                        s1b.c(j * D + mt * 128, j * D + (mt + 1) * 128),
                        psD[0:128, j * 128:(j + 1) * 128], ALU.add,
                        inc="f_psD" if j == 3 else None)
            b.wait("sync", "f_psD")
            for j in range(4):
                blk = c * 4 + j
                dma_out(g["s2"][blk * 128:(blk + 1) * 128, :],
                        s1b.c(j * D, (j + 1) * D), sem="m2w")
        b.wait("sync", "m2w")
        ar.pop()

    if "mpre" in _PHASES:
        m_pre()
    if "ffn1" in _PHASES:
        ffn_call(g["x_in"], g["s1"], "s1w", cid=0)
        b.wait("sync", "s1w")
    if "m1" in _PHASES:
        m1()
    if "m2" in _PHASES:
        m2()
    if "ffn2" in _PHASES:
        b.fence()
        ffn_call(g["s2"], g["out"], SEM_OUT, cid=1)
    else:
        b.do("sync", lambda e: e.dma_start(out=g["out"][0:128, :], in_=g["s1"][0:128, :]),
             inc=SEM_OUT, inc_n=16)

    for name, src in [("s1", g["s1"]), ("s2", g["s2"]), ("xdbl", g["xdbl_h"]),
                      ("xc", g["xc_h"]), ("zs", g["zs_h"])]:
        if name in dbg:
            b.do("sync", lambda e, d=dbg[name], s=src: e.dma_start(out=d, in_=s),
                 inc=SEM_OUT, inc_n=16)

    b.wait("sync", SEM_OUT)

    with nc.Block() as block:
        b.flush(block)
    return nc


_REN = {
    "L0_attn_in_proj_w": "in_proj_w", "L0_attn_conv_w": "conv_w",
    "L0_attn_conv_b": "conv_b", "L0_attn_x_proj_w": "x_proj_w",
    "L0_attn_dt_proj_w": "dt_proj_w", "L0_attn_dt_proj_b": "dt_proj_b",
    "L0_attn_A_log": "A_log", "L0_attn_D": "Dvec",
    "L0_attn_out_proj_w": "out_proj_w",
    "L0_ffn_slot_init": "slot_init", "L0_ffn_w1": "w1", "L0_ffn_b1": "b1",
    "L0_ffn_w2": "w2", "L0_ffn_b2": "b2", "L0_ffn_tau": "tau",
    "L0_ffn_Wih": "Wih", "L0_ffn_Whh": "Whh", "L0_ffn_bih": "bih",
    "L0_ffn_bhh": "bhh", "L0_ffn_wq": "wq", "L0_ffn_bq": "bq",
    "L0_ffn_wk": "wk", "L0_ffn_bk": "bk", "L0_ffn_wv": "wv",
    "L0_ffn_bv": "bv", "L0_ffn_ln_g": "ln_g", "L0_ffn_ln_b": "ln_b",
    "L0_ffn_wvp": "wvp", "L0_ffn_bvp": "bvp", "L0_ffn_wo": "wo",
    "L0_ffn_bo": "bo",
}


def kernel(**inputs):
    x = np.ascontiguousarray(np.asarray(inputs["L0_attn_hidden_states"],
                                        dtype=np.float32))
    weights = {v: np.ascontiguousarray(np.asarray(inputs[k], dtype=np.float32))
               for k, v in _REN.items()}
    nc = build_program()
    in_maps = []
    for c in range(8):
        m = {"x_in": x[c]}
        m.update(weights)
        in_maps.append(m)
    res = run_bass_kernel_spmd(nc, in_maps, core_ids=list(range(8)))
    return np.stack([res.results[c]["out"] for c in range(8)], axis=0)
